# version 01
# baseline (speedup 1.0000x reference)
import math
import contextlib
import numpy as np
import concourse.bass as bass
import concourse.mybir as mybir
from concourse.bass_utils import run_bass_kernel_spmd

F32 = mybir.dt.float32
BF16 = mybir.dt.bfloat16
AF = mybir.ActivationFunctionType
ALU = mybir.AluOpType
AX = mybir.AxisListType

ENGINES = ("tensor", "vector", "scalar", "gpsimd", "sync")
PE, VE, SE, GE, SY = ENGINES
NB = 2
T = 2048
D = 1024
NT = T // 128
LAMBDA_INIT = 0.8 - 0.6 * math.exp(-0.3 * 12)
MAGIC = float(np.float32(1.5 * 2 ** 23))

DEBUG = False
STOP_AFTER = None


class Sem:
    def __init__(self, name):
        self.name = name
        self.h = None
        self.n = 0


class Buf:
    def __init__(self, h, off, cols, dt):
        self.h = h; self.off = off; self.cols = cols; self.dt = dt

    def __getitem__(self, idx):
        ps, fs = idx
        a = fs.start or 0
        b = fs.stop if fs.stop is not None else self.cols
        assert fs.step is None and 0 <= a <= b <= self.cols, (a, b, self.cols)
        return self.h[ps, self.off + a: self.off + b]

    @property
    def full(self):
        return self.h[:, self.off: self.off + self.cols]

    def sub(self, a, b):
        return Buf(self.h, self.off + a, b - a, self.dt)


class Arena:
    def __init__(self, h, n, dt, name):
        self.h = h; self.n = n; self.dt = dt; self.off = 0; self.name = name
        self.marks = []; self.hi = 0

    def alloc(self, cols, align=4):
        o = (self.off + align - 1) // align * align
        assert o + cols <= self.n, f"arena {self.name} overflow: {o}+{cols} > {self.n}"
        self.off = o + cols
        self.hi = max(self.hi, self.off)
        return Buf(self.h, o, cols, self.dt)

    def mark(self):
        self.marks.append(self.off)

    def release(self):
        self.off = self.marks.pop()


class Prog:
    def __init__(self, nc):
        self.nc = nc
        self.streams = {e: [] for e in ENGINES}
        self.sems = {}
        for s in ("pe", "ve", "se", "ge", "sy", "LQ", "LG", "SQ", "CC"):
            self.sems[s] = Sem(s)
        self._emap = {PE: "pe", VE: "ve", SE: "se", GE: "ge", SY: "sy"}

    def em(self, eng, fn):
        self.streams[eng].append(fn)

    def esem(self, eng):
        return self.sems[self._emap[eng]]

    def op(self, eng, fn, sem=None):
        if sem is not None:
            sem.n += 1
            v = sem.n
            self.em(eng, lambda e, fn=fn, s=sem: fn(e).then_inc(s.h, 1))
            return v
        self.em(eng, fn)
        return None

    def opc(self, eng, fn):
        return self.op(eng, fn, self.esem(eng))

    def wait(self, eng, sem, val=None):
        v = sem.n if val is None else val
        self.em(eng, lambda e, s=sem, v=v: e.wait_ge(s.h, v))

    def drain(self, eng):
        self.em(eng, lambda e: e.drain())

    def dma(self, eng, out, in_, sem=None, **kw):
        sem = sem or (self.sems["LG"] if eng == GE else self.sems["LQ"])
        sem.n += 16
        v = sem.n
        self.em(eng, lambda e, o=out, i=in_, s=sem, kw=kw: e.dma_start(out=o, in_=i, **kw).then_inc(s.h, 16))
        return v

    def store(self, out, in_, eng=SY):
        return self.dma(eng, out, in_, sem=self.sems["SQ"])

    def barrier(self):
        self.wait(SY, self.sems["LQ"])
        self.wait(SY, self.sems["SQ"])
        self.wait(GE, self.sems["LG"])
        vals = {}
        for eng in ENGINES:
            s = self.esem(eng)
            s.n += 1
            vals[eng] = s.n
            if eng in (GE, VE):
                self.em(eng, lambda e, s=s: e.engine_nop().then_inc(s.h, 1))
            elif eng in (PE, SE):
                self.drain(eng)
                self.em(eng, lambda e, s=s: e.nop().then_inc(s.h, 1))
            else:
                self.em(eng, lambda e, s=s: e.nop().then_inc(s.h, 1))
        for eng in ENGINES:
            for oth in ENGINES:
                if oth != eng:
                    self.wait(eng, self.esem(oth), vals[oth])


def build_program(debug_names=()):
    nc = bass.Bass("TRN2", target_bir_lowering=False)
    p = Prog(nc)
    S = p.sems

    dr = {}
    def din(name, shape, dt=F32):
        dr[name] = nc.dram_tensor(name, list(shape), dt, kind="ExternalInput").ap()
        return dr[name]
    def dscr(name, shape, dt=F32):
        dr[name] = nc.dram_tensor(name, list(shape), dt).ap()
        return dr[name]

    x_in = din("x_in", (NB, T, D))
    wqT = din("wqT", (D, D)); wkT = din("wkT", (D, D)); wvT = din("wvT", (D, D)); woT = din("woT", (D, D))
    lamq1 = din("lamq1", (32, 1)); lamk1 = din("lamk1", (32, 1))
    lamq2 = din("lamq2", (32, 1)); lamk2 = din("lamk2", (32, 1))
    sublnw = din("sublnw", (1, 64))
    wgT0 = din("wgT0", (D, 4096)); wuT0 = din("wuT0", (D, 4096)); wdT0 = din("wdT0", (4096, D))
    c0T = din("c0T", (D, NB)); wmod0 = din("wmod0", (D, 6 * D))
    convw = din("convw", (128, 32)); convb = din("convb", (1, D))
    inprojT = din("inprojT", (D, 320)); inprojb = din("inprojb", (1, 320))
    alog = din("alog", (64, 1))
    outprojT = din("outprojT", (64, D)); outprojb = din("outprojb", (1, D))
    clipx = din("clipx", (1, 1)); clipd = din("clipd", (1, 1))
    wgT1 = din("wgT1", (D, 2816)); wuT1 = din("wuT1", (D, 2816)); wdT1 = din("wdT1", (2816, D))
    wg1n = din("wg1n", (2816, D)); wu1n = din("wu1n", (2816, D)); wd1n = din("wd1n", (D, 2816))
    c1T = din("c1T", (D, NB)); wmod1 = din("wmod1", (D, 6 * D))

    y_out = nc.dram_tensor("y", [NB, T, D], F32, kind="ExternalOutput").ap()

    xres = dscr("xres", (NB, T, D))
    modscr = dscr("modscr", (2, NB, 6 * D), BF16)
    h2q_sp = dscr("h2q_sp", (NB, NT, 128, 1024), BF16)
    h4_sp = dscr("h4_sp", (NB, NT, 128, 1024), BF16)
    t0_sp = dscr("t0_sp", (NB, NT, 128, 4096), BF16)
    t1_sp = dscr("t1_sp", (NB, NT, 128, 2816), BF16)
    cc_in1 = dscr("cc_in1", (1, 8)); cc_out1 = dscr("cc_out1", (1, 8))
    cc_in2 = dscr("cc_in2", (1, 8)); cc_out2 = dscr("cc_out2", (1, 8))
    negscr = dscr("negscr", (4, 8))
    scr2 = dscr("scr2", (128, 32))
    rmq_sp = dscr("rmq_sp", (NB, NT, 128, 1))
    attn_sp = dscr("attn_sp", (NB, T, D), BF16)

    dbg = {}
    if DEBUG:
        for name, shape, dt in debug_names:
            dbg[name] = nc.dram_tensor("dbg_" + name, list(shape), dt, kind="ExternalOutput").ap()

    es = contextlib.ExitStack()
    sb = lambda n, s, d: es.enter_context(nc.sbuf_tensor(n, s, d))
    psm = lambda n, s, d: es.enter_context(nc.psum_tensor(n, s, d))

    with es:
        A16 = Arena(sb("arena16", [128, 91300], BF16), 91300, BF16, "A16")
        A32 = Arena(sb("arena32", [128, 6200], F32), 6200, F32, "A32")
        ident = sb("ident", [128, 128], BF16)
        ones1 = sb("ones1", [1, 128], BF16)
        ones1f = sb("ones1f", [1, 128], F32)
        ones4 = sb("ones4", [128, 4], BF16)
        io4 = sb("io4", [128, 4], F32)
        row0 = sb("row0", [1, 1024], BF16)
        smal = sb("smal", [128, 512], F32)
        persc = sb("persc", [128, 40], F32)
        pA = psm("pA", [128, 1536], F32)
        pB = psm("pB", [128, 1536], F32)
        pC = psm("pC", [128, 512], F32)
        pbt = psm("pbt", [128, 1024], BF16)
        pb = [Buf(pA, 512 * i, 512, F32) for i in range(3)] + \
             [Buf(pB, 512 * i, 512, F32) for i in range(3)] + [Buf(pC, 0, 512, F32)]

        for name, s in S.items():
            s.h = es.enter_context(nc.semaphore(name))

        def p_init():
            p.op(GE, lambda e: e.memset(ident.ap(), 0.0))
            p.opc(GE, lambda e: e.affine_select(
                out=ident[:, :], in_=ident[:, :], compare_op=ALU.not_equal,
                fill=1.0, base=0, pattern=[[-1, 128]], channel_multiplier=1))
            p.opc(GE, lambda e: e.memset(ones1.ap(), 1.0))
            p.opc(GE, lambda e: e.memset(ones1f.ap(), 1.0))
            p.barrier()

        def bcast_row(src_row, n, dst, pbank=6, f32=False):
            one = ones1f if f32 else ones1
            for g0 in range(0, n, 512):
                w = min(512, n - g0)
                p.wait(PE, S["ve"])
                vpe = p.opc(PE, lambda e, g0=g0, w=w, one=one: e.matmul(
                    pb[pbank][:, 0:w], one[0:1, :], src_row[0:1, g0:g0 + w],
                    start=True, stop=True))
                p.wait(VE, S["pe"], vpe)
                p.opc(VE, lambda e, g0=g0, w=w: e.tensor_copy(dst[:, g0:g0 + w], pb[pbank][:, 0:w]))

        def p_mod(L, cT, wmod):
            A16.mark(); A32.mark()
            cs = A32.alloc(16)
            csf = A32.alloc(16)
            mrow = A32.alloc(1024)
            mrow2 = A32.alloc(1024)
            wm = A32.alloc(8 * 512)
            modsb = A16.alloc(6 * D)
            v = p.dma(SY, cs.full.rearrange("p (c t) -> p c t", c=8),
                      cT.rearrange("(c p) t -> p c t", c=8),
                      allow_slow_non_contiguous=True)
            p.wait(SE, S["LQ"], v)
            vsil = p.opc(SE, lambda e: e.activation(csf.full, cs.full, AF.Silu))
            p.wait(PE, S["se"], vsil)
            evac_v = {}
            for g in range(12):
                if g >= 1:
                    p.wait(SY, S["pe"])
                vld = p.dma(SY, wm.full, wmod[:, 512 * g:512 * (g + 1)].rearrange("(c p) m -> p c m", c=8))
                p.wait(PE, S["LQ"], vld)
                if g >= 2:
                    p.wait(PE, S["ve"], evac_v[g - 2])
                for c in range(8):
                    vpe = p.opc(PE, lambda e, c=c, g=g: e.matmul(
                        pb[g % 2][0:2, :], csf[:, 2 * c:2 * c + 2],
                        wm.full.rearrange("p (c m) -> p c m", c=8)[:, c, :],
                        start=(c == 0), stop=(c == 7)))
                p.wait(VE, S["pe"], vpe)
                evac_v[g] = p.opc(VE, lambda e, g=g: e.tensor_copy(
                    modsb[0:2, 512 * g:512 * (g + 1)], pb[g % 2][0:2, :]))
            for sl in (1, 4):
                p.opc(VE, lambda e, sl=sl: e.tensor_scalar(
                    modsb[0:2, sl * D:(sl + 1) * D], modsb[0:2, sl * D:(sl + 1) * D],
                    1.0, None, ALU.add, ALU.bypass))
            for sl in (2, 5):
                p.opc(VE, lambda e, sl=sl: e.tensor_scalar(
                    mrow[0:2, 0:D], modsb[0:2, sl * D:(sl + 1) * D], 0.0, None, ALU.is_ge, ALU.bypass))
                p.drain(VE)
                p.opc(VE, lambda e: e.tensor_scalar(
                    mrow2[0:2, 0:D], mrow[0:2, 0:D], 2.0, -1.0, ALU.mult, ALU.add))
                p.opc(VE, lambda e, sl=sl: e.tensor_scalar(
                    mrow[0:2, 0:D], modsb[0:2, sl * D:(sl + 1) * D], -1.0, None, ALU.mult, ALU.bypass))
                p.drain(VE)
                p.opc(VE, lambda e, sl=sl: e.tensor_tensor(
                    mrow[0:2, 0:D], mrow[0:2, 0:D], modsb[0:2, sl * D:(sl + 1) * D], ALU.max))
                p.drain(VE)
                p.opc(VE, lambda e: e.tensor_scalar(
                    mrow[0:2, 0:D], mrow[0:2, 0:D], 0.25, None, ALU.max, ALU.bypass))
                p.drain(VE)
                p.opc(VE, lambda e, sl=sl: e.tensor_tensor(
                    modsb[0:2, sl * D:(sl + 1) * D], mrow2[0:2, 0:D], mrow[0:2, 0:D], ALU.mult))
                p.drain(VE)
            vrows = p.opc(VE, lambda e: e.engine_nop())
            p.wait(SY, S["ve"], vrows)
            p.store(modscr[L], modsb[0:2, :])
            if DEBUG and L == 0 and "modrow" in dbg:
                p.store(dbg["modrow"], modsb[0:2, :])
            if DEBUG and L == 0 and "csb" in dbg:
                p.store(dbg["csb"], csf.full)
            A16.release(); A32.release()
            p.barrier()

        SLOT_OF = {"sa": 0, "ca1": 1, "ga": 2, "sb": 3, "cb1": 4, "gb": 5}

        def build_rep(L, rep):
            for (b, kind), dst in rep.items():
                sl = SLOT_OF[kind]
                p.wait(SY, S["pe"])
                vr = p.dma(SY, row0[0:1, 0:D], modscr[L, b, sl * D:(sl + 1) * D][None, :])
                p.wait(PE, S["LQ"], vr)
                bcast_row(row0, D, dst.full)
            p.barrier()

        def ln_h(xt, stat, htm, ca1, sa_, xnb):
            p.opc(VE, lambda e: e.bn_stats(stat[:, 0:6], xt[:, 0:512]))
            p.opc(VE, lambda e: e.bn_stats(stat[:, 6:12], xt[:, 512:1024]))
            p.drain(VE)
            p.opc(VE, lambda e: e.bn_aggr(stat[:, 12:14], stat[:, 0:12]))
            p.drain(VE)
            p.opc(VE, lambda e: e.tensor_scalar(stat[:, 14:15], stat[:, 13:14], 1e-5, None, ALU.add, ALU.bypass))
            p.drain(VE)
            vv = p.opc(VE, lambda e: e.reciprocal(stat[:, 15:16], stat[:, 14:15]))
            p.wait(SE, S["ve"], vv)
            p.drain(SE)
            vs = p.opc(SE, lambda e: e.activation(stat[:, 16:17], stat[:, 15:16], AF.Sqrt))
            p.wait(VE, S["se"], vs)
            p.opc(VE, lambda e: e.tensor_scalar(stat[:, 17:18], stat[:, 12:13], stat[:, 16:17], -1.0, ALU.mult, ALU.mult))
            p.drain(VE)
            vv2 = p.opc(VE, lambda e: e.engine_nop())
            p.wait(SE, S["ve"], vv2)
            p.drain(SE)
            vxn = p.opc(SE, lambda e: e.activation(xnb.full, xt[:, 0:D], AF.Identity,
                                                   bias=stat[:, 17:18], scale=stat[:, 16:17]))
            p.wait(VE, S["se"], vxn)
            p.opc(VE, lambda e: e.tensor_tensor(htm[:, 0:D], xnb.full, ca1.full, ALU.mult))
            vtt = p.opc(VE, lambda e: e.tensor_tensor(htm[:, 0:D], htm[:, 0:D], sa_.full, ALU.add))
            return vxn, vtt

        def tm_to_fm(src, dst_fm, k, nchunk=8, dst_stride=T):
            vv = p.opc(VE, lambda e: e.engine_nop())
            p.wait(PE, S["ve"], vv)
            for c in range(nchunk):
                p.opc(PE, lambda e, c=c: e.transpose(
                    pbt[:, 128 * c:128 * (c + 1)], src[:, 128 * c:128 * (c + 1)], ident[:, :]))
            p.wait(VE, S["pe"])
            for c in range(nchunk):
                p.opc(VE, lambda e, c=c: e.tensor_copy(
                    dst_fm[:, dst_stride * c + 128 * k: dst_stride * c + 128 * (k + 1)],
                    pbt[:, 128 * c:128 * (c + 1)]))

        def p_attn_consts(cst):
            p.opc(GE, lambda e: e.iota(io4.ap(), pattern=[[-32, 4]], base=0,
                                       channel_multiplier=1, allow_small_or_imprecise_dtypes=True))
            p.wait(VE, S["ge"])
            p.opc(VE, lambda e: e.tensor_scalar(smal[:, 0:4], io4[:, :], 0.0, None, ALU.is_ge, ALU.bypass))
            p.drain(VE)
            p.opc(VE, lambda e: e.tensor_scalar(smal[:, 4:8], io4[:, :], 31.5, None, ALU.is_le, ALU.bypass))
            p.drain(VE)
            p.opc(VE, lambda e: e.tensor_tensor(ones4[:, :], smal[:, 0:4], smal[:, 4:8], ALU.mult))
            p.drain(VE)
            v = p.dma(SY, smal[0:32, 8:9], lamq1)
            v = p.dma(SY, smal[0:32, 9:10], lamk1)
            v = p.dma(SY, smal[0:32, 10:11], lamq2)
            v = p.dma(SY, smal[0:32, 11:12], lamk2)
            sw = p.dma(SY, smal[0:1, 16:80], sublnw)
            p.wait(VE, S["LQ"], v)
            p.opc(VE, lambda e: e.tensor_tensor(smal[0:32, 12:13], smal[0:32, 8:9], smal[0:32, 9:10], ALU.mult))
            p.opc(VE, lambda e: e.tensor_tensor(smal[0:32, 13:14], smal[0:32, 10:11], smal[0:32, 11:12], ALU.mult))
            p.drain(VE)
            vv = p.opc(VE, lambda e: e.engine_nop())
            p.wait(GE, S["ve"], vv)
            p.opc(GE, lambda e: e.tensor_reduce(smal[0:1, 14:15], smal[0:32, 12:13], AX.C, ALU.add))
            vg = p.opc(GE, lambda e: e.tensor_reduce(smal[0:1, 15:16], smal[0:32, 13:14], AX.C, ALU.add))
            p.wait(SE, S["ge"], vg)
            p.drain(SE)
            p.opc(SE, lambda e: e.activation(smal[0:1, 14:15], smal[0:1, 14:15], AF.Exp))
            p.drain(SE)
            vs = p.opc(SE, lambda e: e.activation(smal[0:1, 15:16], smal[0:1, 15:16], AF.Exp))
            p.wait(VE, S["se"], vs)
            p.opc(VE, lambda e: e.tensor_tensor(smal[0:1, 14:15], smal[0:1, 14:15], smal[0:1, 15:16], ALU.subtract))
            p.drain(VE)
            p.opc(VE, lambda e: e.tensor_scalar(smal[0:1, 14:15], smal[0:1, 14:15],
                                                LAMBDA_INIT, -1.0, ALU.add, ALU.mult))
            p.drain(VE)
            p.wait(VE, S["LQ"], sw)
            p.opc(VE, lambda e: e.tensor_scalar(smal[0:1, 16:80], smal[0:1, 16:80],
                                                1.0 - LAMBDA_INIT, None, ALU.mult, ALU.bypass))
            p.drain(VE)
            vv2 = p.opc(VE, lambda e: e.engine_nop())
            p.wait(PE, S["ve"], vv2)
            vpe = p.opc(PE, lambda e: e.matmul(pC[:, 0:1], ones1f[0:1, :], smal[0:1, 14:15],
                                               start=True, stop=True))
            vpe = p.opc(PE, lambda e: e.matmul(pC[:, 64:128], ones1f[0:1, :], smal[0:1, 16:80],
                                               start=True, stop=True))
            p.wait(VE, S["pe"], vpe)
            p.opc(VE, lambda e: e.tensor_copy(cst["negl"][:, 0:1], pC[:, 0:1]))
            p.opc(VE, lambda e: e.tensor_copy(cst["w1m"][:, 0:64], pC[:, 64:128]))
            p.barrier()

        def p_qkv(b, h1fm, qfm, kfm, vab, qn, negkb):
            A16.mark(); A32.mark()
            wbuf = A16.alloc(8192)
            q2b = A16.alloc(512)
            kmx = A32.alloc(40)
            DINV = float(1.0 / math.sqrt(32.0))

            def proj_fm(dst, w_dram, scale, qn_dst):
                vw = p.dma(GE, wbuf.full.rearrange("p (r o) -> p r o", r=8),
                           w_dram.rearrange("(r p) o -> p r o", r=8))
                evac = {}
                sq_ev = {}
                for i, (c, g) in enumerate([(c, g) for c in range(8) for g in range(4)]):
                    bank = pb[i % 2]
                    p.wait(PE, S["LG"], vw)
                    if i >= 2:
                        p.wait(PE, S["se"], evac[i - 2])
                    for r in range(8):
                        vpe = p.opc(PE, lambda e, r=r, c=c, g=g, bank=bank: e.matmul(
                            bank[:, 0:512],
                            wbuf[:, 1024 * r + 128 * c: 1024 * r + 128 * (c + 1)],
                            h1fm[:, 2048 * r + 512 * g: 2048 * r + 512 * (g + 1)],
                            start=(r == 0), stop=(r == 7)))
                    p.wait(SE, S["pe"], vpe)
                    evac[i] = p.opc(SE, lambda e, c=c, g=g, bank=bank: e.activation(
                        dst[:, 2048 * c + 512 * g: 2048 * c + 512 * (g + 1)],
                        bank[:, 0:512], AF.Identity, scale=scale))
                    if qn_dst is not None:
                        sq_ev[i] = p.opc(SE, lambda e, c=c, g=g: e.activation(
                            q2b[:, 0:512],
                            dst[:, 2048 * c + 512 * g: 2048 * c + 512 * (g + 1)], AF.Square))
                        p.wait(PE, S["se"], sq_ev[i])
                        if qn_dst == "kmx":
                            p.wait(PE, S["ve"])
                        elif i >= 1:
                            p.wait(PE, S["se"])
                        vn = p.opc(PE, lambda e: e.matmul(
                            pC[0:4, 0:512], ones4[:, :], q2b[:, 0:512], start=True, stop=True))
                        if qn_dst == "kmx":
                            p.wait(VE, S["pe"], vn)
                            p.opc(VE, lambda e, i=i: e.tensor_reduce(
                                kmx[0:4, i:i + 1], pC[0:4, 0:512], AX.X, ALU.max))
                        else:
                            p.wait(SE, S["pe"], vn)
                            p.opc(SE, lambda e, c=c, g=g: e.activation(
                                qn_dst[0:4, 2048 * c + 512 * g: 2048 * c + 512 * (g + 1)],
                                pC[0:4, 0:512], AF.Sqrt))
                p.barrier()

            proj_fm(qfm.full, wqT, DINV, qn)
            proj_fm(kfm.full, wkT, 1.0, "kmx")
            p.drain(VE)
            p.opc(VE, lambda e: e.tensor_reduce(
                kmx[0:4, 32:40],
                kmx[0:4, 0:32].rearrange("p (c g) -> p c g", c=8),
                AX.X, ALU.max))
            p.drain(VE)
            vv = p.opc(VE, lambda e: e.engine_nop())
            p.wait(SE, S["ve"], vv)
            p.drain(SE)
            vs = p.opc(SE, lambda e: e.activation(kmx[0:4, 32:40], kmx[0:4, 32:40], AF.Sqrt))
            p.wait(VE, S["se"], vs)
            p.opc(VE, lambda e: e.tensor_scalar(kmx[0:4, 32:40], kmx[0:4, 32:40], -1.0, None, ALU.mult, ALU.bypass))
            p.drain(VE)
            vneg = p.opc(VE, lambda e: e.engine_nop())
            p.wait(SY, S["ve"], vneg)
            vst = p.store(negscr, kmx[0:4, 32:40])
            p.wait(SY, S["SQ"], vst)
            vrow = p.dma(SY, smal[0:1, 96:128], negscr.rearrange("r c -> c r"),
                         allow_slow_non_contiguous=True)
            p.wait(PE, S["LQ"], vrow)
            vpe = p.opc(PE, lambda e: e.matmul(pC[:, 128:160], ones1f[0:1, :], smal[0:1, 96:128],
                                               start=True, stop=True))
            p.wait(VE, S["pe"], vpe)
            p.opc(VE, lambda e: e.tensor_copy(negkb[:, 0:32], pC[:, 128:160]))
            vw = p.dma(GE, wbuf.full.rearrange("p (r o) -> p r o", r=8),
                       wvT.rearrange("(r p) o -> p r o", r=8))
            p.opc(VE, lambda e: e.memset(
                vab.full.rearrange("p (k h o) -> p k h o", k=16, h=16)[:, :, :, 64:65], 1.0))
            evac = {}
            for i, (kt, dg) in enumerate([(kt, dg) for kt in range(16) for dg in range(2)]):
                bank = pb[2 + i % 2]
                p.wait(PE, S["LG"], vw)
                if i >= 2:
                    p.wait(PE, S["ve"], evac[i - 2])
                for r in range(8):
                    vpe = p.opc(PE, lambda e, r=r, kt=kt, dg=dg, bank=bank: e.matmul(
                        bank[:, 0:512],
                        h1fm[:, 2048 * r + 128 * kt: 2048 * r + 128 * (kt + 1)],
                        wbuf[:, 1024 * r + 512 * dg: 1024 * r + 512 * (dg + 1)],
                        start=(r == 0), stop=(r == 7)))
                p.wait(VE, S["pe"], vpe)
                evac[i] = p.opc(VE, lambda e, kt=kt, dg=dg, bank=bank: e.tensor_copy(
                    vab.full.rearrange("p (k h o) -> p k h o", k=16, h=16)[:, kt, 8 * dg:8 * (dg + 1), 0:64],
                    bank[:, 0:512].rearrange("p (h o) -> p h o", h=8)))
            if DEBUG and b == 0:
                p.barrier()
                if "qfm" in dbg: p.store(dbg["qfm"], qfm.full)
                if "kfm" in dbg: p.store(dbg["kfm"], kfm.full)
                if "vab" in dbg: p.store(dbg["vab"], vab.full)
                if "qn" in dbg: p.store(dbg["qn"][0:4, :], qn[0:4, 0:8 * T])
                if "negkb" in dbg: p.store(dbg["negkb"], negkb.full)
            A16.release(); A32.release()
            p.barrier()

        def p_core(b, qfm, kfm, vab, qn, negkb, cst):
            A16.mark(); A32.mark()
            E = A16.alloc(8192)
            kpair = A16.alloc(2048)
            qpair = A16.alloc(2048)
            stage = A16.alloc(4096)
            dump = A16.alloc(64)
            o0sb = A32.alloc(280)
            comb = A32.alloc(200)
            vcols = A32.alloc(256)
            p.opc(VE, lambda e: e.memset(kpair[32:33, 0:2048], 1.0))
            p.opc(VE, lambda e: e.memset(kpair[96:97, 0:2048], 1.0))
            last_pair_use = 0
            last_pC_use = 0
            exp_slot = {0: 0, 1: 0}
            last_E_use = 0
            st_done = 0
            for qg in range(4):
                for vh in range(16):
                    h2a, h2b = 2 * vh, 2 * vh + 1
                    ca_, ra = h2a // 4, h2a % 4
                    cb_, rb = h2b // 4, h2b % 4
                    p.wait(GE, S["pe"], last_pair_use)
                    vp = p.dma(GE, kpair[0:32, 0:2048], kfm.h[32 * ra:32 * ra + 32, kfm.off + 2048 * ca_: kfm.off + 2048 * (ca_ + 1)])
                    vp = p.dma(GE, kpair[64:96, 0:2048], kfm.h[32 * rb:32 * rb + 32, kfm.off + 2048 * cb_: kfm.off + 2048 * (cb_ + 1)])
                    vp = p.dma(GE, qpair[0:32, 0:2048], qfm.h[32 * ra:32 * ra + 32, qfm.off + 2048 * ca_: qfm.off + 2048 * (ca_ + 1)])
                    vp = p.dma(GE, qpair[64:96, 0:2048], qfm.h[32 * rb:32 * rb + 32, qfm.off + 2048 * cb_: qfm.off + 2048 * (cb_ + 1)])
                    vp = p.dma(GE, qpair[32:33, 0:2048], qn.h[ra:ra + 1, qn.off + 2048 * ca_: qn.off + 2048 * (ca_ + 1)])
                    vp = p.dma(GE, qpair[96:97, 0:2048], qn.h[rb:rb + 1, qn.off + 2048 * cb_: qn.off + 2048 * (cb_ + 1)])
                    p.wait(VE, S["LG"], vp)
                    p.opc(VE, lambda e, h2a=h2a: e.tensor_scalar(
                        qpair[32:33, 0:2048], qpair[32:33, 0:2048],
                        negkb[32:33, h2a:h2a + 1], None, ALU.mult, ALU.bypass))
                    p.opc(VE, lambda e, h2b=h2b: e.tensor_scalar(
                        qpair[96:97, 0:2048], qpair[96:97, 0:2048],
                        negkb[96:97, h2b:h2b + 1], None, ALU.mult, ALU.bypass))
                    p.drain(VE)
                    vqp = p.opc(VE, lambda e: e.engine_nop())
                    for sh in (0, 1):
                        jbase = 0
                        for si, sw_ in enumerate((3, 3, 3, 3, 3, 1)):
                            ps = (pA, pB)[si % 2]
                            p.wait(PE, S["se"], exp_slot[si % 2])
                            p.wait(PE, S["ve"], vqp)
                            p.wait(PE, S["LG"], vp)
                            for jj in range(sw_):
                                j = jbase + jj
                                vpe = p.opc(PE, lambda e, jj=jj, j=j, sh=sh, ps=ps, qg=qg: e.matmul(
                                    ps[:, 512 * jj:512 * (jj + 1)],
                                    kpair[64 * sh:64 * sh + 33, 128 * j:128 * (j + 1)],
                                    qpair[64 * sh:64 * sh + 33, 512 * qg:512 * (qg + 1)],
                                    start=True, stop=True))
                            last_pair_use = vpe
                            p.wait(SE, S["pe"], vpe)
                            exp_slot[si % 2] = p.opc(SE, lambda e, ps=ps, jbase=jbase, sw_=sw_: e.activation(
                                E[:, 512 * jbase:512 * (jbase + sw_)], ps[:, 0:512 * sw_], AF.Exp))
                            jbase += sw_
                        p.wait(PE, S["se"], exp_slot[1])
                        p.wait(PE, S["se"], exp_slot[0])
                        if sh == 0:
                            p.wait(PE, S["ve"], last_pC_use)
                            for qs in range(4):
                                for j in range(16):
                                    vpe = p.opc(PE, lambda e, j=j, qs=qs, vh=vh: e.matmul(
                                        pC[:, 128 * qs:128 * qs + 65],
                                        E[:, 512 * j + 128 * qs:512 * j + 128 * (qs + 1)],
                                        vab[:, 1040 * j + 65 * vh:1040 * j + 65 * (vh + 1)],
                                        start=(j == 0), stop=(j == 15), skip_group_check=True))
                            last_E_use = vpe
                            p.wait(VE, S["pe"], vpe)
                            for qs in range(4):
                                p.opc(VE, lambda e, qs=qs: e.tensor_copy(
                                    o0sb[:, 70 * qs:70 * qs + 65], pC[:, 128 * qs:128 * qs + 65]))
                            vo0 = p.opc(VE, lambda e: e.engine_nop())
                            last_pC_use = vo0
                        else:
                            p.wait(PE, S["ve"], last_pC_use)
                            for qs in range(4):
                                for j in range(16):
                                    vpe = p.opc(PE, lambda e, j=j, qs=qs, vh=vh: e.matmul(
                                        pC[:, 128 * qs:128 * qs + 65],
                                        E[:, 512 * j + 128 * qs:512 * j + 128 * (qs + 1)],
                                        vab[:, 1040 * j + 65 * vh:1040 * j + 65 * (vh + 1)],
                                        start=(j == 0), stop=(j == 15), skip_group_check=True))
                            last_E_use = vpe
                            p.wait(VE, S["pe"], vpe)
                            for qs in range(4):
                                p.opc(VE, lambda e, qs=qs: e.reciprocal(
                                    comb[:, 128:129], o0sb[:, 70 * qs + 64:70 * qs + 65]))
                                p.opc(VE, lambda e, qs=qs: e.reciprocal(
                                    comb[:, 129:130], pC[:, 128 * qs + 64:128 * qs + 65]))
                                p.drain(VE)
                                p.opc(VE, lambda e, qs=qs: e.tensor_scalar(
                                    comb[:, 0:64], o0sb[:, 70 * qs:70 * qs + 64],
                                    comb[:, 128:129], None, ALU.mult, ALU.bypass))
                                p.opc(VE, lambda e, qs=qs: e.tensor_scalar(
                                    comb[:, 64:128], pC[:, 128 * qs:128 * qs + 64],
                                    comb[:, 129:130], cst["negl"][:, 0:1], ALU.mult, ALU.mult))
                                p.drain(VE)
                                vA = p.opc(VE, lambda e, qs=qs, vh=vh: e.tensor_tensor(
                                    stage[:, 1024 * qs + 64 * vh:1024 * qs + 64 * (vh + 1)],
                                    comb[:, 0:64], comb[:, 64:128], ALU.add))
                                p.wait(SE, S["ve"], vA)
                                p.opc(SE, lambda e, qs=qs, vh=vh: e.activation(
                                    dump[:, 0:64],
                                    stage[:, 1024 * qs + 64 * vh:1024 * qs + 64 * (vh + 1)],
                                    AF.Square, accum_out=vcols[:, 16 * qs + vh:16 * qs + vh + 1]))
                            vcomb = p.opc(VE, lambda e: e.engine_nop())
                            last_pC_use = vcomb
                p.wait(VE, S["se"])
                p.opc(VE, lambda e: e.tensor_scalar(
                    vcols[:, 64:128], vcols[:, 0:64], 1.0 / 64.0, 1e-5, ALU.mult, ALU.add))
                p.drain(VE)
                vu = p.opc(VE, lambda e: e.engine_nop())
                p.wait(SE, S["ve"], vu)
                p.drain(SE)
                vsq = p.opc(SE, lambda e: e.activation(
                    vcols[:, 192:256], vcols[:, 64:128], AF.Sqrt))
                p.wait(VE, S["se"], vsq)
                vrc = p.opc(VE, lambda e: e.reciprocal(vcols[:, 128:192], vcols[:, 192:256]))
                p.drain(VE)
                for qs in range(4):
                    for vh in range(16):
                        idx = 16 * qs + vh
                        p.opc(VE, lambda e, qs=qs, vh=vh, idx=idx: e.scalar_tensor_tensor(
                            stage[:, 1024 * qs + 64 * vh:1024 * qs + 64 * (vh + 1)],
                            stage[:, 1024 * qs + 64 * vh:1024 * qs + 64 * (vh + 1)],
                            vcols[:, 128 + idx:128 + idx + 1],
                            cst["w1m"][:, 0:64], ALU.mult, ALU.mult))
                vnrm = p.opc(VE, lambda e: e.engine_nop())
                p.wait(SY, S["ve"], vnrm)
                for qs in range(4):
                    st_done = p.store(attn_sp[b, 512 * qg + 128 * qs:512 * qg + 128 * (qs + 1), :],
                                      stage[:, 1024 * qs:1024 * (qs + 1)])
                p.wait(VE, S["SQ"], st_done)
            A16.release(); A32.release()
            p.barrier()

        def p_attn_out(b, rep):
            A16.mark(); A32.mark()
            wo = A16.alloc(8192)
            afm = [A16.alloc(1024) for _ in range(2)]
            h2tm = A16.alloc(1024)
            h2i = A16.alloc(1024)
            hqs = A16.alloc(1024)
            xnb2 = A16.alloc(1024)
            xt = [A32.alloc(1024) for _ in range(2)]
            xw = [A32.alloc(1024) for _ in range(2)]
            stat = [A32.alloc(24) for _ in range(2)]
            q1f = A32.alloc(1024)
            vw = p.dma(GE, wo.full.rearrange("p (r o) -> p r o", r=8),
                       woT.rearrange("(r p) o -> p r o", r=8))
            last_se = {}
            last_tr = {}
            st_last = {}
            for kt in range(NT):
                pr = kt % 2
                if kt >= 1:
                    p.wait(VE, S["SQ"], st_last[kt - 1])
                if kt >= 2:
                    p.wait(SY, S["pe"], last_tr[kt - 2])
                va_ = None
                for c in range(8):
                    va_ = p.dma(SY, afm[pr][:, 128 * c:128 * (c + 1)],
                                attn_sp[b, 128 * kt:128 * (kt + 1), 128 * c:128 * (c + 1)],
                                transpose=True)
                if kt >= 2 and (kt - 2) in last_se:
                    p.wait(SY, S["se"], last_se[kt - 2])
                vx = p.dma(SY, xt[pr].full, x_in[b, 128 * kt:128 * (kt + 1), :])
                if DEBUG and b == 0 and kt == 0 and "afm0" in dbg:
                    p.wait(SY, S["LQ"], va_)
                    p.store(dbg["afm0"], afm[pr].full)
                p.wait(PE, S["LQ"], va_)
                p.wait(PE, S["LG"], vw)
                for dg in range(2):
                    bank = pb[dg]
                    p.wait(PE, S["ve"])
                    for r in range(8):
                        vpe = p.opc(PE, lambda e, r=r, dg=dg, pr=pr, bank=bank: e.matmul(
                            bank[:, 0:512],
                            afm[pr][:, 128 * r:128 * (r + 1)],
                            wo[:, 1024 * r + 512 * dg:1024 * r + 512 * (dg + 1)],
                            start=(r == 0), stop=(r == 7)))
                last_tr[kt] = vpe
                p.wait(VE, S["pe"], last_tr[kt])
                p.wait(VE, S["LQ"], vx)
                if DEBUG and b == 0 and kt == 0 and "op0" in dbg:
                    p.opc(VE, lambda e: e.tensor_copy(q1f[:, 0:512], pb[0][:, 0:512]))
                    p.opc(VE, lambda e: e.tensor_copy(q1f[:, 512:1024], pb[1][:, 0:512]))
                    vtap = p.opc(VE, lambda e: e.engine_nop())
                    p.wait(SY, S["ve"], vtap)
                    p.store(dbg["op0"], q1f.full)
                for dg in range(2):
                    bank = pb[dg]
                    p.opc(VE, lambda e, dg=dg, pr=pr, bank=bank: e.tensor_tensor(
                        smal[:, 0:512], bank[:, 0:512],
                        rep[(b, "ga")][:, 512 * dg:512 * (dg + 1)], ALU.mult))
                    p.opc(VE, lambda e, dg=dg, pr=pr: e.tensor_tensor(
                        xw[pr][:, 512 * dg:512 * (dg + 1)], smal[:, 0:512],
                        xt[pr][:, 512 * dg:512 * (dg + 1)], ALU.add))
                vxw = p.opc(VE, lambda e: e.engine_nop())
                p.wait(SY, S["ve"], vxw)
                p.store(xres[b, 128 * kt:128 * (kt + 1), :], xw[pr].full)
                if DEBUG and b == 0 and kt == 0 and "rep_ga" in dbg:
                    p.store(dbg["rep_ga"], rep[(b, "ga")].full)
                vxn, _ = ln_h(xw[pr], stat[pr], h2tm, rep[(b, "cb1")], rep[(b, "sb")], xnb2)
                last_se[kt] = vxn
                p.opc(VE, lambda e, pr=pr: e.tensor_reduce(stat[pr][:, 20:21], h2tm[:, 0:D], AX.X, ALU.max,
                                                    apply_absolute_value=True))
                p.drain(VE)
                p.opc(VE, lambda e, pr=pr: e.tensor_scalar(stat[pr][:, 21:22], stat[pr][:, 20:21],
                                                    1e-5, None, ALU.max, ALU.bypass))
                p.drain(VE)
                p.opc(VE, lambda e, pr=pr: e.reciprocal(stat[pr][:, 22:23], stat[pr][:, 21:22]))
                p.drain(VE)
                p.opc(VE, lambda e, pr=pr: e.tensor_scalar(stat[pr][:, 23:24], stat[pr][:, 22:23],
                                                    127.0, None, ALU.mult, ALU.bypass))
                p.drain(VE)
                p.opc(VE, lambda e, pr=pr: e.tensor_scalar(q1f.full, h2tm[:, 0:D],
                                                    stat[pr][:, 23:24], MAGIC, ALU.mult, ALU.add))
                p.opc(VE, lambda e: e.tensor_scalar(q1f.full, q1f.full, -MAGIC, 127.0, ALU.add, ALU.min))
                vq = p.opc(VE, lambda e: e.tensor_scalar(h2i.full, q1f.full, -128.0, None, ALU.max, ALU.bypass))
                p.wait(PE, S["ve"], vq)
                for c in range(8):
                    vt = p.opc(PE, lambda e, c=c: e.transpose(
                        pbt[:, 128 * c:128 * (c + 1)], h2i[:, 128 * c:128 * (c + 1)], ident[:, :]))
                p.wait(VE, S["pe"], vt)
                vh2 = p.opc(VE, lambda e: e.tensor_copy(hqs.full, pbt[:, 0:1024]))
                p.wait(SY, S["ve"], vh2)
                p.store(h2q_sp[b, kt], hqs.full)
                st_last[kt] = p.store(rmq_sp[b, kt], stat[pr][:, 21:22])
            A16.release(); A32.release()
            p.barrier()

        def load_quant_bitnet(wdst, w_dram, nr, nelem, mcol):
            ncols = nelem // 128
            A32.mark()
            q1s = A32.alloc(512)
            vw = p.dma(GE, wdst.full.rearrange("p (r o) -> p r o", r=nr),
                       w_dram.rearrange("(r p) o -> p r o", r=nr))
            p.wait(VE, S["LG"], vw)
            p.opc(VE, lambda e: e.memset(mcol[:, 0:1], 0.0))
            p.drain(VE)
            for s0 in range(0, ncols, 512):
                w_ = min(512, ncols - s0)
                p.opc(VE, lambda e, s0=s0, w_=w_: e.tensor_reduce(
                    smal[:, 8:9], wdst[:, s0:s0 + w_], AX.X, ALU.add, apply_absolute_value=True))
                p.drain(VE)
                p.opc(VE, lambda e: e.tensor_tensor(mcol[:, 0:1], mcol[:, 0:1], smal[:, 8:9], ALU.add))
                p.drain(VE)
            vv = p.opc(VE, lambda e: e.engine_nop())
            p.wait(GE, S["ve"], vv)
            vg = p.opc(GE, lambda e: e.tensor_reduce(smal[0:1, 12:13], mcol[:, 0:1], AX.C, ALU.add))
            p.wait(VE, S["ge"], vg)
            p.opc(VE, lambda e: e.tensor_scalar(smal[0:1, 13:14], smal[0:1, 12:13],
                                                1.0 / nelem, 1e-5, ALU.mult, ALU.max))
            p.drain(VE)
            p.opc(VE, lambda e: e.reciprocal(smal[0:1, 14:15], smal[0:1, 13:14]))
            p.drain(VE)
            p.opc(VE, lambda e: e.tensor_scalar(smal[0:1, 15:16], smal[0:1, 13:14],
                                                1.0 / 127.0, None, ALU.mult, ALU.bypass))
            p.drain(VE)
            vv2 = p.opc(VE, lambda e: e.engine_nop())
            p.wait(PE, S["ve"], vv2)
            vpe = p.opc(PE, lambda e: e.matmul(pC[:, 0:2], ones1f[0:1, :], smal[0:1, 14:16],
                                               start=True, stop=True))
            p.wait(VE, S["pe"], vpe)
            p.opc(VE, lambda e: e.tensor_copy(mcol[:, 1:3], pC[:, 0:2]))
            p.drain(VE)
            for s0 in range(0, ncols, 512):
                w_ = min(512, ncols - s0)
                p.opc(VE, lambda e, s0=s0, w_=w_, q1s=q1s: e.tensor_scalar(
                    q1s[:, 0:w_], wdst[:, s0:s0 + w_],
                    mcol[:, 1:2], MAGIC, ALU.mult, ALU.add))
                p.opc(VE, lambda e, s0=s0, w_=w_, q1s=q1s: e.tensor_scalar(
                    q1s[:, 0:w_], q1s[:, 0:w_], -MAGIC, 1.0, ALU.add, ALU.min))
                p.opc(VE, lambda e, s0=s0, w_=w_, q1s=q1s: e.tensor_scalar(
                    wdst[:, s0:s0 + w_], q1s[:, 0:w_], -1.0, None, ALU.max, ALU.bypass))
            vq = p.opc(VE, lambda e: e.engine_nop())
            A32.release()
            return vq

        def p_ffn0_A():
            A16.mark(); A32.mark()
            wgi = A16.alloc(32768)
            wui = A16.alloc(32768)
            h2qt = [A16.alloc(1024) for _ in range(2)]
            gt = A16.alloc(4096)
            ut = A16.alloc(4096)
            tt = A16.alloc(4096)
            mcolg = A32.alloc(4)
            mcolu = A32.alloc(4)
            rmc = [A32.alloc(4) for _ in range(2)]
            last_vt = 0
            load_quant_bitnet(wgi, wgT0, 8, 4096 * 1024, mcolg)
            load_quant_bitnet(wui, wuT0, 8, 4096 * 1024, mcolu)
            p.barrier()
            last_tt = {}
            for i in range(2 * NT):
                b, kt = divmod(i, NT)
                pr = i % 2
                vh_ = p.dma(SY, h2qt[pr].full, h2q_sp[b, kt])
                vr = p.dma(SY, rmc[pr][:, 0:1], rmq_sp[b, kt])
                p.wait(VE, S["LQ"], vr)
                p.opc(VE, lambda e, pr=pr: e.tensor_scalar(
                    rmc[pr][:, 1:2], rmc[pr][:, 0:1], mcolg[:, 2:3], None, ALU.mult, ALU.bypass))
                p.opc(VE, lambda e, pr=pr: e.tensor_scalar(
                    rmc[pr][:, 2:3], rmc[pr][:, 0:1], mcolu[:, 2:3], None, ALU.mult, ALU.bypass))
                p.drain(VE)
                vsc = p.opc(VE, lambda e: e.engine_nop())
                if i >= 1:
                    p.wait(SE, S["ve"], last_vt)
                p.wait(PE, S["LQ"], vh_)
                for which, wsrc, dstb, col in ((0, wgi, gt, 1), (1, wui, ut, 2)):
                    for ig in range(8):
                        bank = pb[(which * 8 + ig) % 4]
                        p.wait(PE, S["se"])
                        for r in range(8):
                            vpe = p.opc(PE, lambda e, r=r, ig=ig, pr=pr, wsrc=wsrc, bank=bank: e.matmul(
                                bank[:, 0:512],
                                h2qt[pr][:, 128 * r:128 * (r + 1)],
                                wsrc[:, 4096 * r + 512 * ig:4096 * r + 512 * (ig + 1)],
                                start=(r == 0), stop=(r == 7)))
                        p.wait(SE, S["pe"], vpe)
                        p.wait(SE, S["ve"], vsc)
                        if which == 0:
                            p.opc(SE, lambda e, ig=ig, pr=pr, bank=bank, col=col: e.activation(
                                gt[:, 512 * ig:512 * (ig + 1)], bank[:, 0:512],
                                AF.Silu, scale=rmc[pr][:, col:col + 1]))
                        else:
                            p.opc(SE, lambda e, ig=ig, pr=pr, bank=bank, col=col: e.activation(
                                ut[:, 512 * ig:512 * (ig + 1)], bank[:, 0:512],
                                AF.Identity, scale=rmc[pr][:, col:col + 1]))
                p.drain(SE)
                vse = p.opc(SE, lambda e: e.nop())
                p.wait(VE, S["se"], vse)
                if i >= 1:
                    p.wait(VE, S["SQ"], last_tt[i - 1])
                vt = p.opc(VE, lambda e: e.tensor_tensor(tt.full, gt.full, ut.full, ALU.mult))
                last_vt = vt
                p.wait(SY, S["ve"], vt)
                last_tt[i] = p.store(t0_sp[b, kt], tt.full)
            A16.release(); A32.release()
            p.barrier()

        def p_ffn0_B(rep, repL1, h3fm):
            A16.mark(); A32.mark()
            wdi = A16.alloc(32768)
            ttl = [A16.alloc(4096) for _ in range(2)]
            ti = A16.alloc(4096)
            tif = A16.alloc(4096)
            h3tm = A16.alloc(1024)
            xnb3 = A16.alloc(1024)
            mcold = A32.alloc(4)
            q1l = A32.alloc(1024)
            xt = [A32.alloc(1024) for _ in range(2)]
            xw = [A32.alloc(1024) for _ in range(2)]
            stat = [A32.alloc(24) for _ in range(2)]
            load_quant_bitnet(wdi, wdT0, 32, 4096 * 1024, mcold)
            p.barrier()
            last_se = {}
            stx_last = {}
            for i in range(2 * NT):
                b, kt = divmod(i, NT)
                pr = i % 2
                if i >= 2 and (i - 2) in last_se:
                    p.wait(SY, S["se"], last_se[i - 2])
                if i >= 2:
                    p.wait(VE, S["SQ"], stx_last[i - 2])
                vtl = p.dma(SY, ttl[pr].full, t0_sp[b, kt])
                vx = p.dma(SY, xt[pr].full, xres[b, 128 * kt:128 * (kt + 1), :])
                p.wait(VE, S["LQ"], vtl)
                p.opc(VE, lambda e, pr=pr: e.tensor_reduce(
                    stat[pr][:, 20:21], ttl[pr].full, AX.X, ALU.max, apply_absolute_value=True))
                p.drain(VE)
                p.opc(VE, lambda e, pr=pr: e.tensor_scalar(
                    stat[pr][:, 21:22], stat[pr][:, 20:21], 1e-5, None, ALU.max, ALU.bypass))
                p.drain(VE)
                p.opc(VE, lambda e, pr=pr: e.reciprocal(stat[pr][:, 22:23], stat[pr][:, 21:22]))
                p.drain(VE)
                p.opc(VE, lambda e, pr=pr: e.tensor_scalar(
                    stat[pr][:, 23:24], stat[pr][:, 22:23], 127.0, None, ALU.mult, ALU.bypass))
                p.opc(VE, lambda e, pr=pr: e.tensor_scalar(
                    stat[pr][:, 19:20], stat[pr][:, 21:22], mcold[:, 2:3], None, ALU.mult, ALU.bypass))
                p.drain(VE)
                for hf in range(4):
                    p.opc(VE, lambda e, hf=hf, pr=pr: e.tensor_scalar(
                        q1l.full, ttl[pr][:, 1024 * hf:1024 * (hf + 1)],
                        stat[pr][:, 23:24], MAGIC, ALU.mult, ALU.add))
                    p.opc(VE, lambda e, hf=hf: e.tensor_scalar(
                        q1l.full, q1l.full, -MAGIC, 127.0, ALU.add, ALU.min))
                    p.opc(VE, lambda e, hf=hf: e.tensor_scalar(
                        ti[:, 1024 * hf:1024 * (hf + 1)], q1l.full, -128.0, None, ALU.max, ALU.bypass))
                vq = p.opc(VE, lambda e: e.engine_nop())
                p.wait(PE, S["ve"], vq)
                for grp in range(4):
                    if grp > 0:
                        p.wait(PE, S["ve"])
                    for c8 in range(8):
                        c = 8 * grp + c8
                        vt = p.opc(PE, lambda e, c=c, c8=c8: e.transpose(
                            pbt[:, 128 * c8:128 * (c8 + 1)], ti[:, 128 * c:128 * (c + 1)], ident[:, :]))
                    p.wait(VE, S["pe"], vt)
                    p.opc(VE, lambda e, grp=grp: e.tensor_copy(
                        tif[:, 1024 * grp:1024 * (grp + 1)], pbt[:, 0:1024]))
                vtf = p.opc(VE, lambda e: e.engine_nop())
                p.wait(PE, S["ve"], vtf)
                for og in range(2):
                    bank = pb[4 + og]
                    p.wait(PE, S["ve"])
                    for r in range(32):
                        vpe = p.opc(PE, lambda e, r=r, og=og, bank=bank: e.matmul(
                            bank[:, 0:512],
                            tif[:, 128 * r:128 * (r + 1)],
                            wdi[:, 1024 * r + 512 * og:1024 * r + 512 * (og + 1)],
                            start=(r == 0), stop=(r == 31)))
                p.wait(VE, S["pe"], vpe)
                p.wait(VE, S["LQ"], vx)
                for og in range(2):
                    bank = pb[4 + og]
                    p.opc(VE, lambda e, og=og, pr=pr, bank=bank: e.tensor_scalar(
                        smal[:, 0:512], bank[:, 0:512], stat[pr][:, 19:20], None, ALU.mult, ALU.bypass))
                    p.opc(VE, lambda e, og=og, bank=bank, b=b: e.tensor_tensor(
                        smal[:, 0:512], smal[:, 0:512], rep[(b, "gb")][:, 512 * og:512 * (og + 1)], ALU.mult))
                    p.opc(VE, lambda e, og=og, pr=pr: e.tensor_tensor(
                        xw[pr][:, 512 * og:512 * (og + 1)], smal[:, 0:512],
                        xt[pr][:, 512 * og:512 * (og + 1)], ALU.add))
                vxw = p.opc(VE, lambda e: e.engine_nop())
                p.wait(SY, S["ve"], vxw)
                stx_last[i] = p.store(xres[b, 128 * kt:128 * (kt + 1), :], xw[pr].full)
                vxn, _ = ln_h(xw[pr], stat[pr], h3tm, repL1[(b, "ca1")], repL1[(b, "sa")], xnb3)
                last_se[i] = vxn
                tm_to_fm(h3tm.full, h3fm[b].full, kt)
            A16.release(); A32.release()
            p.barrier()

        def p_l1_attn(repL1, h3fm, gacc):
            A16.mark(); A32.mark()
            xc = A16.alloc(16 * D)
            proj = A16.alloc(16 * 320)
            dtm = A16.alloc(1024)
            atm = A16.alloc(1024)
            btm = A16.alloc(1024)
            ctm = A16.alloc(1024)
            dsm = A16.alloc(2048)
            bsm = A16.alloc(2048)
            csm = A16.alloc(2048)
            abar = A16.alloc(2048)
            bbar = A16.alloc(2048)
            ssc1 = A16.alloc(2048)
            ssc2 = A16.alloc(2048)
            hsm = ssc1
            ysm = ssc2
            stm = A16.alloc(1024)
            h4tm = A16.alloc(1024)
            xnb4 = A16.alloc(1024)
            h4s = A16.alloc(1024)
            cw = A32.alloc(32)
            cvb = A32.alloc(8)
            ipw = A16.alloc(8 * 320)
            ipbrep = A32.alloc(320)
            woutb = A16.alloc(1024)
            opbrep = A16.alloc(1024)
            acol = A32.alloc(8)
            xt = [A32.alloc(1024) for _ in range(2)]
            xw = [A32.alloc(1024) for _ in range(2)]
            stat = [A32.alloc(24) for _ in range(2)]
            v1 = p.dma(SY, cw.full, convw)
            v2 = p.dma(SY, cvb.full.rearrange("p (q c) -> p q c", q=1),
                       convb.rearrange("q (c p) -> p q c", c=8), allow_slow_non_contiguous=True)
            v3 = p.dma(GE, ipw.full.rearrange("p (r o) -> p r o", r=8),
                       inprojT.rearrange("(r p) o -> p r o", r=8))
            v4 = p.dma(GE, woutb[0:64, 0:1024], outprojT)
            v5 = p.dma(SY, smal[0:1, 20:340], inprojb)
            valog = p.dma(SY, smal[0:64, 344:345], alog)
            p.wait(PE, S["LQ"], v5)
            vpe = p.opc(PE, lambda e: e.matmul(pC[:, 0:320], ones1f[0:1, :], smal[0:1, 20:340],
                                               start=True, stop=True))
            p.wait(VE, S["pe"], vpe)
            p.opc(VE, lambda e: e.tensor_copy(ipbrep.full, pC[:, 0:320]))
            vop = p.dma(GE, row0[0:1, 0:1024], outprojb)
            p.wait(PE, S["LG"], vop)
            p.wait(PE, S["ve"])
            bcast_row(row0, D, opbrep.full)
            p.wait(SE, S["LQ"], valog)
            p.drain(SE)
            vex = p.opc(SE, lambda e: e.activation(acol[0:64, 0:1], smal[0:64, 344:345], AF.Exp))
            p.wait(VE, S["se"], vex)
            p.opc(VE, lambda e: e.tensor_scalar(acol[0:64, 0:1], acol[0:64, 0:1], -1.0, None, ALU.mult, ALU.bypass))
            p.drain(VE)
            p.opc(VE, lambda e: e.tensor_scalar(acol[0:64, 3:4], acol[0:64, 0:1], -1.0, None, ALU.mult, ALU.bypass))
            p.drain(VE)
            p.opc(VE, lambda e: e.tensor_scalar(acol[0:64, 3:4], acol[0:64, 3:4], 1e-4, None, ALU.is_lt, ALU.bypass))
            p.drain(VE)
            p.opc(VE, lambda e: e.tensor_scalar(acol[0:64, 4:5], acol[0:64, 3:4], -1.0, 1.0, ALU.mult, ALU.add))
            p.drain(VE)
            p.opc(VE, lambda e: e.tensor_scalar(acol[0:64, 1:2], acol[0:64, 0:1], acol[0:64, 4:5], None, ALU.mult, ALU.bypass))
            p.drain(VE)
            p.opc(VE, lambda e: e.tensor_tensor(acol[0:64, 1:2], acol[0:64, 1:2], acol[0:64, 3:4], ALU.add))
            p.drain(VE)
            p.opc(VE, lambda e: e.reciprocal(acol[0:64, 2:3], acol[0:64, 1:2]))
            p.drain(VE)
            p.barrier()
            for b in range(NB):
                hb = h3fm[b]
                p.wait(VE, S["LQ"], v2)
                for c in range(8):
                    p.opc(VE, lambda e, c=c, hb=hb: e.tensor_scalar(
                        xc[:, 2048 * c: 2048 * c + 2047], hb[:, 2048 * c: 2048 * c + 2047],
                        cw[:, 4 * c + 1:4 * c + 2], cvb[:, c:c + 1], ALU.mult, ALU.add))
                    p.opc(VE, lambda e, c=c, hb=hb: e.scalar_tensor_tensor(
                        xc[:, 2048 * c: 2048 * c + 2047], hb[:, 2048 * c + 1: 2048 * c + 2048],
                        cw[:, 4 * c + 2:4 * c + 3], xc[:, 2048 * c: 2048 * c + 2047], ALU.mult, ALU.add))
                    p.opc(VE, lambda e, c=c, hb=hb: e.scalar_tensor_tensor(
                        xc[:, 2048 * c + 1: 2048 * c + 2047], hb[:, 2048 * c: 2048 * c + 2046],
                        cw[:, 4 * c:4 * c + 1], xc[:, 2048 * c + 1: 2048 * c + 2047], ALU.mult, ALU.add))
                    p.opc(VE, lambda e, c=c, hb=hb: e.scalar_tensor_tensor(
                        xc[:, 2048 * c: 2048 * c + 2046], hb[:, 2048 * c + 2: 2048 * c + 2048],
                        cw[:, 4 * c + 3:4 * c + 4], xc[:, 2048 * c: 2048 * c + 2046], ALU.mult, ALU.add))
                    p.opc(VE, lambda e, c=c: e.memset(xc[:, 2048 * c + 2047: 2048 * c + 2048], 0.0))
                vconv = p.opc(VE, lambda e: e.engine_nop())
                p.wait(PE, S["ve"], vconv)
                p.wait(PE, S["LG"], v3)
                for kt in range(NT):
                    bank = pb[kt % 2]
                    p.wait(PE, S["ve"])
                    for r in range(8):
                        vpe = p.opc(PE, lambda e, r=r, kt=kt, bank=bank: e.matmul(
                            bank[:, 0:320],
                            xc[:, 2048 * r + 128 * kt: 2048 * r + 128 * (kt + 1)],
                            ipw[:, 320 * r: 320 * (r + 1)],
                            start=(r == 0), stop=(r == 7)))
                    p.wait(VE, S["pe"], vpe)
                    p.opc(VE, lambda e, kt=kt, bank=bank: e.tensor_tensor(
                        proj[:, 320 * kt:320 * (kt + 1)], bank[:, 0:320], ipbrep.full, ALU.add))
                vproj = p.opc(VE, lambda e: e.engine_nop())
                prv = proj.full.rearrange("p (k o) -> p k o", k=16)
                p.opc(VE, lambda e: e.tensor_scalar(atm.full, prv[:, :, 0:64], -1.0, None, ALU.mult, ALU.bypass))
                p.drain(VE)
                p.opc(VE, lambda e: e.tensor_tensor(stm.full, atm.full.rearrange("p (k o) -> p k o", k=16), prv[:, :, 0:64], ALU.max))
                p.drain(VE)
                vabs = p.opc(VE, lambda e: e.engine_nop())
                p.wait(SE, S["ve"], vabs)
                p.drain(SE)
                p.opc(SE, lambda e: e.activation(atm.full, stm.full, AF.Exp, scale=-1.0))
                p.drain(SE)
                vln = p.opc(SE, lambda e: e.activation(stm.full, atm.full, AF.Ln, bias=1.0))
                p.wait(VE, S["se"], vln)
                p.opc(VE, lambda e: e.tensor_scalar(atm.full, prv[:, :, 0:64], 0.0, None, ALU.max, ALU.bypass))
                p.drain(VE)
                p.opc(VE, lambda e: e.tensor_tensor(dtm.full, atm.full, stm.full, ALU.add))
                p.opc(VE, lambda e: e.tensor_tensor(
                    btm.full, prv[:, :, 64:192].rearrange("p k (o two) -> p k o two", two=2)[:, :, :, 0],
                    prv[:, :, 64:192].rearrange("p k (o two) -> p k o two", two=2)[:, :, :, 1], ALU.add))
                p.opc(VE, lambda e: e.tensor_scalar(btm.full, btm.full, 0.5, None, ALU.mult, ALU.bypass))
                p.opc(VE, lambda e: e.tensor_tensor(
                    ctm.full, prv[:, :, 192:320].rearrange("p k (o two) -> p k o two", two=2)[:, :, :, 0],
                    prv[:, :, 192:320].rearrange("p k (o two) -> p k o two", two=2)[:, :, :, 1], ALU.add))
                p.opc(VE, lambda e: e.tensor_scalar(ctm.full, ctm.full, 0.5, None, ALU.mult, ALU.bypass))
                p.drain(VE)
                vtmq = p.opc(VE, lambda e: e.engine_nop())
                p.wait(PE, S["ve"], vtmq)
                for qi, (src, dst) in enumerate(((dtm, dsm), (btm, bsm), (ctm, csm))):
                    for kt in range(NT):
                        slot = kt % 8
                        if kt % 8 == 0:
                            p.wait(PE, S["ve"])
                        vt = p.opc(PE, lambda e, kt=kt, slot=slot, src=src: e.transpose(
                            pbt[0:64, 128 * slot:128 * (slot + 1)],
                            src[:, 64 * kt:64 * (kt + 1)], ident[:, :]))
                        if kt % 8 == 7:
                            p.wait(VE, S["pe"], vt)
                            g8 = kt // 8
                            p.opc(VE, lambda e, g8=g8, dst=dst: e.tensor_copy(
                                dst[0:64, 1024 * g8:1024 * (g8 + 1)], pbt[0:64, 0:1024]))
                vsm = p.opc(VE, lambda e: e.engine_nop())
                p.wait(SE, S["ve"], vsm)
                p.drain(SE)
                vab_ = p.opc(SE, lambda e: e.activation(abar[0:64, 0:2048], dsm[0:64, 0:2048],
                                                        AF.Exp, scale=acol[0:64, 0:1]))
                p.wait(VE, S["se"], vab_)
                p.opc(VE, lambda e: e.tensor_scalar(dsm[0:64, 0:2048], dsm[0:64, 0:2048],
                                                    acol[0:64, 0:1], None, ALU.mult, ALU.bypass))
                p.drain(VE)
                p.opc(VE, lambda e: e.tensor_scalar(ssc1[0:64, 0:2048], abar[0:64, 0:2048],
                                                    -1.0, acol[0:64, 2:3], ALU.add, ALU.mult))
                p.opc(VE, lambda e: e.tensor_scalar(ssc2[0:64, 0:2048], dsm[0:64, 0:2048],
                                                    1.0 / 6.0, 0.5, ALU.mult, ALU.add))
                p.drain(VE)
                p.opc(VE, lambda e: e.tensor_tensor(ssc2[0:64, 0:2048], ssc2[0:64, 0:2048],
                                                    dsm[0:64, 0:2048], ALU.mult))
                p.opc(VE, lambda e: e.tensor_scalar(ssc2[0:64, 0:2048], ssc2[0:64, 0:2048],
                                                    1.0, None, ALU.add, ALU.bypass))
                p.drain(VE)
                p.opc(VE, lambda e: e.tensor_scalar(ssc1[0:64, 0:2048], ssc1[0:64, 0:2048],
                                                    acol[0:64, 4:5], None, ALU.mult, ALU.bypass))
                p.drain(VE)
                p.opc(VE, lambda e: e.scalar_tensor_tensor(ssc1[0:64, 0:2048], ssc2[0:64, 0:2048],
                                                           acol[0:64, 3:4], ssc1[0:64, 0:2048], ALU.mult, ALU.add))
                p.opc(VE, lambda e: e.tensor_tensor(bbar[0:64, 0:2048], ssc1[0:64, 0:2048],
                                                    bsm[0:64, 0:2048], ALU.mult))
                p.drain(VE)
                p.opc(VE, lambda e: e.tensor_tensor_scan(hsm[0:64, 0:2048], abar[0:64, 0:2048],
                                                         bbar[0:64, 0:2048], 0.0, ALU.mult, ALU.add))
                p.drain(VE)
                p.opc(VE, lambda e: e.tensor_tensor(ysm[0:64, 0:2048], csm[0:64, 0:2048],
                                                    hsm[0:64, 0:2048], ALU.mult))
                p.drain(VE)
                vy = p.opc(VE, lambda e: e.engine_nop())
                p.wait(PE, S["ve"], vy)
                p.wait(PE, S["LG"], v4)
                last_se = {}
                st_last = {}
                for kt in range(NT):
                    pr = kt % 2
                    if kt >= 2 and (kt - 2) in last_se:
                        p.wait(SY, S["se"], last_se[kt - 2])
                    if kt >= 2:
                        p.wait(VE, S["SQ"], st_last[kt - 2])
                    vx = p.dma(SY, xt[pr].full, xres[b, 128 * kt:128 * (kt + 1), :])
                    for og in range(2):
                        bank = pb[2 + og]
                        p.wait(PE, S["ve"])
                        vpe = p.opc(PE, lambda e, kt=kt, og=og, bank=bank: e.matmul(
                            bank[:, 0:512],
                            ysm[0:64, 128 * kt:128 * (kt + 1)],
                            woutb[0:64, 512 * og:512 * (og + 1)],
                            start=True, stop=True))
                    p.wait(VE, S["pe"], vpe)
                    p.wait(VE, S["LQ"], vx)
                    for og in range(2):
                        bank = pb[2 + og]
                        p.opc(VE, lambda e, og=og, bank=bank: e.tensor_tensor(
                            smal[:, 0:512], bank[:, 0:512], opbrep[:, 512 * og:512 * (og + 1)], ALU.add))
                        p.opc(VE, lambda e, og=og, b=b: e.tensor_tensor(
                            smal[:, 0:512], smal[:, 0:512], repL1[(b, "ga")][:, 512 * og:512 * (og + 1)], ALU.mult))
                        p.opc(VE, lambda e, og=og, pr=pr: e.tensor_tensor(
                            xw[pr][:, 512 * og:512 * (og + 1)], smal[:, 0:512],
                            xt[pr][:, 512 * og:512 * (og + 1)], ALU.add))
                    vxw = p.opc(VE, lambda e: e.engine_nop())
                    p.wait(SY, S["ve"], vxw)
                    p.store(xres[b, 128 * kt:128 * (kt + 1), :], xw[pr].full)
                    vxn, _ = ln_h(xw[pr], stat[pr], h4tm, repL1[(b, "cb1")], repL1[(b, "sb")], xnb4)
                    last_se[kt] = vxn
                    p.opc(VE, lambda e, pr=pr: e.tensor_reduce(
                        stat[pr][:, 20:21], h4tm[:, 0:D], AX.X, ALU.max, apply_absolute_value=True))
                    p.drain(VE)
                    p.opc(VE, lambda e, pr=pr: e.tensor_tensor(
                        gacc[:, 0:1], gacc[:, 0:1], stat[pr][:, 20:21], ALU.max))
                    p.drain(VE)
                    vq = p.opc(VE, lambda e: e.engine_nop())
                    p.wait(PE, S["ve"], vq)
                    for c in range(8):
                        vt = p.opc(PE, lambda e, c=c: e.transpose(
                            pbt[:, 128 * c:128 * (c + 1)], h4tm[:, 128 * c:128 * (c + 1)], ident[:, :]))
                    p.wait(VE, S["pe"], vt)
                    vh4 = p.opc(VE, lambda e: e.tensor_copy(h4s.full, pbt[:, 0:1024]))
                    p.wait(SY, S["ve"], vh4)
                    st_last[kt] = p.store(h4_sp[b, kt], h4s.full)
            A16.release(); A32.release()
            p.barrier()

        def p_gmax_cc(gacc, ccin, ccout, dst_col):
            vv = p.opc(VE, lambda e: e.engine_nop())
            p.wait(GE, S["ve"], vv)
            p.opc(GE, lambda e: e.tensor_reduce(persc[0:1, 0:1], gacc[:, 0:1], AX.C, ALU.max))
            p.opc(GE, lambda e: e.memset(persc[0:1, 4:12], 0.0))
            vg = p.opc(GE, lambda e: e.tensor_copy(persc[0:1, 4:5], persc[0:1, 0:1]))
            vd = p.dma(GE, ccin, persc[0:1, 4:12])
            p.wait(GE, S["LG"], vd)
            vcc = p.op(GE, lambda e: e.collective_compute(
                "AllReduce", ALU.max, replica_groups=[list(range(8))],
                ins=[ccin], outs=[ccout]), S["CC"])
            p.wait(GE, S["CC"], vcc)
            vb = p.dma(GE, persc[0:1, 4:12], ccout)
            p.wait(VE, S["LG"], vb)
            p.opc(VE, lambda e, dst_col=dst_col: e.tensor_copy(persc[0:1, dst_col:dst_col + 1], persc[0:1, 4:5]))
            p.drain(VE)
            p.barrier()

        def rowmax_to_row(w_nat, nrows, width, rowf):
            A16.mark(); A32.mark()
            wt = A16.alloc(width)
            nt_ = nrows // 128
            sqc = A32.alloc(nt_ + 1)
            for r in range(nt_):
                vw = p.dma(GE, wt.full, w_nat[128 * r:128 * (r + 1), :])
                p.wait(VE, S["LG"], vw)
                p.opc(VE, lambda e, r=r: e.tensor_reduce(
                    sqc[:, r:r + 1], wt.full, AX.X, ALU.max, apply_absolute_value=True))
                p.drain(VE)
                vv = p.opc(VE, lambda e: e.engine_nop())
                p.wait(GE, S["ve"], vv)
            vv = p.opc(VE, lambda e: e.engine_nop())
            p.wait(SY, S["ve"], vv)
            vst = p.store(scr2[:, 0:nt_], sqc[:, 0:nt_])
            p.wait(SY, S["SQ"], vst)
            vrb = p.dma(SY, rowf.h[0:1, rowf.off:rowf.off + nrows].rearrange("q (r p) -> q r p", r=nt_),
                        scr2[:, 0:nt_].rearrange("p r -> r p"), allow_slow_non_contiguous=True)
            p.wait(VE, S["LQ"], vrb)
            A16.release(); A32.release()

        def quant_w_rows(wi, nr, width, rrg):
            A32.mark()
            q1s = A32.alloc(512)
            for r in range(nr):
                for j0 in range(0, width, 512):
                    w_ = min(512, width - j0)
                    p.opc(VE, lambda e, r=r, j0=j0, w_=w_, q1s=q1s: e.tensor_tensor(
                        q1s[:, 0:w_], wi[:, width * r + j0:width * r + j0 + w_],
                        rrg[:, j0:j0 + w_], ALU.mult))
                    p.opc(VE, lambda e, w_=w_, q1s=q1s: e.tensor_scalar(
                        q1s[:, 0:w_], q1s[:, 0:w_], MAGIC, -MAGIC, ALU.add, ALU.add))
                    p.opc(VE, lambda e, w_=w_, q1s=q1s: e.tensor_scalar(
                        q1s[:, 0:w_], q1s[:, 0:w_], 127.0, -128.0, ALU.min, ALU.max))
                    p.opc(VE, lambda e, r=r, j0=j0, w_=w_, q1s=q1s: e.tensor_copy(
                        wi[:, width * r + j0:width * r + j0 + w_], q1s[:, 0:w_]))
            return p.opc(VE, lambda e: e.engine_nop())

        IW = 2816
        IGW = [512, 512, 512, 512, 512, 256]

        def p_ffn1_A(gcol):
            A16.mark(); A32.mark()
            wgi1 = A16.alloc(8 * IW)
            wui1 = A16.alloc(8 * IW)
            rr_g = A16.alloc(IW)
            er_g = A16.alloc(IW)
            rr_u = A16.alloc(IW)
            er_u = A16.alloc(IW)
            h4t = [A16.alloc(1024) for _ in range(2)]
            h4q = A16.alloc(1024)
            gt1 = A16.alloc(IW)
            ut1 = A16.alloc(IW)
            tt1 = A16.alloc(IW)
            cols = A32.alloc(8)
            gacc2 = A32.alloc(4)
            A32.mark()
            rowf = A32.alloc(IW)
            rowr = A32.alloc(IW)
            vcx = p.dma(SY, persc[0:1, 16:17], clipx)
            p.wait(SE, S["LQ"], vcx)
            p.drain(SE)
            vsg = p.opc(SE, lambda e: e.activation(persc[0:1, 17:18], persc[0:1, 16:17], AF.Sigmoid))
            p.wait(VE, S["se"], vsg)
            p.opc(VE, lambda e, gcol=gcol: e.tensor_tensor(
                persc[0:1, 18:19], persc[0:1, 17:18], persc[0:1, gcol:gcol + 1], ALU.mult))
            p.drain(VE)
            p.opc(VE, lambda e: e.tensor_scalar(persc[0:1, 19:20], persc[0:1, 18:19],
                                                1.0 / 127.0, None, ALU.mult, ALU.bypass))
            p.drain(VE)
            p.opc(VE, lambda e: e.reciprocal(persc[0:1, 20:21], persc[0:1, 19:20]))
            p.drain(VE)
            vs4 = p.opc(VE, lambda e: e.engine_nop())
            p.wait(PE, S["ve"], vs4)
            vpe = p.opc(PE, lambda e: e.matmul(pC[:, 0:1], ones1f[0:1, :], persc[0:1, 20:21],
                                               start=True, stop=True))
            p.wait(VE, S["pe"], vpe)
            p.opc(VE, lambda e: e.tensor_copy(cols[:, 0:1], pC[:, 0:1]))
            p.drain(VE)
            for (wn, wT, rr_, er_) in ((wg1n, wgT1, rr_g, er_g), (wu1n, wuT1, rr_u, er_u)):
                rowmax_to_row(wn, IW, 1024, rowf)
                p.opc(VE, lambda e: e.reciprocal(rowr[0:1, 0:IW], rowf[0:1, 0:IW]))
                p.drain(VE)
                p.opc(VE, lambda e: e.tensor_scalar(rowr[0:1, 0:IW], rowr[0:1, 0:IW], 127.0, None, ALU.mult, ALU.bypass))
                p.drain(VE)
                vrr = p.opc(VE, lambda e: e.engine_nop())
                p.wait(PE, S["ve"], vrr)
                for g0 in range(0, IW, 512):
                    w_ = min(512, IW - g0)
                    p.wait(PE, S["ve"])
                    vpe = p.opc(PE, lambda e, g0=g0, w_=w_: e.matmul(
                        pC[:, 0:w_], ones1f[0:1, :], rowr.h[0:1, rowr.off + g0:rowr.off + g0 + w_],
                        start=True, stop=True))
                    p.wait(VE, S["pe"], vpe)
                    p.opc(VE, lambda e, g0=g0, w_=w_, rr_=rr_: e.tensor_copy(rr_[:, g0:g0 + w_], pC[:, 0:w_]))
                p.opc(VE, lambda e: e.tensor_scalar(rowr[0:1, 0:IW], rowf[0:1, 0:IW], persc[0:1, 19:20],
                                                    1.0 / 127.0, ALU.mult, ALU.mult))
                p.drain(VE)
                ver = p.opc(VE, lambda e: e.engine_nop())
                p.wait(PE, S["ve"], ver)
                for g0 in range(0, IW, 512):
                    w_ = min(512, IW - g0)
                    p.wait(PE, S["ve"])
                    vpe = p.opc(PE, lambda e, g0=g0, w_=w_: e.matmul(
                        pC[:, 0:w_], ones1f[0:1, :], rowr.h[0:1, rowr.off + g0:rowr.off + g0 + w_],
                        start=True, stop=True))
                    p.wait(VE, S["pe"], vpe)
                    p.opc(VE, lambda e, g0=g0, w_=w_, er_=er_: e.tensor_copy(er_[:, g0:g0 + w_], pC[:, 0:w_]))
            A32.release()
            q1h = A32.alloc(1024)
            vwg = p.dma(GE, wgi1.full.rearrange("p (r o) -> p r o", r=8),
                        wgT1.rearrange("(r p) o -> p r o", r=8))
            p.wait(VE, S["LG"], vwg)
            quant_w_rows(wgi1, 8, IW, rr_g)
            vwu = p.dma(GE, wui1.full.rearrange("p (r o) -> p r o", r=8),
                        wuT1.rearrange("(r p) o -> p r o", r=8))
            p.wait(VE, S["LG"], vwu)
            vqu = quant_w_rows(wui1, 8, IW, rr_u)
            p.opc(VE, lambda e: e.memset(gacc2[:, 0:1], 0.0))
            p.drain(VE)
            if DEBUG and "erg" in dbg:
                p.store(dbg["erg"], er_g[:, 0:IW])
                p.store(dbg["rrg"], rr_g[:, 0:IW])
            p.barrier()
            last_tt = {}
            last_vt = 0
            for i in range(2 * NT):
                b, kt = divmod(i, NT)
                pr = i % 2
                vh_ = p.dma(SY, h4t[pr].full, h4_sp[b, kt])
                p.wait(VE, S["LQ"], vh_)
                if i >= 1:
                    p.wait(VE, S["pe"], last_mm)
                p.opc(VE, lambda e, pr=pr: e.tensor_scalar(
                    q1h.full, h4t[pr].full, cols[:, 0:1], MAGIC, ALU.mult, ALU.add))
                p.opc(VE, lambda e: e.tensor_scalar(q1h.full, q1h.full, -MAGIC, 127.0, ALU.add, ALU.min))
                vq = p.opc(VE, lambda e: e.tensor_scalar(h4q.full, q1h.full, -128.0, None, ALU.max, ALU.bypass))
                if i >= 1:
                    p.wait(SE, S["ve"], last_vt)
                p.wait(PE, S["ve"], vq)
                for which, wsrc, er_ in ((0, wgi1, er_g), (1, wui1, er_u)):
                    j0 = 0
                    for gi, gw in enumerate(IGW):
                        bank = pb[(which * 6 + gi) % 4]
                        p.wait(PE, S["ve"])
                        p.wait(PE, S["se"])
                        for r in range(8):
                            vpe = p.opc(PE, lambda e, r=r, j0=j0, gw=gw, wsrc=wsrc, bank=bank: e.matmul(
                                bank[:, 0:gw],
                                h4q[:, 128 * r:128 * (r + 1)],
                                wsrc[:, IW * r + j0:IW * r + j0 + gw],
                                start=(r == 0), stop=(r == 7)))
                        p.wait(VE, S["pe"], vpe)
                        if which == 0:
                            p.opc(VE, lambda e, j0=j0, gw=gw, bank=bank: e.tensor_tensor(
                                smal[:, 0:gw], bank[:, 0:gw], er_g[:, j0:j0 + gw], ALU.mult))
                            p.drain(VE)
                            vsm = p.opc(VE, lambda e: e.engine_nop())
                            p.wait(SE, S["ve"], vsm)
                            p.opc(SE, lambda e, j0=j0, gw=gw: e.activation(
                                gt1[:, j0:j0 + gw], smal[:, 0:gw], AF.Silu))
                        else:
                            p.opc(VE, lambda e, j0=j0, gw=gw, bank=bank: e.tensor_tensor(
                                ut1[:, j0:j0 + gw], bank[:, 0:gw], er_u[:, j0:j0 + gw], ALU.mult))
                        j0 += gw
                last_mm = vpe
                p.drain(SE)
                vse = p.opc(SE, lambda e: e.nop())
                p.wait(VE, S["se"], vse)
                if i >= 1:
                    p.wait(VE, S["SQ"], last_tt[i - 1])
                vt = p.opc(VE, lambda e: e.tensor_tensor(tt1.full, gt1.full, ut1.full, ALU.mult))
                last_vt = vt
                p.drain(VE)
                p.opc(VE, lambda e: e.tensor_reduce(smal[:, 8:9], tt1.full, AX.X, ALU.max,
                                                    apply_absolute_value=True))
                p.drain(VE)
                p.opc(VE, lambda e: e.tensor_tensor(gacc2[:, 0:1], gacc2[:, 0:1], smal[:, 8:9], ALU.max))
                p.drain(VE)
                vt2 = p.opc(VE, lambda e: e.engine_nop())
                p.wait(SY, S["ve"], vt2)
                last_tt[i] = p.store(t1_sp[b, kt], tt1.full)
                if DEBUG and i == 0 and "tt1" in dbg:
                    p.store(dbg["tt1"], tt1.full)
                    p.store(dbg["h4q0"], h4q.full)
            p.barrier()
            p_gmax_cc(gacc2, cc_in2, cc_out2, 30)
            A16.release(); A32.release()
            p.barrier()

        def p_ffn1_B(repg):
            A16.mark(); A32.mark()
            wdi1 = A16.alloc(22 * 1024)
            rr_d = A16.alloc(1024)
            er_d = A16.alloc(1024)
            ttl = [A16.alloc(IW) for _ in range(2)]
            ti1 = A16.alloc(IW)
            tif1 = A16.alloc(IW)
            q1s2 = A32.alloc(512)
            cols = A32.alloc(8)
            xt = [A32.alloc(1024) for _ in range(2)]
            A32.mark()
            rowf = A32.alloc(1024)
            rowr = A32.alloc(1024)
            vcx = p.dma(SY, persc[0:1, 16:17], clipd)
            p.wait(SE, S["LQ"], vcx)
            p.drain(SE)
            vsg = p.opc(SE, lambda e: e.activation(persc[0:1, 17:18], persc[0:1, 16:17], AF.Sigmoid))
            p.wait(VE, S["se"], vsg)
            p.opc(VE, lambda e: e.tensor_tensor(
                persc[0:1, 18:19], persc[0:1, 17:18], persc[0:1, 30:31], ALU.mult))
            p.drain(VE)
            p.opc(VE, lambda e: e.tensor_scalar(persc[0:1, 19:20], persc[0:1, 18:19],
                                                1.0 / 127.0, None, ALU.mult, ALU.bypass))
            p.drain(VE)
            p.opc(VE, lambda e: e.reciprocal(persc[0:1, 20:21], persc[0:1, 19:20]))
            p.drain(VE)
            vs5 = p.opc(VE, lambda e: e.engine_nop())
            p.wait(PE, S["ve"], vs5)
            vpe = p.opc(PE, lambda e: e.matmul(pC[:, 0:1], ones1f[0:1, :], persc[0:1, 20:21],
                                               start=True, stop=True))
            p.wait(VE, S["pe"], vpe)
            p.opc(VE, lambda e: e.tensor_copy(cols[:, 0:1], pC[:, 0:1]))
            p.drain(VE)
            rowmax_to_row(wd1n, 1024, IW, rowf)
            p.opc(VE, lambda e: e.reciprocal(rowr[0:1, 0:1024], rowf[0:1, 0:1024]))
            p.drain(VE)
            p.opc(VE, lambda e: e.tensor_scalar(rowr[0:1, 0:1024], rowr[0:1, 0:1024], 127.0, None, ALU.mult, ALU.bypass))
            p.drain(VE)
            vrr = p.opc(VE, lambda e: e.engine_nop())
            p.wait(PE, S["ve"], vrr)
            for g0 in range(0, 1024, 512):
                p.wait(PE, S["ve"])
                vpe = p.opc(PE, lambda e, g0=g0: e.matmul(
                    pC[:, 0:512], ones1f[0:1, :], rowr.h[0:1, rowr.off + g0:rowr.off + g0 + 512],
                    start=True, stop=True))
                p.wait(VE, S["pe"], vpe)
                p.opc(VE, lambda e, g0=g0: e.tensor_copy(rr_d[:, g0:g0 + 512], pC[:, 0:512]))
            p.opc(VE, lambda e: e.tensor_scalar(rowr[0:1, 0:1024], rowf[0:1, 0:1024], persc[0:1, 19:20],
                                                1.0 / 127.0, ALU.mult, ALU.mult))
            p.drain(VE)
            ver = p.opc(VE, lambda e: e.engine_nop())
            p.wait(PE, S["ve"], ver)
            for g0 in range(0, 1024, 512):
                p.wait(PE, S["ve"])
                vpe = p.opc(PE, lambda e, g0=g0: e.matmul(
                    pC[:, 0:512], ones1f[0:1, :], rowr.h[0:1, rowr.off + g0:rowr.off + g0 + 512],
                    start=True, stop=True))
                p.wait(VE, S["pe"], vpe)
                p.opc(VE, lambda e, g0=g0: e.tensor_copy(er_d[:, g0:g0 + 512], pC[:, 0:512]))
            A32.release()
            vwd = p.dma(GE, wdi1.full.rearrange("p (r o) -> p r o", r=22),
                        wdT1.rearrange("(r p) o -> p r o", r=22))
            p.wait(VE, S["LG"], vwd)
            quant_w_rows(wdi1, 22, 1024, rr_d)
            p.barrier()
            last_se = {}
            sty_last = {}
            nloop = 0 if STOP_AFTER == "ffnBsetup" else (2 if STOP_AFTER == "ffnB2" else 2 * NT)
            for i in range(nloop):
                b, kt = divmod(i, NT)
                pr = i % 2
                if i >= 2 and (i - 2) in last_se:
                    p.wait(SY, S["ve"], last_se[i - 2])
                    p.wait(SY, S["SQ"], sty_last[i - 2])
                vtl = p.dma(SY, ttl[pr].full, t1_sp[b, kt])
                vx = p.dma(SY, xt[pr].full, xres[b, 128 * kt:128 * (kt + 1), :])
                p.wait(VE, S["LQ"], vtl)
                for j0 in range(0, IW, 512):
                    w_ = min(512, IW - j0)
                    p.opc(VE, lambda e, j0=j0, w_=w_, pr=pr: e.tensor_scalar(
                        q1s2[:, 0:w_], ttl[pr][:, j0:j0 + w_], cols[:, 0:1], MAGIC, ALU.mult, ALU.add))
                    p.opc(VE, lambda e, w_=w_: e.tensor_scalar(
                        q1s2[:, 0:w_], q1s2[:, 0:w_], -MAGIC, 127.0, ALU.add, ALU.min))
                    p.opc(VE, lambda e, j0=j0, w_=w_: e.tensor_scalar(
                        ti1[:, j0:j0 + w_], q1s2[:, 0:w_], -128.0, None, ALU.max, ALU.bypass))
                vq = p.opc(VE, lambda e: e.engine_nop())
                p.wait(PE, S["ve"], vq)
                for grp in range(3):
                    n8 = 8 if grp < 2 else 6
                    if grp > 0:
                        p.wait(PE, S["ve"])
                    for c8 in range(n8):
                        c = 8 * grp + c8
                        vt = p.opc(PE, lambda e, c=c, c8=c8: e.transpose(
                            pbt[:, 128 * c8:128 * (c8 + 1)], ti1[:, 128 * c:128 * (c + 1)], ident[:, :]))
                    p.wait(VE, S["pe"], vt)
                    p.opc(VE, lambda e, grp=grp, n8=n8: e.tensor_copy(
                        tif1[:, 1024 * grp:1024 * grp + 128 * n8], pbt[:, 0:128 * n8]))
                vtf = p.opc(VE, lambda e: e.engine_nop())
                p.wait(PE, S["ve"], vtf)
                for og in range(2):
                    bank = pb[4 + og]
                    p.wait(PE, S["ve"])
                    for r in range(22):
                        vpe = p.opc(PE, lambda e, r=r, og=og, bank=bank: e.matmul(
                            bank[:, 0:512],
                            tif1[:, 128 * r:128 * (r + 1)],
                            wdi1[:, 1024 * r + 512 * og:1024 * r + 512 * (og + 1)],
                            start=(r == 0), stop=(r == 21)))
                p.wait(VE, S["pe"], vpe)
                p.wait(VE, S["LQ"], vx)
                for og in range(2):
                    bank = pb[4 + og]
                    p.opc(VE, lambda e, og=og, bank=bank: e.tensor_tensor(
                        smal[:, 0:512], bank[:, 0:512], er_d[:, 512 * og:512 * (og + 1)], ALU.mult))
                    p.opc(VE, lambda e, og=og, b=b: e.tensor_tensor(
                        smal[:, 0:512], smal[:, 0:512], repg[(b, "gb")][:, 512 * og:512 * (og + 1)], ALU.mult))
                    p.opc(VE, lambda e, og=og: e.tensor_copy(q1s2[:, 0:512], smal[:, 0:512]))
                    vfin = p.opc(VE, lambda e, og=og, pr=pr: e.tensor_tensor(
                        xt[pr][:, 512 * og:512 * (og + 1)], q1s2[:, 0:512],
                        xt[pr][:, 512 * og:512 * (og + 1)], ALU.add))
                last_se[i] = vfin
                p.wait(SY, S["ve"], vfin)
                sty = p.store(y_out[b, 128 * kt:128 * (kt + 1), :], xt[pr].full)
                sty_last[i] = sty
            A16.release(); A32.release()
            p.barrier()

        def p_h1(b, rep, h1fm, h1tm, xnb):
            A32.mark()
            xt = [A32.alloc(1024) for _ in range(2)]
            stat = [A32.alloc(24) for _ in range(2)]
            last_se = {}
            for k in range(NT):
                if k >= 2 and (k - 2) in last_se:
                    p.wait(SY, S["se"], last_se[k - 2])
                vld = p.dma(SY, xt[k % 2].full, x_in[b, 128 * k:128 * (k + 1), :])
                p.wait(VE, S["LQ"], vld)
                vxn, _ = ln_h(xt[k % 2], stat[k % 2], h1tm, rep[(b, "ca1")], rep[(b, "sa")], xnb)
                last_se[k] = vxn
                tm_to_fm(h1tm.full, h1fm, k)
            A32.release()
            p.barrier()

        cst = {"negl": A32.alloc(4), "w1m": A16.alloc(64)}
        p_init()
        p_attn_consts(cst)
        p_mod(0, c0T, wmod0)
        p_mod(1, c1T, wmod1)
        for b in range(NB):
            A16.mark(); A32.mark()
            qfm = A16.alloc(8 * T); kfm = A16.alloc(8 * T)
            vab = A16.alloc(16 * 1040); qn = A16.alloc(8 * T)
            negkb = A32.alloc(32)
            A16.mark(); A32.mark()
            h1fm = A16.alloc(8 * T)
            A16.mark()
            rep1 = {(b, "ca1"): A16.alloc(D), (b, "sa"): A16.alloc(D)}
            build_rep(0, rep1)
            h1tm = A16.alloc(D); xnb = A16.alloc(D)
            p_h1(b, rep1, h1fm.full, h1tm, xnb)
            A16.release()
            p_qkv(b, h1fm.full, qfm, kfm, vab, qn, negkb)
            A16.release(); A32.release()
            p_core(b, qfm, kfm, vab, qn, negkb, cst)
            A16.release(); A32.release()
            A16.mark(); A32.mark()
            rep2 = {(b, "ga"): A16.alloc(D), (b, "cb1"): A16.alloc(D), (b, "sb"): A16.alloc(D)}
            build_rep(0, rep2)
            p_attn_out(b, rep2)
            A16.release(); A32.release()

        if DEBUG and "x1" in dbg:
            for b in range(NB):
                p.store(dbg["x1"][b], xres[b])
            p.barrier()

        p_ffn0_A()
        A16.mark()
        h3fm = [A16.alloc(8 * T) for _ in range(NB)]
        A16.mark(); A32.mark()
        repB = {}
        repL1 = {}
        for b in range(NB):
            repB[(b, "gb")] = A16.alloc(D)
            repL1[(b, "ca1")] = A16.alloc(D)
            repL1[(b, "sa")] = A16.alloc(D)
        build_rep(0, repB)
        build_rep(1, repL1)
        if DEBUG and "repgb" in dbg:
            p.barrier()
            for b in range(NB):
                p.store(dbg["repgb"][b], repB[(b, "gb")].full)
            for b in range(NB):
                p.store(dbg["repca1"][b], repL1[(b, "ca1")].full)
        p_ffn0_B(repB, repL1, h3fm)
        A16.release(); A32.release()

        if DEBUG and "t0" in dbg:
            for b in range(NB):
                p.store(dbg["t0"][b], t0_sp[b, 0])
        if DEBUG and "rmq" in dbg:
            for b in range(NB):
                for kt in range(NT):
                    p.store(dbg["rmq"][b, kt], rmq_sp[b, kt])
        if DEBUG and "h2q" in dbg:
            for b in range(NB):
                for kt in range(2):
                    p.store(dbg["h2q"][b, kt], h2q_sp[b, kt])
        if DEBUG and "x2" in dbg:
            for b in range(NB):
                p.store(dbg["x2"][b], xres[b])
        if DEBUG and "h3fm" in dbg:
            for c in range(8):
                p.store(dbg["h3fm"][c], h3fm[0][:, T * c:T * (c + 1)])
        if DEBUG and ("x2" in dbg or "h3fm" in dbg):
            p.barrier()

        gacc1 = A32.alloc(4)
        p.opc(VE, lambda e: e.memset(gacc1[:, 0:1], 0.0))
        p.drain(VE)
        A16.mark(); A32.mark()
        repL1b = {}
        for b in range(NB):
            repL1b[(b, "ga")] = A16.alloc(D)
            repL1b[(b, "cb1")] = A16.alloc(D)
            repL1b[(b, "sb")] = A16.alloc(D)
        build_rep(1, repL1b)
        p_l1_attn(repL1b, h3fm, gacc1)
        A16.release(); A32.release()
        A16.release()

        if DEBUG and "x3" in dbg:
            for b in range(NB):
                p.store(dbg["x3"][b], xres[b])
            p.barrier()

        if STOP_AFTER not in ("x3",):
            p_gmax_cc(gacc1, cc_in1, cc_out1, 21)
        if STOP_AFTER not in ("x3", "cc1"):
            p_ffn1_A(21)
        if STOP_AFTER not in ("x3", "cc1", "ffnA"):
            A16.mark(); A32.mark()
            repg1 = {}
            for b in range(NB):
                repg1[(b, "gb")] = A16.alloc(D)
            build_rep(1, repg1)
            p_ffn1_B(repg1)
            A16.release(); A32.release()
        if DEBUG and "attnsp" in dbg:
            for b in range(NB):
                p.store(dbg["attnsp"][b], attn_sp[b])
        if DEBUG and "scol" in dbg:
            p.store(dbg["scol"], persc[0:1, 0:32])
        p.barrier()
        p.wait(SY, S["SQ"])
        p.wait(SY, S["LQ"])

        with nc.Block() as block:
            @block.tensor
            def _(e):
                for f in p.streams[PE]: f(e)

            @block.vector
            def _(e):
                for f in p.streams[VE]: f(e)

            @block.scalar
            def _(e):
                for f in p.streams[SE]: f(e)

            @block.gpsimd
            def _(e):
                for f in p.streams[GE]: f(e)

            @block.sync
            def _(e):
                for f in p.streams[SY]: f(e)

    return nc


def prep_inputs(I):
    C = np.ascontiguousarray
    shared = {
        "wqT": C(I["L0_attn_Wq"].T), "wkT": C(I["L0_attn_Wk"].T),
        "wvT": C(I["L0_attn_Wv"].T), "woT": C(I["L0_attn_Wo"].T),
        "lamq1": C(I["L0_attn_lambda_q1"].reshape(32, 1)),
        "lamk1": C(I["L0_attn_lambda_k1"].reshape(32, 1)),
        "lamq2": C(I["L0_attn_lambda_q2"].reshape(32, 1)),
        "lamk2": C(I["L0_attn_lambda_k2"].reshape(32, 1)),
        "sublnw": C(I["L0_attn_subln_w"].reshape(1, 64)),
        "wgT0": C(I["L0_ffn_w_gate"].T), "wuT0": C(I["L0_ffn_w_up"].T),
        "wdT0": C(I["L0_ffn_w_down"].T),
        "wmod0": C(I["L0_wmod"]),
        "convw": C(I["L1_attn_conv_w"][:, 0, :].reshape(8, 128, 4).transpose(1, 0, 2).reshape(128, 32)),
        "convb": C(I["L1_attn_conv_b"].reshape(1, D)),
        "inprojT": C(I["L1_attn_in_proj_w"].T),
        "inprojb": C(I["L1_attn_in_proj_b"].reshape(1, 320)),
        "alog": C(I["L1_attn_A_log"].reshape(64, 1)),
        "outprojT": C(I["L1_attn_out_proj_w"].T),
        "outprojb": C(I["L1_attn_out_proj_b"].reshape(1, D)),
        "clipx": C(I["L1_ffn_clip_x"].reshape(1, 1)),
        "clipd": C(I["L1_ffn_clip_down"].reshape(1, 1)),
        "wgT1": C(I["L1_ffn_w_gate"].T), "wuT1": C(I["L1_ffn_w_up"].T),
        "wdT1": C(I["L1_ffn_w_down"].T),
        "wg1n": C(I["L1_ffn_w_gate"]), "wu1n": C(I["L1_ffn_w_up"]),
        "wd1n": C(I["L1_ffn_w_down"]),
        "wmod1": C(I["L1_wmod"]),
    }
    c0T = C(I["L0_c"].T)
    c1T = C(I["L1_c"].T)
    per_core = []
    for cid in range(8):
        m = dict(shared)
        m["x_in"] = C(I["L0_attn_x"][2 * cid:2 * cid + 2])
        m["c0T"] = C(c0T[:, 2 * cid:2 * cid + 2])
        m["c1T"] = C(c1T[:, 2 * cid:2 * cid + 2])
        per_core.append(m)
    return per_core


def run(I, debug_names=()):
    in_maps = prep_inputs(I)
    nc = build_program(debug_names=debug_names)
    return run_bass_kernel_spmd(nc, in_maps, core_ids=list(range(8)))

def kernel(**inputs):
    in_maps = prep_inputs(inputs)
    nc = build_program()
    res = run_bass_kernel_spmd(nc, in_maps, core_ids=list(range(8)))
    return np.concatenate([np.asarray(r["y"]) for r in res.results], axis=0)


# version 02
# speedup vs baseline: 1.0038x; 1.0038x over previous
import math
import contextlib
import numpy as np
import concourse.bass as bass
import concourse.mybir as mybir
from concourse.bass_utils import run_bass_kernel_spmd

F32 = mybir.dt.float32
BF16 = mybir.dt.bfloat16
AF = mybir.ActivationFunctionType
ALU = mybir.AluOpType
AX = mybir.AxisListType

ENGINES = ("tensor", "vector", "scalar", "gpsimd", "sync")
PE, VE, SE, GE, SY = ENGINES
NB = 2
T = 2048
D = 1024
NT = T // 128
LAMBDA_INIT = 0.8 - 0.6 * math.exp(-0.3 * 12)
MAGIC = float(np.float32(1.5 * 2 ** 23))

DEBUG = False
STOP_AFTER = None


class Sem:
    def __init__(self, name):
        self.name = name
        self.h = None
        self.n = 0


class Buf:
    def __init__(self, h, off, cols, dt):
        self.h = h; self.off = off; self.cols = cols; self.dt = dt

    def __getitem__(self, idx):
        ps, fs = idx
        a = fs.start or 0
        b = fs.stop if fs.stop is not None else self.cols
        assert fs.step is None and 0 <= a <= b <= self.cols, (a, b, self.cols)
        return self.h[ps, self.off + a: self.off + b]

    @property
    def full(self):
        return self.h[:, self.off: self.off + self.cols]

    def sub(self, a, b):
        return Buf(self.h, self.off + a, b - a, self.dt)


class Arena:
    def __init__(self, h, n, dt, name):
        self.h = h; self.n = n; self.dt = dt; self.off = 0; self.name = name
        self.marks = []; self.hi = 0

    def alloc(self, cols, align=4):
        o = (self.off + align - 1) // align * align
        assert o + cols <= self.n, f"arena {self.name} overflow: {o}+{cols} > {self.n}"
        self.off = o + cols
        self.hi = max(self.hi, self.off)
        return Buf(self.h, o, cols, self.dt)

    def mark(self):
        self.marks.append(self.off)

    def release(self):
        self.off = self.marks.pop()


class Prog:
    def __init__(self, nc):
        self.nc = nc
        self.streams = {e: [] for e in ENGINES}
        self.sems = {}
        for s in ("pe", "ve", "se", "ge", "sy", "LQ", "LG", "SQ", "CC"):
            self.sems[s] = Sem(s)
        self._emap = {PE: "pe", VE: "ve", SE: "se", GE: "ge", SY: "sy"}

    def em(self, eng, fn):
        self.streams[eng].append(fn)

    def esem(self, eng):
        return self.sems[self._emap[eng]]

    def op(self, eng, fn, sem=None):
        if sem is not None:
            sem.n += 1
            v = sem.n
            self.em(eng, lambda e, fn=fn, s=sem: fn(e).then_inc(s.h, 1))
            return v
        self.em(eng, fn)
        return None

    def opc(self, eng, fn):
        return self.op(eng, fn, self.esem(eng))

    def wait(self, eng, sem, val=None):
        v = sem.n if val is None else val
        self.em(eng, lambda e, s=sem, v=v: e.wait_ge(s.h, v))

    def drain(self, eng):
        self.em(eng, lambda e: e.drain())

    def dma(self, eng, out, in_, sem=None, **kw):
        sem = sem or (self.sems["LG"] if eng == GE else self.sems["LQ"])
        sem.n += 16
        v = sem.n
        self.em(eng, lambda e, o=out, i=in_, s=sem, kw=kw: e.dma_start(out=o, in_=i, **kw).then_inc(s.h, 16))
        return v

    def store(self, out, in_, eng=SY):
        return self.dma(eng, out, in_, sem=self.sems["SQ"])

    def barrier(self):
        self.wait(SY, self.sems["LQ"])
        self.wait(SY, self.sems["SQ"])
        self.wait(GE, self.sems["LG"])
        vals = {}
        for eng in ENGINES:
            s = self.esem(eng)
            s.n += 1
            vals[eng] = s.n
            if eng in (GE, VE):
                self.em(eng, lambda e, s=s: e.engine_nop().then_inc(s.h, 1))
            elif eng in (PE, SE):
                self.drain(eng)
                self.em(eng, lambda e, s=s: e.nop().then_inc(s.h, 1))
            else:
                self.em(eng, lambda e, s=s: e.nop().then_inc(s.h, 1))
        for eng in ENGINES:
            for oth in ENGINES:
                if oth != eng:
                    self.wait(eng, self.esem(oth), vals[oth])


def build_program(debug_names=()):
    nc = bass.Bass("TRN2", target_bir_lowering=False)
    p = Prog(nc)
    S = p.sems

    dr = {}
    def din(name, shape, dt=F32):
        dr[name] = nc.dram_tensor(name, list(shape), dt, kind="ExternalInput").ap()
        return dr[name]
    def dscr(name, shape, dt=F32):
        dr[name] = nc.dram_tensor(name, list(shape), dt).ap()
        return dr[name]

    x_in = din("x_in", (NB, T, D))
    wqT = din("wqT", (D, D)); wkT = din("wkT", (D, D)); wvT = din("wvT", (D, D)); woT = din("woT", (D, D))
    lamq1 = din("lamq1", (32, 1)); lamk1 = din("lamk1", (32, 1))
    lamq2 = din("lamq2", (32, 1)); lamk2 = din("lamk2", (32, 1))
    sublnw = din("sublnw", (1, 64))
    wgT0 = din("wgT0", (D, 4096)); wuT0 = din("wuT0", (D, 4096)); wdT0 = din("wdT0", (4096, D))
    c0T = din("c0T", (D, NB)); wmod0 = din("wmod0", (D, 6 * D))
    convw = din("convw", (128, 32)); convb = din("convb", (1, D))
    inprojT = din("inprojT", (D, 320)); inprojb = din("inprojb", (1, 320))
    alog = din("alog", (64, 1))
    outprojT = din("outprojT", (64, D)); outprojb = din("outprojb", (1, D))
    clipx = din("clipx", (1, 1)); clipd = din("clipd", (1, 1))
    wgT1 = din("wgT1", (D, 2816)); wuT1 = din("wuT1", (D, 2816)); wdT1 = din("wdT1", (2816, D))
    wg1n = din("wg1n", (2816, D)); wu1n = din("wu1n", (2816, D)); wd1n = din("wd1n", (D, 2816))
    c1T = din("c1T", (D, NB)); wmod1 = din("wmod1", (D, 6 * D))

    y_out = nc.dram_tensor("y", [NB, T, D], F32, kind="ExternalOutput").ap()

    xres = dscr("xres", (NB, T, D))
    modscr = dscr("modscr", (2, NB, 6 * D), BF16)
    h2q_sp = dscr("h2q_sp", (NB, NT, 128, 1024), BF16)
    h4_sp = dscr("h4_sp", (NB, NT, 128, 1024), BF16)
    t0_sp = dscr("t0_sp", (NB, NT, 128, 4096), BF16)
    t1_sp = dscr("t1_sp", (NB, NT, 128, 2816), BF16)
    cc_in1 = dscr("cc_in1", (1, 8)); cc_out1 = dscr("cc_out1", (1, 8))
    cc_in2 = dscr("cc_in2", (1, 8)); cc_out2 = dscr("cc_out2", (1, 8))
    negscr = dscr("negscr", (4, 8))
    scr2 = dscr("scr2", (128, 32))
    rmq_sp = dscr("rmq_sp", (NB, NT, 128, 1))
    attn_sp = dscr("attn_sp", (NB, T, D), BF16)

    dbg = {}
    if DEBUG:
        for name, shape, dt in debug_names:
            dbg[name] = nc.dram_tensor("dbg_" + name, list(shape), dt, kind="ExternalOutput").ap()

    es = contextlib.ExitStack()
    sb = lambda n, s, d: es.enter_context(nc.sbuf_tensor(n, s, d))
    psm = lambda n, s, d: es.enter_context(nc.psum_tensor(n, s, d))

    with es:
        A16 = Arena(sb("arena16", [128, 91300], BF16), 91300, BF16, "A16")
        A32 = Arena(sb("arena32", [128, 6200], F32), 6200, F32, "A32")
        ident = sb("ident", [128, 128], BF16)
        ones1 = sb("ones1", [1, 128], BF16)
        ones1f = sb("ones1f", [1, 128], F32)
        ones4 = sb("ones4", [128, 4], BF16)
        io4 = sb("io4", [128, 4], F32)
        row0 = sb("row0", [1, 1024], BF16)
        smal = sb("smal", [128, 512], F32)
        persc = sb("persc", [128, 40], F32)
        pA = psm("pA", [128, 1536], F32)
        pB = psm("pB", [128, 1536], F32)
        pC = psm("pC", [128, 512], F32)
        pbt = psm("pbt", [128, 1024], BF16)
        pb = [Buf(pA, 512 * i, 512, F32) for i in range(3)] + \
             [Buf(pB, 512 * i, 512, F32) for i in range(3)] + [Buf(pC, 0, 512, F32)]

        for name, s in S.items():
            s.h = es.enter_context(nc.semaphore(name))

        def p_init():
            p.op(GE, lambda e: e.memset(ident.ap(), 0.0))
            p.opc(GE, lambda e: e.affine_select(
                out=ident[:, :], in_=ident[:, :], compare_op=ALU.not_equal,
                fill=1.0, base=0, pattern=[[-1, 128]], channel_multiplier=1))
            p.opc(GE, lambda e: e.memset(ones1.ap(), 1.0))
            p.opc(GE, lambda e: e.memset(ones1f.ap(), 1.0))
            p.barrier()

        def bcast_row(src_row, n, dst, pbank=6, f32=False):
            one = ones1f if f32 else ones1
            for g0 in range(0, n, 512):
                w = min(512, n - g0)
                p.wait(PE, S["ve"])
                vpe = p.opc(PE, lambda e, g0=g0, w=w, one=one: e.matmul(
                    pb[pbank][:, 0:w], one[0:1, :], src_row[0:1, g0:g0 + w],
                    start=True, stop=True))
                p.wait(VE, S["pe"], vpe)
                p.opc(VE, lambda e, g0=g0, w=w: e.tensor_copy(dst[:, g0:g0 + w], pb[pbank][:, 0:w]))

        def p_mod(L, cT, wmod):
            A16.mark(); A32.mark()
            cs = A32.alloc(16)
            csf = A32.alloc(16)
            mrow = A32.alloc(1024)
            mrow2 = A32.alloc(1024)
            wm = A32.alloc(8 * 512)
            modsb = A16.alloc(6 * D)
            v = p.dma(SY, cs.full.rearrange("p (c t) -> p c t", c=8),
                      cT.rearrange("(c p) t -> p c t", c=8),
                      allow_slow_non_contiguous=True)
            p.wait(SE, S["LQ"], v)
            vsil = p.opc(SE, lambda e: e.activation(csf.full, cs.full, AF.Silu))
            p.wait(PE, S["se"], vsil)
            evac_v = {}
            for g in range(12):
                if g >= 1:
                    p.wait(SY, S["pe"])
                vld = p.dma(SY, wm.full, wmod[:, 512 * g:512 * (g + 1)].rearrange("(c p) m -> p c m", c=8))
                p.wait(PE, S["LQ"], vld)
                if g >= 2:
                    p.wait(PE, S["ve"], evac_v[g - 2])
                for c in range(8):
                    vpe = p.opc(PE, lambda e, c=c, g=g: e.matmul(
                        pb[g % 2][0:2, :], csf[:, 2 * c:2 * c + 2],
                        wm.full.rearrange("p (c m) -> p c m", c=8)[:, c, :],
                        start=(c == 0), stop=(c == 7)))
                p.wait(VE, S["pe"], vpe)
                evac_v[g] = p.opc(VE, lambda e, g=g: e.tensor_copy(
                    modsb[0:2, 512 * g:512 * (g + 1)], pb[g % 2][0:2, :]))
            for sl in (1, 4):
                p.opc(VE, lambda e, sl=sl: e.tensor_scalar(
                    modsb[0:2, sl * D:(sl + 1) * D], modsb[0:2, sl * D:(sl + 1) * D],
                    1.0, None, ALU.add, ALU.bypass))
            for sl in (2, 5):
                p.opc(VE, lambda e, sl=sl: e.tensor_scalar(
                    mrow[0:2, 0:D], modsb[0:2, sl * D:(sl + 1) * D], 0.0, None, ALU.is_ge, ALU.bypass))
                p.drain(VE)
                p.opc(VE, lambda e: e.tensor_scalar(
                    mrow2[0:2, 0:D], mrow[0:2, 0:D], 2.0, -1.0, ALU.mult, ALU.add))
                p.opc(VE, lambda e, sl=sl: e.tensor_scalar(
                    mrow[0:2, 0:D], modsb[0:2, sl * D:(sl + 1) * D], -1.0, None, ALU.mult, ALU.bypass))
                p.drain(VE)
                p.opc(VE, lambda e, sl=sl: e.tensor_tensor(
                    mrow[0:2, 0:D], mrow[0:2, 0:D], modsb[0:2, sl * D:(sl + 1) * D], ALU.max))
                p.drain(VE)
                p.opc(VE, lambda e: e.tensor_scalar(
                    mrow[0:2, 0:D], mrow[0:2, 0:D], 0.25, None, ALU.max, ALU.bypass))
                p.drain(VE)
                p.opc(VE, lambda e, sl=sl: e.tensor_tensor(
                    modsb[0:2, sl * D:(sl + 1) * D], mrow2[0:2, 0:D], mrow[0:2, 0:D], ALU.mult))
                p.drain(VE)
            vrows = p.opc(VE, lambda e: e.engine_nop())
            p.wait(SY, S["ve"], vrows)
            p.store(modscr[L], modsb[0:2, :])
            if DEBUG and L == 0 and "modrow" in dbg:
                p.store(dbg["modrow"], modsb[0:2, :])
            if DEBUG and L == 0 and "csb" in dbg:
                p.store(dbg["csb"], csf.full)
            A16.release(); A32.release()
            p.barrier()

        SLOT_OF = {"sa": 0, "ca1": 1, "ga": 2, "sb": 3, "cb1": 4, "gb": 5}

        def build_rep(L, rep):
            for (b, kind), dst in rep.items():
                sl = SLOT_OF[kind]
                p.wait(SY, S["pe"])
                vr = p.dma(SY, row0[0:1, 0:D], modscr[L, b, sl * D:(sl + 1) * D][None, :])
                p.wait(PE, S["LQ"], vr)
                bcast_row(row0, D, dst.full)
            p.barrier()

        def ln_h(xt, stat, htm, ca1, sa_, xnb):
            p.opc(VE, lambda e: e.bn_stats(stat[:, 0:6], xt[:, 0:512]))
            p.opc(VE, lambda e: e.bn_stats(stat[:, 6:12], xt[:, 512:1024]))
            p.drain(VE)
            p.opc(VE, lambda e: e.bn_aggr(stat[:, 12:14], stat[:, 0:12]))
            p.drain(VE)
            p.opc(VE, lambda e: e.tensor_scalar(stat[:, 14:15], stat[:, 13:14], 1e-5, None, ALU.add, ALU.bypass))
            p.drain(VE)
            vv = p.opc(VE, lambda e: e.reciprocal(stat[:, 15:16], stat[:, 14:15]))
            p.wait(SE, S["ve"], vv)
            p.drain(SE)
            vs = p.opc(SE, lambda e: e.activation(stat[:, 16:17], stat[:, 15:16], AF.Sqrt))
            p.wait(VE, S["se"], vs)
            p.opc(VE, lambda e: e.tensor_scalar(stat[:, 17:18], stat[:, 12:13], stat[:, 16:17], -1.0, ALU.mult, ALU.mult))
            p.drain(VE)
            vv2 = p.opc(VE, lambda e: e.engine_nop())
            p.wait(SE, S["ve"], vv2)
            p.drain(SE)
            vxn = p.opc(SE, lambda e: e.activation(xnb.full, xt[:, 0:D], AF.Identity,
                                                   bias=stat[:, 17:18], scale=stat[:, 16:17]))
            p.wait(VE, S["se"], vxn)
            p.opc(VE, lambda e: e.tensor_tensor(htm[:, 0:D], xnb.full, ca1.full, ALU.mult))
            vtt = p.opc(VE, lambda e: e.tensor_tensor(htm[:, 0:D], htm[:, 0:D], sa_.full, ALU.add))
            return vxn, vtt

        def tm_to_fm(src, dst_fm, k, nchunk=8, dst_stride=T):
            vv = p.opc(VE, lambda e: e.engine_nop())
            p.wait(PE, S["ve"], vv)
            for c in range(nchunk):
                p.opc(PE, lambda e, c=c: e.transpose(
                    pbt[:, 128 * c:128 * (c + 1)], src[:, 128 * c:128 * (c + 1)], ident[:, :]))
            p.wait(VE, S["pe"])
            for c in range(nchunk):
                p.opc(VE, lambda e, c=c: e.tensor_copy(
                    dst_fm[:, dst_stride * c + 128 * k: dst_stride * c + 128 * (k + 1)],
                    pbt[:, 128 * c:128 * (c + 1)]))

        def p_attn_consts(cst):
            p.opc(GE, lambda e: e.iota(io4.ap(), pattern=[[-32, 4]], base=0,
                                       channel_multiplier=1, allow_small_or_imprecise_dtypes=True))
            p.wait(VE, S["ge"])
            p.opc(VE, lambda e: e.tensor_scalar(smal[:, 0:4], io4[:, :], 0.0, None, ALU.is_ge, ALU.bypass))
            p.drain(VE)
            p.opc(VE, lambda e: e.tensor_scalar(smal[:, 4:8], io4[:, :], 31.5, None, ALU.is_le, ALU.bypass))
            p.drain(VE)
            p.opc(VE, lambda e: e.tensor_tensor(ones4[:, :], smal[:, 0:4], smal[:, 4:8], ALU.mult))
            p.drain(VE)
            v = p.dma(SY, smal[0:32, 8:9], lamq1)
            v = p.dma(SY, smal[0:32, 9:10], lamk1)
            v = p.dma(SY, smal[0:32, 10:11], lamq2)
            v = p.dma(SY, smal[0:32, 11:12], lamk2)
            sw = p.dma(SY, smal[0:1, 16:80], sublnw)
            p.wait(VE, S["LQ"], v)
            p.opc(VE, lambda e: e.tensor_tensor(smal[0:32, 12:13], smal[0:32, 8:9], smal[0:32, 9:10], ALU.mult))
            p.opc(VE, lambda e: e.tensor_tensor(smal[0:32, 13:14], smal[0:32, 10:11], smal[0:32, 11:12], ALU.mult))
            p.drain(VE)
            vv = p.opc(VE, lambda e: e.engine_nop())
            p.wait(GE, S["ve"], vv)
            p.opc(GE, lambda e: e.tensor_reduce(smal[0:1, 14:15], smal[0:32, 12:13], AX.C, ALU.add))
            vg = p.opc(GE, lambda e: e.tensor_reduce(smal[0:1, 15:16], smal[0:32, 13:14], AX.C, ALU.add))
            p.wait(SE, S["ge"], vg)
            p.drain(SE)
            p.opc(SE, lambda e: e.activation(smal[0:1, 14:15], smal[0:1, 14:15], AF.Exp))
            p.drain(SE)
            vs = p.opc(SE, lambda e: e.activation(smal[0:1, 15:16], smal[0:1, 15:16], AF.Exp))
            p.wait(VE, S["se"], vs)
            p.opc(VE, lambda e: e.tensor_tensor(smal[0:1, 14:15], smal[0:1, 14:15], smal[0:1, 15:16], ALU.subtract))
            p.drain(VE)
            p.opc(VE, lambda e: e.tensor_scalar(smal[0:1, 14:15], smal[0:1, 14:15],
                                                LAMBDA_INIT, -1.0, ALU.add, ALU.mult))
            p.drain(VE)
            p.wait(VE, S["LQ"], sw)
            p.opc(VE, lambda e: e.tensor_scalar(smal[0:1, 16:80], smal[0:1, 16:80],
                                                1.0 - LAMBDA_INIT, None, ALU.mult, ALU.bypass))
            p.drain(VE)
            vv2 = p.opc(VE, lambda e: e.engine_nop())
            p.wait(PE, S["ve"], vv2)
            vpe = p.opc(PE, lambda e: e.matmul(pC[:, 0:1], ones1f[0:1, :], smal[0:1, 14:15],
                                               start=True, stop=True))
            vpe = p.opc(PE, lambda e: e.matmul(pC[:, 64:128], ones1f[0:1, :], smal[0:1, 16:80],
                                               start=True, stop=True))
            p.wait(VE, S["pe"], vpe)
            p.opc(VE, lambda e: e.tensor_copy(cst["negl"][:, 0:1], pC[:, 0:1]))
            p.opc(VE, lambda e: e.tensor_copy(cst["w1m"][:, 0:64], pC[:, 64:128]))
            p.barrier()

        def p_qkv(b, h1fm, qfm, kfm, vab, qn, negkb):
            A16.mark(); A32.mark()
            wbuf = A16.alloc(8192)
            q2b = A16.alloc(512)
            kmx = A32.alloc(40)
            DINV = float(1.0 / math.sqrt(32.0))

            def proj_fm(dst, w_dram, scale, qn_dst):
                vw = p.dma(GE, wbuf.full.rearrange("p (r o) -> p r o", r=8),
                           w_dram.rearrange("(r p) o -> p r o", r=8))
                evac = {}
                sq_ev = {}
                for i, (c, g) in enumerate([(c, g) for c in range(8) for g in range(4)]):
                    bank = pb[i % 2]
                    p.wait(PE, S["LG"], vw)
                    if i >= 2:
                        p.wait(PE, S["se"], evac[i - 2])
                    for r in range(8):
                        vpe = p.opc(PE, lambda e, r=r, c=c, g=g, bank=bank: e.matmul(
                            bank[:, 0:512],
                            wbuf[:, 1024 * r + 128 * c: 1024 * r + 128 * (c + 1)],
                            h1fm[:, 2048 * r + 512 * g: 2048 * r + 512 * (g + 1)],
                            start=(r == 0), stop=(r == 7)))
                    p.wait(SE, S["pe"], vpe)
                    evac[i] = p.opc(SE, lambda e, c=c, g=g, bank=bank: e.activation(
                        dst[:, 2048 * c + 512 * g: 2048 * c + 512 * (g + 1)],
                        bank[:, 0:512], AF.Identity, scale=scale))
                    if qn_dst is not None:
                        sq_ev[i] = p.opc(SE, lambda e, c=c, g=g: e.activation(
                            q2b[:, 0:512],
                            dst[:, 2048 * c + 512 * g: 2048 * c + 512 * (g + 1)], AF.Square))
                        p.wait(PE, S["se"], sq_ev[i])
                        if qn_dst == "kmx":
                            p.wait(PE, S["ve"])
                        elif i >= 1:
                            p.wait(PE, S["se"])
                        vn = p.opc(PE, lambda e: e.matmul(
                            pC[0:4, 0:512], ones4[:, :], q2b[:, 0:512], start=True, stop=True))
                        if qn_dst == "kmx":
                            p.wait(VE, S["pe"], vn)
                            p.opc(VE, lambda e, i=i: e.tensor_reduce(
                                kmx[0:4, i:i + 1], pC[0:4, 0:512], AX.X, ALU.max))
                        else:
                            p.wait(SE, S["pe"], vn)
                            p.opc(SE, lambda e, c=c, g=g: e.activation(
                                qn_dst[0:4, 2048 * c + 512 * g: 2048 * c + 512 * (g + 1)],
                                pC[0:4, 0:512], AF.Sqrt))
                p.barrier()

            proj_fm(qfm.full, wqT, DINV, qn)
            proj_fm(kfm.full, wkT, 1.0, "kmx")
            p.drain(VE)
            p.opc(VE, lambda e: e.tensor_reduce(
                kmx[0:4, 32:40],
                kmx[0:4, 0:32].rearrange("p (c g) -> p c g", c=8),
                AX.X, ALU.max))
            p.drain(VE)
            vv = p.opc(VE, lambda e: e.engine_nop())
            p.wait(SE, S["ve"], vv)
            p.drain(SE)
            vs = p.opc(SE, lambda e: e.activation(kmx[0:4, 32:40], kmx[0:4, 32:40], AF.Sqrt))
            p.wait(VE, S["se"], vs)
            p.opc(VE, lambda e: e.tensor_scalar(kmx[0:4, 32:40], kmx[0:4, 32:40], -1.0, None, ALU.mult, ALU.bypass))
            p.drain(VE)
            vneg = p.opc(VE, lambda e: e.engine_nop())
            p.wait(SY, S["ve"], vneg)
            vst = p.store(negscr, kmx[0:4, 32:40])
            p.wait(SY, S["SQ"], vst)
            vrow = p.dma(SY, smal[0:1, 96:128], negscr.rearrange("r c -> c r"),
                         allow_slow_non_contiguous=True)
            p.wait(PE, S["LQ"], vrow)
            vpe = p.opc(PE, lambda e: e.matmul(pC[:, 128:160], ones1f[0:1, :], smal[0:1, 96:128],
                                               start=True, stop=True))
            p.wait(VE, S["pe"], vpe)
            p.opc(VE, lambda e: e.tensor_copy(negkb[:, 0:32], pC[:, 128:160]))
            vw = p.dma(GE, wbuf.full.rearrange("p (r o) -> p r o", r=8),
                       wvT.rearrange("(r p) o -> p r o", r=8))
            p.opc(VE, lambda e: e.memset(
                vab.full.rearrange("p (k h o) -> p k h o", k=16, h=16)[:, :, :, 64:65], 1.0))
            evac = {}
            for i, (kt, dg) in enumerate([(kt, dg) for kt in range(16) for dg in range(2)]):
                bank = pb[2 + i % 2]
                p.wait(PE, S["LG"], vw)
                if i >= 2:
                    p.wait(PE, S["ve"], evac[i - 2])
                for r in range(8):
                    vpe = p.opc(PE, lambda e, r=r, kt=kt, dg=dg, bank=bank: e.matmul(
                        bank[:, 0:512],
                        h1fm[:, 2048 * r + 128 * kt: 2048 * r + 128 * (kt + 1)],
                        wbuf[:, 1024 * r + 512 * dg: 1024 * r + 512 * (dg + 1)],
                        start=(r == 0), stop=(r == 7)))
                p.wait(VE, S["pe"], vpe)
                evac[i] = p.opc(VE, lambda e, kt=kt, dg=dg, bank=bank: e.tensor_copy(
                    vab.full.rearrange("p (k h o) -> p k h o", k=16, h=16)[:, kt, 8 * dg:8 * (dg + 1), 0:64],
                    bank[:, 0:512].rearrange("p (h o) -> p h o", h=8)))
            if DEBUG and b == 0:
                p.barrier()
                if "qfm" in dbg: p.store(dbg["qfm"], qfm.full)
                if "kfm" in dbg: p.store(dbg["kfm"], kfm.full)
                if "vab" in dbg: p.store(dbg["vab"], vab.full)
                if "qn" in dbg: p.store(dbg["qn"][0:4, :], qn[0:4, 0:8 * T])
                if "negkb" in dbg: p.store(dbg["negkb"], negkb.full)
            A16.release(); A32.release()
            p.barrier()

        def p_core(b, qfm, kfm, vab, qn, negkb, cst):
            A16.mark(); A32.mark()
            E2 = [A16.alloc(8192) for _ in range(2)]
            kpair = A16.alloc(2048)
            qpair = A16.alloc(2048)
            stage = A16.alloc(4096)
            dump = A16.alloc(64)
            o0sb = A32.alloc(280)
            comb = A32.alloc(200)
            vcols = A32.alloc(256)
            p.opc(VE, lambda e: e.memset(kpair[32:33, 0:2048], 1.0))
            p.opc(VE, lambda e: e.memset(kpair[96:97, 0:2048], 1.0))
            last_pair_use = 0
            last_pC_use = 0
            exp_slot = {0: 0, 1: 0}
            scount = 0
            st_done = 0
            for qg in range(4):
                for vh in range(16):
                    h2a, h2b = 2 * vh, 2 * vh + 1
                    ca_, ra = h2a // 4, h2a % 4
                    cb_, rb = h2b // 4, h2b % 4
                    p.wait(GE, S["pe"], last_pair_use)
                    vp = p.dma(GE, kpair[0:32, 0:2048], kfm.h[32 * ra:32 * ra + 32, kfm.off + 2048 * ca_: kfm.off + 2048 * (ca_ + 1)])
                    vp = p.dma(GE, kpair[64:96, 0:2048], kfm.h[32 * rb:32 * rb + 32, kfm.off + 2048 * cb_: kfm.off + 2048 * (cb_ + 1)])
                    vp = p.dma(GE, qpair[0:32, 0:2048], qfm.h[32 * ra:32 * ra + 32, qfm.off + 2048 * ca_: qfm.off + 2048 * (ca_ + 1)])
                    vp = p.dma(GE, qpair[64:96, 0:2048], qfm.h[32 * rb:32 * rb + 32, qfm.off + 2048 * cb_: qfm.off + 2048 * (cb_ + 1)])
                    vp = p.dma(GE, qpair[32:33, 0:2048], qn.h[ra:ra + 1, qn.off + 2048 * ca_: qn.off + 2048 * (ca_ + 1)])
                    vp = p.dma(GE, qpair[96:97, 0:2048], qn.h[rb:rb + 1, qn.off + 2048 * cb_: qn.off + 2048 * (cb_ + 1)])
                    p.wait(VE, S["LG"], vp)
                    p.opc(VE, lambda e, h2a=h2a: e.tensor_scalar(
                        qpair[32:33, 0:2048], qpair[32:33, 0:2048],
                        negkb[32:33, h2a:h2a + 1], None, ALU.mult, ALU.bypass))
                    p.opc(VE, lambda e, h2b=h2b: e.tensor_scalar(
                        qpair[96:97, 0:2048], qpair[96:97, 0:2048],
                        negkb[96:97, h2b:h2b + 1], None, ALU.mult, ALU.bypass))
                    p.drain(VE)
                    vqp = p.opc(VE, lambda e: e.engine_nop())
                    for sh in (0, 1):
                        E = E2[sh]
                        jbase = 0
                        for si, sw_ in enumerate((3, 3, 3, 3, 3, 1)):
                            ps = (pA, pB)[scount % 2]
                            p.wait(PE, S["se"], exp_slot[scount % 2])
                            p.wait(PE, S["ve"], vqp)
                            p.wait(PE, S["LG"], vp)
                            for jj in range(sw_):
                                j = jbase + jj
                                vpe = p.opc(PE, lambda e, jj=jj, j=j, sh=sh, ps=ps, qg=qg: e.matmul(
                                    ps[:, 512 * jj:512 * (jj + 1)],
                                    kpair[64 * sh:64 * sh + 33, 128 * j:128 * (j + 1)],
                                    qpair[64 * sh:64 * sh + 33, 512 * qg:512 * (qg + 1)],
                                    start=True, stop=True))
                            last_pair_use = vpe
                            p.wait(SE, S["pe"], vpe)
                            exp_slot[scount % 2] = p.opc(SE, lambda e, ps=ps, jbase=jbase, sw_=sw_, E=E: e.activation(
                                E[:, 512 * jbase:512 * (jbase + sw_)], ps[:, 0:512 * sw_], AF.Exp))
                            jbase += sw_
                            scount += 1
                    for sh in (0, 1):
                        E = E2[sh]
                        p.wait(PE, S["se"], exp_slot[0])
                        p.wait(PE, S["se"], exp_slot[1])
                        p.wait(PE, S["ve"], last_pC_use)
                        for qs in range(4):
                            for j in range(16):
                                vpe = p.opc(PE, lambda e, j=j, qs=qs, vh=vh, E=E: e.matmul(
                                    pC[:, 128 * qs:128 * qs + 65],
                                    E[:, 512 * j + 128 * qs:512 * j + 128 * (qs + 1)],
                                    vab[:, 1040 * j + 65 * vh:1040 * j + 65 * (vh + 1)],
                                    start=(j == 0), stop=(j == 15), skip_group_check=True))
                        p.wait(VE, S["pe"], vpe)
                        if sh == 0:
                            for qs in range(4):
                                p.opc(VE, lambda e, qs=qs: e.tensor_copy(
                                    o0sb[:, 70 * qs:70 * qs + 65], pC[:, 128 * qs:128 * qs + 65]))
                            vo0 = p.opc(VE, lambda e: e.engine_nop())
                            last_pC_use = vo0
                        else:
                            for qs in range(4):
                                p.opc(VE, lambda e, qs=qs: e.reciprocal(
                                    comb[:, 128:129], o0sb[:, 70 * qs + 64:70 * qs + 65]))
                                p.opc(VE, lambda e, qs=qs: e.reciprocal(
                                    comb[:, 129:130], pC[:, 128 * qs + 64:128 * qs + 65]))
                                p.drain(VE)
                                p.opc(VE, lambda e, qs=qs: e.tensor_scalar(
                                    comb[:, 0:64], o0sb[:, 70 * qs:70 * qs + 64],
                                    comb[:, 128:129], None, ALU.mult, ALU.bypass))
                                p.opc(VE, lambda e, qs=qs: e.tensor_scalar(
                                    comb[:, 64:128], pC[:, 128 * qs:128 * qs + 64],
                                    comb[:, 129:130], cst["negl"][:, 0:1], ALU.mult, ALU.mult))
                                p.drain(VE)
                                vA = p.opc(VE, lambda e, qs=qs, vh=vh: e.tensor_tensor(
                                    stage[:, 1024 * qs + 64 * vh:1024 * qs + 64 * (vh + 1)],
                                    comb[:, 0:64], comb[:, 64:128], ALU.add))
                                p.wait(SE, S["ve"], vA)
                                p.opc(SE, lambda e, qs=qs, vh=vh: e.activation(
                                    dump[:, 0:64],
                                    stage[:, 1024 * qs + 64 * vh:1024 * qs + 64 * (vh + 1)],
                                    AF.Square, accum_out=vcols[:, 16 * qs + vh:16 * qs + vh + 1]))
                            vcomb = p.opc(VE, lambda e: e.engine_nop())
                            last_pC_use = vcomb
                p.wait(VE, S["se"])
                p.opc(VE, lambda e: e.tensor_scalar(
                    vcols[:, 64:128], vcols[:, 0:64], 1.0 / 64.0, 1e-5, ALU.mult, ALU.add))
                p.drain(VE)
                vu = p.opc(VE, lambda e: e.engine_nop())
                p.wait(SE, S["ve"], vu)
                p.drain(SE)
                vsq = p.opc(SE, lambda e: e.activation(
                    vcols[:, 192:256], vcols[:, 64:128], AF.Sqrt))
                p.wait(VE, S["se"], vsq)
                vrc = p.opc(VE, lambda e: e.reciprocal(vcols[:, 128:192], vcols[:, 192:256]))
                p.drain(VE)
                for qs in range(4):
                    for vh in range(16):
                        idx = 16 * qs + vh
                        p.opc(VE, lambda e, qs=qs, vh=vh, idx=idx: e.scalar_tensor_tensor(
                            stage[:, 1024 * qs + 64 * vh:1024 * qs + 64 * (vh + 1)],
                            stage[:, 1024 * qs + 64 * vh:1024 * qs + 64 * (vh + 1)],
                            vcols[:, 128 + idx:128 + idx + 1],
                            cst["w1m"][:, 0:64], ALU.mult, ALU.mult))
                vnrm = p.opc(VE, lambda e: e.engine_nop())
                p.wait(SY, S["ve"], vnrm)
                for qs in range(4):
                    st_done = p.store(attn_sp[b, 512 * qg + 128 * qs:512 * qg + 128 * (qs + 1), :],
                                      stage[:, 1024 * qs:1024 * (qs + 1)])
                p.wait(VE, S["SQ"], st_done)
            A16.release(); A32.release()
            p.barrier()

        def p_attn_out(b, rep):
            A16.mark(); A32.mark()
            wo = A16.alloc(8192)
            afm = [A16.alloc(1024) for _ in range(2)]
            h2tm = A16.alloc(1024)
            h2i = A16.alloc(1024)
            hqs = A16.alloc(1024)
            xnb2 = A16.alloc(1024)
            xt = [A32.alloc(1024) for _ in range(2)]
            xw = [A32.alloc(1024) for _ in range(2)]
            stat = [A32.alloc(24) for _ in range(2)]
            q1f = A32.alloc(1024)
            vw = p.dma(GE, wo.full.rearrange("p (r o) -> p r o", r=8),
                       woT.rearrange("(r p) o -> p r o", r=8))
            last_se = {}
            last_tr = {}
            st_last = {}
            for kt in range(NT):
                pr = kt % 2
                if kt >= 1:
                    p.wait(VE, S["SQ"], st_last[kt - 1])
                if kt >= 2:
                    p.wait(SY, S["pe"], last_tr[kt - 2])
                va_ = None
                for c in range(8):
                    va_ = p.dma(SY, afm[pr][:, 128 * c:128 * (c + 1)],
                                attn_sp[b, 128 * kt:128 * (kt + 1), 128 * c:128 * (c + 1)],
                                transpose=True)
                if kt >= 2 and (kt - 2) in last_se:
                    p.wait(SY, S["se"], last_se[kt - 2])
                vx = p.dma(SY, xt[pr].full, x_in[b, 128 * kt:128 * (kt + 1), :])
                if DEBUG and b == 0 and kt == 0 and "afm0" in dbg:
                    p.wait(SY, S["LQ"], va_)
                    p.store(dbg["afm0"], afm[pr].full)
                p.wait(PE, S["LQ"], va_)
                p.wait(PE, S["LG"], vw)
                for dg in range(2):
                    bank = pb[dg]
                    p.wait(PE, S["ve"])
                    for r in range(8):
                        vpe = p.opc(PE, lambda e, r=r, dg=dg, pr=pr, bank=bank: e.matmul(
                            bank[:, 0:512],
                            afm[pr][:, 128 * r:128 * (r + 1)],
                            wo[:, 1024 * r + 512 * dg:1024 * r + 512 * (dg + 1)],
                            start=(r == 0), stop=(r == 7)))
                last_tr[kt] = vpe
                p.wait(VE, S["pe"], last_tr[kt])
                p.wait(VE, S["LQ"], vx)
                if DEBUG and b == 0 and kt == 0 and "op0" in dbg:
                    p.opc(VE, lambda e: e.tensor_copy(q1f[:, 0:512], pb[0][:, 0:512]))
                    p.opc(VE, lambda e: e.tensor_copy(q1f[:, 512:1024], pb[1][:, 0:512]))
                    vtap = p.opc(VE, lambda e: e.engine_nop())
                    p.wait(SY, S["ve"], vtap)
                    p.store(dbg["op0"], q1f.full)
                for dg in range(2):
                    bank = pb[dg]
                    p.opc(VE, lambda e, dg=dg, pr=pr, bank=bank: e.tensor_tensor(
                        smal[:, 0:512], bank[:, 0:512],
                        rep[(b, "ga")][:, 512 * dg:512 * (dg + 1)], ALU.mult))
                    p.opc(VE, lambda e, dg=dg, pr=pr: e.tensor_tensor(
                        xw[pr][:, 512 * dg:512 * (dg + 1)], smal[:, 0:512],
                        xt[pr][:, 512 * dg:512 * (dg + 1)], ALU.add))
                vxw = p.opc(VE, lambda e: e.engine_nop())
                p.wait(SY, S["ve"], vxw)
                p.store(xres[b, 128 * kt:128 * (kt + 1), :], xw[pr].full)
                if DEBUG and b == 0 and kt == 0 and "rep_ga" in dbg:
                    p.store(dbg["rep_ga"], rep[(b, "ga")].full)
                vxn, _ = ln_h(xw[pr], stat[pr], h2tm, rep[(b, "cb1")], rep[(b, "sb")], xnb2)
                last_se[kt] = vxn
                p.opc(VE, lambda e, pr=pr: e.tensor_reduce(stat[pr][:, 20:21], h2tm[:, 0:D], AX.X, ALU.max,
                                                    apply_absolute_value=True))
                p.drain(VE)
                p.opc(VE, lambda e, pr=pr: e.tensor_scalar(stat[pr][:, 21:22], stat[pr][:, 20:21],
                                                    1e-5, None, ALU.max, ALU.bypass))
                p.drain(VE)
                p.opc(VE, lambda e, pr=pr: e.reciprocal(stat[pr][:, 22:23], stat[pr][:, 21:22]))
                p.drain(VE)
                p.opc(VE, lambda e, pr=pr: e.tensor_scalar(stat[pr][:, 23:24], stat[pr][:, 22:23],
                                                    127.0, None, ALU.mult, ALU.bypass))
                p.drain(VE)
                p.opc(VE, lambda e, pr=pr: e.tensor_scalar(q1f.full, h2tm[:, 0:D],
                                                    stat[pr][:, 23:24], MAGIC, ALU.mult, ALU.add))
                p.opc(VE, lambda e: e.tensor_scalar(q1f.full, q1f.full, -MAGIC, 127.0, ALU.add, ALU.min))
                vq = p.opc(VE, lambda e: e.tensor_scalar(h2i.full, q1f.full, -128.0, None, ALU.max, ALU.bypass))
                p.wait(PE, S["ve"], vq)
                for c in range(8):
                    vt = p.opc(PE, lambda e, c=c: e.transpose(
                        pbt[:, 128 * c:128 * (c + 1)], h2i[:, 128 * c:128 * (c + 1)], ident[:, :]))
                p.wait(VE, S["pe"], vt)
                vh2 = p.opc(VE, lambda e: e.tensor_copy(hqs.full, pbt[:, 0:1024]))
                p.wait(SY, S["ve"], vh2)
                p.store(h2q_sp[b, kt], hqs.full)
                st_last[kt] = p.store(rmq_sp[b, kt], stat[pr][:, 21:22])
            A16.release(); A32.release()
            p.barrier()

        def load_quant_bitnet(wdst, w_dram, nr, nelem, mcol):
            ncols = nelem // 128
            A32.mark()
            q1s = A32.alloc(512)
            vw = p.dma(GE, wdst.full.rearrange("p (r o) -> p r o", r=nr),
                       w_dram.rearrange("(r p) o -> p r o", r=nr))
            p.wait(VE, S["LG"], vw)
            p.opc(VE, lambda e: e.memset(mcol[:, 0:1], 0.0))
            p.drain(VE)
            for s0 in range(0, ncols, 512):
                w_ = min(512, ncols - s0)
                p.opc(VE, lambda e, s0=s0, w_=w_: e.tensor_reduce(
                    smal[:, 8:9], wdst[:, s0:s0 + w_], AX.X, ALU.add, apply_absolute_value=True))
                p.drain(VE)
                p.opc(VE, lambda e: e.tensor_tensor(mcol[:, 0:1], mcol[:, 0:1], smal[:, 8:9], ALU.add))
                p.drain(VE)
            vv = p.opc(VE, lambda e: e.engine_nop())
            p.wait(GE, S["ve"], vv)
            vg = p.opc(GE, lambda e: e.tensor_reduce(smal[0:1, 12:13], mcol[:, 0:1], AX.C, ALU.add))
            p.wait(VE, S["ge"], vg)
            p.opc(VE, lambda e: e.tensor_scalar(smal[0:1, 13:14], smal[0:1, 12:13],
                                                1.0 / nelem, 1e-5, ALU.mult, ALU.max))
            p.drain(VE)
            p.opc(VE, lambda e: e.reciprocal(smal[0:1, 14:15], smal[0:1, 13:14]))
            p.drain(VE)
            p.opc(VE, lambda e: e.tensor_scalar(smal[0:1, 15:16], smal[0:1, 13:14],
                                                1.0 / 127.0, None, ALU.mult, ALU.bypass))
            p.drain(VE)
            vv2 = p.opc(VE, lambda e: e.engine_nop())
            p.wait(PE, S["ve"], vv2)
            vpe = p.opc(PE, lambda e: e.matmul(pC[:, 0:2], ones1f[0:1, :], smal[0:1, 14:16],
                                               start=True, stop=True))
            p.wait(VE, S["pe"], vpe)
            p.opc(VE, lambda e: e.tensor_copy(mcol[:, 1:3], pC[:, 0:2]))
            p.drain(VE)
            for s0 in range(0, ncols, 512):
                w_ = min(512, ncols - s0)
                p.opc(VE, lambda e, s0=s0, w_=w_, q1s=q1s: e.tensor_scalar(
                    q1s[:, 0:w_], wdst[:, s0:s0 + w_],
                    mcol[:, 1:2], MAGIC, ALU.mult, ALU.add))
                p.opc(VE, lambda e, s0=s0, w_=w_, q1s=q1s: e.tensor_scalar(
                    q1s[:, 0:w_], q1s[:, 0:w_], -MAGIC, 1.0, ALU.add, ALU.min))
                p.opc(VE, lambda e, s0=s0, w_=w_, q1s=q1s: e.tensor_scalar(
                    wdst[:, s0:s0 + w_], q1s[:, 0:w_], -1.0, None, ALU.max, ALU.bypass))
            vq = p.opc(VE, lambda e: e.engine_nop())
            A32.release()
            return vq

        def p_ffn0_A():
            A16.mark(); A32.mark()
            wgi = A16.alloc(32768)
            wui = A16.alloc(32768)
            h2qt = [A16.alloc(1024) for _ in range(2)]
            gt = A16.alloc(4096)
            ut = A16.alloc(4096)
            tt = A16.alloc(4096)
            mcolg = A32.alloc(4)
            mcolu = A32.alloc(4)
            rmc = [A32.alloc(4) for _ in range(2)]
            last_vt = 0
            load_quant_bitnet(wgi, wgT0, 8, 4096 * 1024, mcolg)
            load_quant_bitnet(wui, wuT0, 8, 4096 * 1024, mcolu)
            p.barrier()
            last_tt = {}
            for i in range(2 * NT):
                b, kt = divmod(i, NT)
                pr = i % 2
                vh_ = p.dma(SY, h2qt[pr].full, h2q_sp[b, kt])
                vr = p.dma(SY, rmc[pr][:, 0:1], rmq_sp[b, kt])
                p.wait(VE, S["LQ"], vr)
                p.opc(VE, lambda e, pr=pr: e.tensor_scalar(
                    rmc[pr][:, 1:2], rmc[pr][:, 0:1], mcolg[:, 2:3], None, ALU.mult, ALU.bypass))
                p.opc(VE, lambda e, pr=pr: e.tensor_scalar(
                    rmc[pr][:, 2:3], rmc[pr][:, 0:1], mcolu[:, 2:3], None, ALU.mult, ALU.bypass))
                p.drain(VE)
                vsc = p.opc(VE, lambda e: e.engine_nop())
                if i >= 1:
                    p.wait(SE, S["ve"], last_vt)
                p.wait(PE, S["LQ"], vh_)
                for which, wsrc, dstb, col in ((0, wgi, gt, 1), (1, wui, ut, 2)):
                    for ig in range(8):
                        bank = pb[(which * 8 + ig) % 4]
                        p.wait(PE, S["se"])
                        for r in range(8):
                            vpe = p.opc(PE, lambda e, r=r, ig=ig, pr=pr, wsrc=wsrc, bank=bank: e.matmul(
                                bank[:, 0:512],
                                h2qt[pr][:, 128 * r:128 * (r + 1)],
                                wsrc[:, 4096 * r + 512 * ig:4096 * r + 512 * (ig + 1)],
                                start=(r == 0), stop=(r == 7)))
                        p.wait(SE, S["pe"], vpe)
                        p.wait(SE, S["ve"], vsc)
                        if which == 0:
                            p.opc(SE, lambda e, ig=ig, pr=pr, bank=bank, col=col: e.activation(
                                gt[:, 512 * ig:512 * (ig + 1)], bank[:, 0:512],
                                AF.Silu, scale=rmc[pr][:, col:col + 1]))
                        else:
                            p.opc(SE, lambda e, ig=ig, pr=pr, bank=bank, col=col: e.activation(
                                ut[:, 512 * ig:512 * (ig + 1)], bank[:, 0:512],
                                AF.Identity, scale=rmc[pr][:, col:col + 1]))
                p.drain(SE)
                vse = p.opc(SE, lambda e: e.nop())
                p.wait(VE, S["se"], vse)
                if i >= 1:
                    p.wait(VE, S["SQ"], last_tt[i - 1])
                vt = p.opc(VE, lambda e: e.tensor_tensor(tt.full, gt.full, ut.full, ALU.mult))
                last_vt = vt
                p.wait(SY, S["ve"], vt)
                last_tt[i] = p.store(t0_sp[b, kt], tt.full)
            A16.release(); A32.release()
            p.barrier()

        def p_ffn0_B(rep, repL1, h3fm):
            A16.mark(); A32.mark()
            wdi = A16.alloc(32768)
            ttl = [A16.alloc(4096) for _ in range(2)]
            ti = A16.alloc(4096)
            tif = A16.alloc(4096)
            h3tm = A16.alloc(1024)
            xnb3 = A16.alloc(1024)
            mcold = A32.alloc(4)
            q1l = A32.alloc(1024)
            xt = [A32.alloc(1024) for _ in range(2)]
            xw = [A32.alloc(1024) for _ in range(2)]
            stat = [A32.alloc(24) for _ in range(2)]
            load_quant_bitnet(wdi, wdT0, 32, 4096 * 1024, mcold)
            p.barrier()
            last_se = {}
            stx_last = {}
            for i in range(2 * NT):
                b, kt = divmod(i, NT)
                pr = i % 2
                if i >= 2 and (i - 2) in last_se:
                    p.wait(SY, S["se"], last_se[i - 2])
                if i >= 2:
                    p.wait(VE, S["SQ"], stx_last[i - 2])
                vtl = p.dma(SY, ttl[pr].full, t0_sp[b, kt])
                vx = p.dma(SY, xt[pr].full, xres[b, 128 * kt:128 * (kt + 1), :])
                p.wait(VE, S["LQ"], vtl)
                p.opc(VE, lambda e, pr=pr: e.tensor_reduce(
                    stat[pr][:, 20:21], ttl[pr].full, AX.X, ALU.max, apply_absolute_value=True))
                p.drain(VE)
                p.opc(VE, lambda e, pr=pr: e.tensor_scalar(
                    stat[pr][:, 21:22], stat[pr][:, 20:21], 1e-5, None, ALU.max, ALU.bypass))
                p.drain(VE)
                p.opc(VE, lambda e, pr=pr: e.reciprocal(stat[pr][:, 22:23], stat[pr][:, 21:22]))
                p.drain(VE)
                p.opc(VE, lambda e, pr=pr: e.tensor_scalar(
                    stat[pr][:, 23:24], stat[pr][:, 22:23], 127.0, None, ALU.mult, ALU.bypass))
                p.opc(VE, lambda e, pr=pr: e.tensor_scalar(
                    stat[pr][:, 19:20], stat[pr][:, 21:22], mcold[:, 2:3], None, ALU.mult, ALU.bypass))
                p.drain(VE)
                for hf in range(4):
                    p.opc(VE, lambda e, hf=hf, pr=pr: e.tensor_scalar(
                        q1l.full, ttl[pr][:, 1024 * hf:1024 * (hf + 1)],
                        stat[pr][:, 23:24], MAGIC, ALU.mult, ALU.add))
                    p.opc(VE, lambda e, hf=hf: e.tensor_scalar(
                        q1l.full, q1l.full, -MAGIC, 127.0, ALU.add, ALU.min))
                    p.opc(VE, lambda e, hf=hf: e.tensor_scalar(
                        ti[:, 1024 * hf:1024 * (hf + 1)], q1l.full, -128.0, None, ALU.max, ALU.bypass))
                vq = p.opc(VE, lambda e: e.engine_nop())
                p.wait(PE, S["ve"], vq)
                for grp in range(4):
                    if grp > 0:
                        p.wait(PE, S["ve"])
                    for c8 in range(8):
                        c = 8 * grp + c8
                        vt = p.opc(PE, lambda e, c=c, c8=c8: e.transpose(
                            pbt[:, 128 * c8:128 * (c8 + 1)], ti[:, 128 * c:128 * (c + 1)], ident[:, :]))
                    p.wait(VE, S["pe"], vt)
                    p.opc(VE, lambda e, grp=grp: e.tensor_copy(
                        tif[:, 1024 * grp:1024 * (grp + 1)], pbt[:, 0:1024]))
                vtf = p.opc(VE, lambda e: e.engine_nop())
                p.wait(PE, S["ve"], vtf)
                for og in range(2):
                    bank = pb[4 + og]
                    p.wait(PE, S["ve"])
                    for r in range(32):
                        vpe = p.opc(PE, lambda e, r=r, og=og, bank=bank: e.matmul(
                            bank[:, 0:512],
                            tif[:, 128 * r:128 * (r + 1)],
                            wdi[:, 1024 * r + 512 * og:1024 * r + 512 * (og + 1)],
                            start=(r == 0), stop=(r == 31)))
                p.wait(VE, S["pe"], vpe)
                p.wait(VE, S["LQ"], vx)
                for og in range(2):
                    bank = pb[4 + og]
                    p.opc(VE, lambda e, og=og, pr=pr, bank=bank: e.tensor_scalar(
                        smal[:, 0:512], bank[:, 0:512], stat[pr][:, 19:20], None, ALU.mult, ALU.bypass))
                    p.opc(VE, lambda e, og=og, bank=bank, b=b: e.tensor_tensor(
                        smal[:, 0:512], smal[:, 0:512], rep[(b, "gb")][:, 512 * og:512 * (og + 1)], ALU.mult))
                    p.opc(VE, lambda e, og=og, pr=pr: e.tensor_tensor(
                        xw[pr][:, 512 * og:512 * (og + 1)], smal[:, 0:512],
                        xt[pr][:, 512 * og:512 * (og + 1)], ALU.add))
                vxw = p.opc(VE, lambda e: e.engine_nop())
                p.wait(SY, S["ve"], vxw)
                stx_last[i] = p.store(xres[b, 128 * kt:128 * (kt + 1), :], xw[pr].full)
                vxn, _ = ln_h(xw[pr], stat[pr], h3tm, repL1[(b, "ca1")], repL1[(b, "sa")], xnb3)
                last_se[i] = vxn
                tm_to_fm(h3tm.full, h3fm[b].full, kt)
            A16.release(); A32.release()
            p.barrier()

        def p_l1_attn(repL1, h3fm, gacc):
            A16.mark(); A32.mark()
            xc = A16.alloc(16 * D)
            proj = A16.alloc(16 * 320)
            dtm = A16.alloc(1024)
            atm = A16.alloc(1024)
            btm = A16.alloc(1024)
            ctm = A16.alloc(1024)
            dsm = A16.alloc(2048)
            bsm = A16.alloc(2048)
            csm = A16.alloc(2048)
            abar = A16.alloc(2048)
            bbar = A16.alloc(2048)
            ssc1 = A16.alloc(2048)
            ssc2 = A16.alloc(2048)
            hsm = ssc1
            ysm = ssc2
            stm = A16.alloc(1024)
            h4tm = A16.alloc(1024)
            xnb4 = A16.alloc(1024)
            h4s = A16.alloc(1024)
            cw = A32.alloc(32)
            cvb = A32.alloc(8)
            ipw = A16.alloc(8 * 320)
            ipbrep = A32.alloc(320)
            woutb = A16.alloc(1024)
            opbrep = A16.alloc(1024)
            acol = A32.alloc(8)
            xt = [A32.alloc(1024) for _ in range(2)]
            xw = [A32.alloc(1024) for _ in range(2)]
            stat = [A32.alloc(24) for _ in range(2)]
            v1 = p.dma(SY, cw.full, convw)
            v2 = p.dma(SY, cvb.full.rearrange("p (q c) -> p q c", q=1),
                       convb.rearrange("q (c p) -> p q c", c=8), allow_slow_non_contiguous=True)
            v3 = p.dma(GE, ipw.full.rearrange("p (r o) -> p r o", r=8),
                       inprojT.rearrange("(r p) o -> p r o", r=8))
            v4 = p.dma(GE, woutb[0:64, 0:1024], outprojT)
            v5 = p.dma(SY, smal[0:1, 20:340], inprojb)
            valog = p.dma(SY, smal[0:64, 344:345], alog)
            p.wait(PE, S["LQ"], v5)
            vpe = p.opc(PE, lambda e: e.matmul(pC[:, 0:320], ones1f[0:1, :], smal[0:1, 20:340],
                                               start=True, stop=True))
            p.wait(VE, S["pe"], vpe)
            p.opc(VE, lambda e: e.tensor_copy(ipbrep.full, pC[:, 0:320]))
            vop = p.dma(GE, row0[0:1, 0:1024], outprojb)
            p.wait(PE, S["LG"], vop)
            p.wait(PE, S["ve"])
            bcast_row(row0, D, opbrep.full)
            p.wait(SE, S["LQ"], valog)
            p.drain(SE)
            vex = p.opc(SE, lambda e: e.activation(acol[0:64, 0:1], smal[0:64, 344:345], AF.Exp))
            p.wait(VE, S["se"], vex)
            p.opc(VE, lambda e: e.tensor_scalar(acol[0:64, 0:1], acol[0:64, 0:1], -1.0, None, ALU.mult, ALU.bypass))
            p.drain(VE)
            p.opc(VE, lambda e: e.tensor_scalar(acol[0:64, 3:4], acol[0:64, 0:1], -1.0, None, ALU.mult, ALU.bypass))
            p.drain(VE)
            p.opc(VE, lambda e: e.tensor_scalar(acol[0:64, 3:4], acol[0:64, 3:4], 1e-4, None, ALU.is_lt, ALU.bypass))
            p.drain(VE)
            p.opc(VE, lambda e: e.tensor_scalar(acol[0:64, 4:5], acol[0:64, 3:4], -1.0, 1.0, ALU.mult, ALU.add))
            p.drain(VE)
            p.opc(VE, lambda e: e.tensor_scalar(acol[0:64, 1:2], acol[0:64, 0:1], acol[0:64, 4:5], None, ALU.mult, ALU.bypass))
            p.drain(VE)
            p.opc(VE, lambda e: e.tensor_tensor(acol[0:64, 1:2], acol[0:64, 1:2], acol[0:64, 3:4], ALU.add))
            p.drain(VE)
            p.opc(VE, lambda e: e.reciprocal(acol[0:64, 2:3], acol[0:64, 1:2]))
            p.drain(VE)
            p.barrier()
            for b in range(NB):
                hb = h3fm[b]
                p.wait(VE, S["LQ"], v2)
                for c in range(8):
                    p.opc(VE, lambda e, c=c, hb=hb: e.tensor_scalar(
                        xc[:, 2048 * c: 2048 * c + 2047], hb[:, 2048 * c: 2048 * c + 2047],
                        cw[:, 4 * c + 1:4 * c + 2], cvb[:, c:c + 1], ALU.mult, ALU.add))
                    p.opc(VE, lambda e, c=c, hb=hb: e.scalar_tensor_tensor(
                        xc[:, 2048 * c: 2048 * c + 2047], hb[:, 2048 * c + 1: 2048 * c + 2048],
                        cw[:, 4 * c + 2:4 * c + 3], xc[:, 2048 * c: 2048 * c + 2047], ALU.mult, ALU.add))
                    p.opc(VE, lambda e, c=c, hb=hb: e.scalar_tensor_tensor(
                        xc[:, 2048 * c + 1: 2048 * c + 2047], hb[:, 2048 * c: 2048 * c + 2046],
                        cw[:, 4 * c:4 * c + 1], xc[:, 2048 * c + 1: 2048 * c + 2047], ALU.mult, ALU.add))
                    p.opc(VE, lambda e, c=c, hb=hb: e.scalar_tensor_tensor(
                        xc[:, 2048 * c: 2048 * c + 2046], hb[:, 2048 * c + 2: 2048 * c + 2048],
                        cw[:, 4 * c + 3:4 * c + 4], xc[:, 2048 * c: 2048 * c + 2046], ALU.mult, ALU.add))
                    p.opc(VE, lambda e, c=c: e.memset(xc[:, 2048 * c + 2047: 2048 * c + 2048], 0.0))
                vconv = p.opc(VE, lambda e: e.engine_nop())
                p.wait(PE, S["ve"], vconv)
                p.wait(PE, S["LG"], v3)
                for kt in range(NT):
                    bank = pb[kt % 2]
                    p.wait(PE, S["ve"])
                    for r in range(8):
                        vpe = p.opc(PE, lambda e, r=r, kt=kt, bank=bank: e.matmul(
                            bank[:, 0:320],
                            xc[:, 2048 * r + 128 * kt: 2048 * r + 128 * (kt + 1)],
                            ipw[:, 320 * r: 320 * (r + 1)],
                            start=(r == 0), stop=(r == 7)))
                    p.wait(VE, S["pe"], vpe)
                    p.opc(VE, lambda e, kt=kt, bank=bank: e.tensor_tensor(
                        proj[:, 320 * kt:320 * (kt + 1)], bank[:, 0:320], ipbrep.full, ALU.add))
                vproj = p.opc(VE, lambda e: e.engine_nop())
                prv = proj.full.rearrange("p (k o) -> p k o", k=16)
                p.opc(VE, lambda e: e.tensor_scalar(atm.full, prv[:, :, 0:64], -1.0, None, ALU.mult, ALU.bypass))
                p.drain(VE)
                p.opc(VE, lambda e: e.tensor_tensor(stm.full, atm.full.rearrange("p (k o) -> p k o", k=16), prv[:, :, 0:64], ALU.max))
                p.drain(VE)
                vabs = p.opc(VE, lambda e: e.engine_nop())
                p.wait(SE, S["ve"], vabs)
                p.drain(SE)
                p.opc(SE, lambda e: e.activation(atm.full, stm.full, AF.Exp, scale=-1.0))
                p.drain(SE)
                vln = p.opc(SE, lambda e: e.activation(stm.full, atm.full, AF.Ln, bias=1.0))
                p.wait(VE, S["se"], vln)
                p.opc(VE, lambda e: e.tensor_scalar(atm.full, prv[:, :, 0:64], 0.0, None, ALU.max, ALU.bypass))
                p.drain(VE)
                p.opc(VE, lambda e: e.tensor_tensor(dtm.full, atm.full, stm.full, ALU.add))
                p.opc(VE, lambda e: e.tensor_tensor(
                    btm.full, prv[:, :, 64:192].rearrange("p k (o two) -> p k o two", two=2)[:, :, :, 0],
                    prv[:, :, 64:192].rearrange("p k (o two) -> p k o two", two=2)[:, :, :, 1], ALU.add))
                p.opc(VE, lambda e: e.tensor_scalar(btm.full, btm.full, 0.5, None, ALU.mult, ALU.bypass))
                p.opc(VE, lambda e: e.tensor_tensor(
                    ctm.full, prv[:, :, 192:320].rearrange("p k (o two) -> p k o two", two=2)[:, :, :, 0],
                    prv[:, :, 192:320].rearrange("p k (o two) -> p k o two", two=2)[:, :, :, 1], ALU.add))
                p.opc(VE, lambda e: e.tensor_scalar(ctm.full, ctm.full, 0.5, None, ALU.mult, ALU.bypass))
                p.drain(VE)
                vtmq = p.opc(VE, lambda e: e.engine_nop())
                p.wait(PE, S["ve"], vtmq)
                for qi, (src, dst) in enumerate(((dtm, dsm), (btm, bsm), (ctm, csm))):
                    for kt in range(NT):
                        slot = kt % 8
                        if kt % 8 == 0:
                            p.wait(PE, S["ve"])
                        vt = p.opc(PE, lambda e, kt=kt, slot=slot, src=src: e.transpose(
                            pbt[0:64, 128 * slot:128 * (slot + 1)],
                            src[:, 64 * kt:64 * (kt + 1)], ident[:, :]))
                        if kt % 8 == 7:
                            p.wait(VE, S["pe"], vt)
                            g8 = kt // 8
                            p.opc(VE, lambda e, g8=g8, dst=dst: e.tensor_copy(
                                dst[0:64, 1024 * g8:1024 * (g8 + 1)], pbt[0:64, 0:1024]))
                vsm = p.opc(VE, lambda e: e.engine_nop())
                p.wait(SE, S["ve"], vsm)
                p.drain(SE)
                vab_ = p.opc(SE, lambda e: e.activation(abar[0:64, 0:2048], dsm[0:64, 0:2048],
                                                        AF.Exp, scale=acol[0:64, 0:1]))
                p.wait(VE, S["se"], vab_)
                p.opc(VE, lambda e: e.tensor_scalar(dsm[0:64, 0:2048], dsm[0:64, 0:2048],
                                                    acol[0:64, 0:1], None, ALU.mult, ALU.bypass))
                p.drain(VE)
                p.opc(VE, lambda e: e.tensor_scalar(ssc1[0:64, 0:2048], abar[0:64, 0:2048],
                                                    -1.0, acol[0:64, 2:3], ALU.add, ALU.mult))
                p.opc(VE, lambda e: e.tensor_scalar(ssc2[0:64, 0:2048], dsm[0:64, 0:2048],
                                                    1.0 / 6.0, 0.5, ALU.mult, ALU.add))
                p.drain(VE)
                p.opc(VE, lambda e: e.tensor_tensor(ssc2[0:64, 0:2048], ssc2[0:64, 0:2048],
                                                    dsm[0:64, 0:2048], ALU.mult))
                p.opc(VE, lambda e: e.tensor_scalar(ssc2[0:64, 0:2048], ssc2[0:64, 0:2048],
                                                    1.0, None, ALU.add, ALU.bypass))
                p.drain(VE)
                p.opc(VE, lambda e: e.tensor_scalar(ssc1[0:64, 0:2048], ssc1[0:64, 0:2048],
                                                    acol[0:64, 4:5], None, ALU.mult, ALU.bypass))
                p.drain(VE)
                p.opc(VE, lambda e: e.scalar_tensor_tensor(ssc1[0:64, 0:2048], ssc2[0:64, 0:2048],
                                                           acol[0:64, 3:4], ssc1[0:64, 0:2048], ALU.mult, ALU.add))
                p.opc(VE, lambda e: e.tensor_tensor(bbar[0:64, 0:2048], ssc1[0:64, 0:2048],
                                                    bsm[0:64, 0:2048], ALU.mult))
                p.drain(VE)
                p.opc(VE, lambda e: e.tensor_tensor_scan(hsm[0:64, 0:2048], abar[0:64, 0:2048],
                                                         bbar[0:64, 0:2048], 0.0, ALU.mult, ALU.add))
                p.drain(VE)
                p.opc(VE, lambda e: e.tensor_tensor(ysm[0:64, 0:2048], csm[0:64, 0:2048],
                                                    hsm[0:64, 0:2048], ALU.mult))
                p.drain(VE)
                vy = p.opc(VE, lambda e: e.engine_nop())
                p.wait(PE, S["ve"], vy)
                p.wait(PE, S["LG"], v4)
                last_se = {}
                st_last = {}
                for kt in range(NT):
                    pr = kt % 2
                    if kt >= 2 and (kt - 2) in last_se:
                        p.wait(SY, S["se"], last_se[kt - 2])
                    if kt >= 2:
                        p.wait(VE, S["SQ"], st_last[kt - 2])
                    vx = p.dma(SY, xt[pr].full, xres[b, 128 * kt:128 * (kt + 1), :])
                    for og in range(2):
                        bank = pb[2 + og]
                        p.wait(PE, S["ve"])
                        vpe = p.opc(PE, lambda e, kt=kt, og=og, bank=bank: e.matmul(
                            bank[:, 0:512],
                            ysm[0:64, 128 * kt:128 * (kt + 1)],
                            woutb[0:64, 512 * og:512 * (og + 1)],
                            start=True, stop=True))
                    p.wait(VE, S["pe"], vpe)
                    p.wait(VE, S["LQ"], vx)
                    for og in range(2):
                        bank = pb[2 + og]
                        p.opc(VE, lambda e, og=og, bank=bank: e.tensor_tensor(
                            smal[:, 0:512], bank[:, 0:512], opbrep[:, 512 * og:512 * (og + 1)], ALU.add))
                        p.opc(VE, lambda e, og=og, b=b: e.tensor_tensor(
                            smal[:, 0:512], smal[:, 0:512], repL1[(b, "ga")][:, 512 * og:512 * (og + 1)], ALU.mult))
                        p.opc(VE, lambda e, og=og, pr=pr: e.tensor_tensor(
                            xw[pr][:, 512 * og:512 * (og + 1)], smal[:, 0:512],
                            xt[pr][:, 512 * og:512 * (og + 1)], ALU.add))
                    vxw = p.opc(VE, lambda e: e.engine_nop())
                    p.wait(SY, S["ve"], vxw)
                    p.store(xres[b, 128 * kt:128 * (kt + 1), :], xw[pr].full)
                    vxn, _ = ln_h(xw[pr], stat[pr], h4tm, repL1[(b, "cb1")], repL1[(b, "sb")], xnb4)
                    last_se[kt] = vxn
                    p.opc(VE, lambda e, pr=pr: e.tensor_reduce(
                        stat[pr][:, 20:21], h4tm[:, 0:D], AX.X, ALU.max, apply_absolute_value=True))
                    p.drain(VE)
                    p.opc(VE, lambda e, pr=pr: e.tensor_tensor(
                        gacc[:, 0:1], gacc[:, 0:1], stat[pr][:, 20:21], ALU.max))
                    p.drain(VE)
                    vq = p.opc(VE, lambda e: e.engine_nop())
                    p.wait(PE, S["ve"], vq)
                    for c in range(8):
                        vt = p.opc(PE, lambda e, c=c: e.transpose(
                            pbt[:, 128 * c:128 * (c + 1)], h4tm[:, 128 * c:128 * (c + 1)], ident[:, :]))
                    p.wait(VE, S["pe"], vt)
                    vh4 = p.opc(VE, lambda e: e.tensor_copy(h4s.full, pbt[:, 0:1024]))
                    p.wait(SY, S["ve"], vh4)
                    st_last[kt] = p.store(h4_sp[b, kt], h4s.full)
            A16.release(); A32.release()
            p.barrier()

        def p_gmax_cc(gacc, ccin, ccout, dst_col):
            vv = p.opc(VE, lambda e: e.engine_nop())
            p.wait(GE, S["ve"], vv)
            p.opc(GE, lambda e: e.tensor_reduce(persc[0:1, 0:1], gacc[:, 0:1], AX.C, ALU.max))
            p.opc(GE, lambda e: e.memset(persc[0:1, 4:12], 0.0))
            vg = p.opc(GE, lambda e: e.tensor_copy(persc[0:1, 4:5], persc[0:1, 0:1]))
            vd = p.dma(GE, ccin, persc[0:1, 4:12])
            p.wait(GE, S["LG"], vd)
            vcc = p.op(GE, lambda e: e.collective_compute(
                "AllReduce", ALU.max, replica_groups=[list(range(8))],
                ins=[ccin], outs=[ccout]), S["CC"])
            p.wait(GE, S["CC"], vcc)
            vb = p.dma(GE, persc[0:1, 4:12], ccout)
            p.wait(VE, S["LG"], vb)
            p.opc(VE, lambda e, dst_col=dst_col: e.tensor_copy(persc[0:1, dst_col:dst_col + 1], persc[0:1, 4:5]))
            p.drain(VE)
            p.barrier()

        def rowmax_to_row(w_nat, nrows, width, rowf):
            A16.mark(); A32.mark()
            wt = A16.alloc(width)
            nt_ = nrows // 128
            sqc = A32.alloc(nt_ + 1)
            for r in range(nt_):
                vw = p.dma(GE, wt.full, w_nat[128 * r:128 * (r + 1), :])
                p.wait(VE, S["LG"], vw)
                p.opc(VE, lambda e, r=r: e.tensor_reduce(
                    sqc[:, r:r + 1], wt.full, AX.X, ALU.max, apply_absolute_value=True))
                p.drain(VE)
                vv = p.opc(VE, lambda e: e.engine_nop())
                p.wait(GE, S["ve"], vv)
            vv = p.opc(VE, lambda e: e.engine_nop())
            p.wait(SY, S["ve"], vv)
            vst = p.store(scr2[:, 0:nt_], sqc[:, 0:nt_])
            p.wait(SY, S["SQ"], vst)
            vrb = p.dma(SY, rowf.h[0:1, rowf.off:rowf.off + nrows].rearrange("q (r p) -> q r p", r=nt_),
                        scr2[:, 0:nt_].rearrange("p r -> r p"), allow_slow_non_contiguous=True)
            p.wait(VE, S["LQ"], vrb)
            A16.release(); A32.release()

        def quant_w_rows(wi, nr, width, rrg):
            A32.mark()
            q1s = A32.alloc(512)
            for r in range(nr):
                for j0 in range(0, width, 512):
                    w_ = min(512, width - j0)
                    p.opc(VE, lambda e, r=r, j0=j0, w_=w_, q1s=q1s: e.tensor_tensor(
                        q1s[:, 0:w_], wi[:, width * r + j0:width * r + j0 + w_],
                        rrg[:, j0:j0 + w_], ALU.mult))
                    p.opc(VE, lambda e, w_=w_, q1s=q1s: e.tensor_scalar(
                        q1s[:, 0:w_], q1s[:, 0:w_], MAGIC, -MAGIC, ALU.add, ALU.add))
                    p.opc(VE, lambda e, w_=w_, q1s=q1s: e.tensor_scalar(
                        q1s[:, 0:w_], q1s[:, 0:w_], 127.0, -128.0, ALU.min, ALU.max))
                    p.opc(VE, lambda e, r=r, j0=j0, w_=w_, q1s=q1s: e.tensor_copy(
                        wi[:, width * r + j0:width * r + j0 + w_], q1s[:, 0:w_]))
            return p.opc(VE, lambda e: e.engine_nop())

        IW = 2816
        IGW = [512, 512, 512, 512, 512, 256]

        def p_ffn1_A(gcol):
            A16.mark(); A32.mark()
            wgi1 = A16.alloc(8 * IW)
            wui1 = A16.alloc(8 * IW)
            rr_g = A16.alloc(IW)
            er_g = A16.alloc(IW)
            rr_u = A16.alloc(IW)
            er_u = A16.alloc(IW)
            h4t = [A16.alloc(1024) for _ in range(2)]
            h4q = A16.alloc(1024)
            gt1 = A16.alloc(IW)
            ut1 = A16.alloc(IW)
            tt1 = A16.alloc(IW)
            cols = A32.alloc(8)
            gacc2 = A32.alloc(4)
            A32.mark()
            rowf = A32.alloc(IW)
            rowr = A32.alloc(IW)
            vcx = p.dma(SY, persc[0:1, 16:17], clipx)
            p.wait(SE, S["LQ"], vcx)
            p.drain(SE)
            vsg = p.opc(SE, lambda e: e.activation(persc[0:1, 17:18], persc[0:1, 16:17], AF.Sigmoid))
            p.wait(VE, S["se"], vsg)
            p.opc(VE, lambda e, gcol=gcol: e.tensor_tensor(
                persc[0:1, 18:19], persc[0:1, 17:18], persc[0:1, gcol:gcol + 1], ALU.mult))
            p.drain(VE)
            p.opc(VE, lambda e: e.tensor_scalar(persc[0:1, 19:20], persc[0:1, 18:19],
                                                1.0 / 127.0, None, ALU.mult, ALU.bypass))
            p.drain(VE)
            p.opc(VE, lambda e: e.reciprocal(persc[0:1, 20:21], persc[0:1, 19:20]))
            p.drain(VE)
            vs4 = p.opc(VE, lambda e: e.engine_nop())
            p.wait(PE, S["ve"], vs4)
            vpe = p.opc(PE, lambda e: e.matmul(pC[:, 0:1], ones1f[0:1, :], persc[0:1, 20:21],
                                               start=True, stop=True))
            p.wait(VE, S["pe"], vpe)
            p.opc(VE, lambda e: e.tensor_copy(cols[:, 0:1], pC[:, 0:1]))
            p.drain(VE)
            for (wn, wT, rr_, er_) in ((wg1n, wgT1, rr_g, er_g), (wu1n, wuT1, rr_u, er_u)):
                rowmax_to_row(wn, IW, 1024, rowf)
                p.opc(VE, lambda e: e.reciprocal(rowr[0:1, 0:IW], rowf[0:1, 0:IW]))
                p.drain(VE)
                p.opc(VE, lambda e: e.tensor_scalar(rowr[0:1, 0:IW], rowr[0:1, 0:IW], 127.0, None, ALU.mult, ALU.bypass))
                p.drain(VE)
                vrr = p.opc(VE, lambda e: e.engine_nop())
                p.wait(PE, S["ve"], vrr)
                for g0 in range(0, IW, 512):
                    w_ = min(512, IW - g0)
                    p.wait(PE, S["ve"])
                    vpe = p.opc(PE, lambda e, g0=g0, w_=w_: e.matmul(
                        pC[:, 0:w_], ones1f[0:1, :], rowr.h[0:1, rowr.off + g0:rowr.off + g0 + w_],
                        start=True, stop=True))
                    p.wait(VE, S["pe"], vpe)
                    p.opc(VE, lambda e, g0=g0, w_=w_, rr_=rr_: e.tensor_copy(rr_[:, g0:g0 + w_], pC[:, 0:w_]))
                p.opc(VE, lambda e: e.tensor_scalar(rowr[0:1, 0:IW], rowf[0:1, 0:IW], persc[0:1, 19:20],
                                                    1.0 / 127.0, ALU.mult, ALU.mult))
                p.drain(VE)
                ver = p.opc(VE, lambda e: e.engine_nop())
                p.wait(PE, S["ve"], ver)
                for g0 in range(0, IW, 512):
                    w_ = min(512, IW - g0)
                    p.wait(PE, S["ve"])
                    vpe = p.opc(PE, lambda e, g0=g0, w_=w_: e.matmul(
                        pC[:, 0:w_], ones1f[0:1, :], rowr.h[0:1, rowr.off + g0:rowr.off + g0 + w_],
                        start=True, stop=True))
                    p.wait(VE, S["pe"], vpe)
                    p.opc(VE, lambda e, g0=g0, w_=w_, er_=er_: e.tensor_copy(er_[:, g0:g0 + w_], pC[:, 0:w_]))
            A32.release()
            q1h = A32.alloc(1024)
            vwg = p.dma(GE, wgi1.full.rearrange("p (r o) -> p r o", r=8),
                        wgT1.rearrange("(r p) o -> p r o", r=8))
            p.wait(VE, S["LG"], vwg)
            quant_w_rows(wgi1, 8, IW, rr_g)
            vwu = p.dma(GE, wui1.full.rearrange("p (r o) -> p r o", r=8),
                        wuT1.rearrange("(r p) o -> p r o", r=8))
            p.wait(VE, S["LG"], vwu)
            vqu = quant_w_rows(wui1, 8, IW, rr_u)
            p.opc(VE, lambda e: e.memset(gacc2[:, 0:1], 0.0))
            p.drain(VE)
            if DEBUG and "erg" in dbg:
                p.store(dbg["erg"], er_g[:, 0:IW])
                p.store(dbg["rrg"], rr_g[:, 0:IW])
            p.barrier()
            last_tt = {}
            last_vt = 0
            for i in range(2 * NT):
                b, kt = divmod(i, NT)
                pr = i % 2
                vh_ = p.dma(SY, h4t[pr].full, h4_sp[b, kt])
                p.wait(VE, S["LQ"], vh_)
                if i >= 1:
                    p.wait(VE, S["pe"], last_mm)
                p.opc(VE, lambda e, pr=pr: e.tensor_scalar(
                    q1h.full, h4t[pr].full, cols[:, 0:1], MAGIC, ALU.mult, ALU.add))
                p.opc(VE, lambda e: e.tensor_scalar(q1h.full, q1h.full, -MAGIC, 127.0, ALU.add, ALU.min))
                vq = p.opc(VE, lambda e: e.tensor_scalar(h4q.full, q1h.full, -128.0, None, ALU.max, ALU.bypass))
                if i >= 1:
                    p.wait(SE, S["ve"], last_vt)
                p.wait(PE, S["ve"], vq)
                for which, wsrc, er_ in ((0, wgi1, er_g), (1, wui1, er_u)):
                    j0 = 0
                    for gi, gw in enumerate(IGW):
                        bank = pb[(which * 6 + gi) % 4]
                        p.wait(PE, S["ve"])
                        p.wait(PE, S["se"])
                        for r in range(8):
                            vpe = p.opc(PE, lambda e, r=r, j0=j0, gw=gw, wsrc=wsrc, bank=bank: e.matmul(
                                bank[:, 0:gw],
                                h4q[:, 128 * r:128 * (r + 1)],
                                wsrc[:, IW * r + j0:IW * r + j0 + gw],
                                start=(r == 0), stop=(r == 7)))
                        p.wait(VE, S["pe"], vpe)
                        if which == 0:
                            p.opc(VE, lambda e, j0=j0, gw=gw, bank=bank: e.tensor_tensor(
                                smal[:, 0:gw], bank[:, 0:gw], er_g[:, j0:j0 + gw], ALU.mult))
                            p.drain(VE)
                            vsm = p.opc(VE, lambda e: e.engine_nop())
                            p.wait(SE, S["ve"], vsm)
                            p.opc(SE, lambda e, j0=j0, gw=gw: e.activation(
                                gt1[:, j0:j0 + gw], smal[:, 0:gw], AF.Silu))
                        else:
                            p.opc(VE, lambda e, j0=j0, gw=gw, bank=bank: e.tensor_tensor(
                                ut1[:, j0:j0 + gw], bank[:, 0:gw], er_u[:, j0:j0 + gw], ALU.mult))
                        j0 += gw
                last_mm = vpe
                p.drain(SE)
                vse = p.opc(SE, lambda e: e.nop())
                p.wait(VE, S["se"], vse)
                if i >= 1:
                    p.wait(VE, S["SQ"], last_tt[i - 1])
                vt = p.opc(VE, lambda e: e.tensor_tensor(tt1.full, gt1.full, ut1.full, ALU.mult))
                last_vt = vt
                p.drain(VE)
                p.opc(VE, lambda e: e.tensor_reduce(smal[:, 8:9], tt1.full, AX.X, ALU.max,
                                                    apply_absolute_value=True))
                p.drain(VE)
                p.opc(VE, lambda e: e.tensor_tensor(gacc2[:, 0:1], gacc2[:, 0:1], smal[:, 8:9], ALU.max))
                p.drain(VE)
                vt2 = p.opc(VE, lambda e: e.engine_nop())
                p.wait(SY, S["ve"], vt2)
                last_tt[i] = p.store(t1_sp[b, kt], tt1.full)
                if DEBUG and i == 0 and "tt1" in dbg:
                    p.store(dbg["tt1"], tt1.full)
                    p.store(dbg["h4q0"], h4q.full)
            p.barrier()
            p_gmax_cc(gacc2, cc_in2, cc_out2, 30)
            A16.release(); A32.release()
            p.barrier()

        def p_ffn1_B(repg):
            A16.mark(); A32.mark()
            wdi1 = A16.alloc(22 * 1024)
            rr_d = A16.alloc(1024)
            er_d = A16.alloc(1024)
            ttl = [A16.alloc(IW) for _ in range(2)]
            ti1 = A16.alloc(IW)
            tif1 = A16.alloc(IW)
            q1s2 = A32.alloc(512)
            cols = A32.alloc(8)
            xt = [A32.alloc(1024) for _ in range(2)]
            A32.mark()
            rowf = A32.alloc(1024)
            rowr = A32.alloc(1024)
            vcx = p.dma(SY, persc[0:1, 16:17], clipd)
            p.wait(SE, S["LQ"], vcx)
            p.drain(SE)
            vsg = p.opc(SE, lambda e: e.activation(persc[0:1, 17:18], persc[0:1, 16:17], AF.Sigmoid))
            p.wait(VE, S["se"], vsg)
            p.opc(VE, lambda e: e.tensor_tensor(
                persc[0:1, 18:19], persc[0:1, 17:18], persc[0:1, 30:31], ALU.mult))
            p.drain(VE)
            p.opc(VE, lambda e: e.tensor_scalar(persc[0:1, 19:20], persc[0:1, 18:19],
                                                1.0 / 127.0, None, ALU.mult, ALU.bypass))
            p.drain(VE)
            p.opc(VE, lambda e: e.reciprocal(persc[0:1, 20:21], persc[0:1, 19:20]))
            p.drain(VE)
            vs5 = p.opc(VE, lambda e: e.engine_nop())
            p.wait(PE, S["ve"], vs5)
            vpe = p.opc(PE, lambda e: e.matmul(pC[:, 0:1], ones1f[0:1, :], persc[0:1, 20:21],
                                               start=True, stop=True))
            p.wait(VE, S["pe"], vpe)
            p.opc(VE, lambda e: e.tensor_copy(cols[:, 0:1], pC[:, 0:1]))
            p.drain(VE)
            rowmax_to_row(wd1n, 1024, IW, rowf)
            p.opc(VE, lambda e: e.reciprocal(rowr[0:1, 0:1024], rowf[0:1, 0:1024]))
            p.drain(VE)
            p.opc(VE, lambda e: e.tensor_scalar(rowr[0:1, 0:1024], rowr[0:1, 0:1024], 127.0, None, ALU.mult, ALU.bypass))
            p.drain(VE)
            vrr = p.opc(VE, lambda e: e.engine_nop())
            p.wait(PE, S["ve"], vrr)
            for g0 in range(0, 1024, 512):
                p.wait(PE, S["ve"])
                vpe = p.opc(PE, lambda e, g0=g0: e.matmul(
                    pC[:, 0:512], ones1f[0:1, :], rowr.h[0:1, rowr.off + g0:rowr.off + g0 + 512],
                    start=True, stop=True))
                p.wait(VE, S["pe"], vpe)
                p.opc(VE, lambda e, g0=g0: e.tensor_copy(rr_d[:, g0:g0 + 512], pC[:, 0:512]))
            p.opc(VE, lambda e: e.tensor_scalar(rowr[0:1, 0:1024], rowf[0:1, 0:1024], persc[0:1, 19:20],
                                                1.0 / 127.0, ALU.mult, ALU.mult))
            p.drain(VE)
            ver = p.opc(VE, lambda e: e.engine_nop())
            p.wait(PE, S["ve"], ver)
            for g0 in range(0, 1024, 512):
                p.wait(PE, S["ve"])
                vpe = p.opc(PE, lambda e, g0=g0: e.matmul(
                    pC[:, 0:512], ones1f[0:1, :], rowr.h[0:1, rowr.off + g0:rowr.off + g0 + 512],
                    start=True, stop=True))
                p.wait(VE, S["pe"], vpe)
                p.opc(VE, lambda e, g0=g0: e.tensor_copy(er_d[:, g0:g0 + 512], pC[:, 0:512]))
            A32.release()
            vwd = p.dma(GE, wdi1.full.rearrange("p (r o) -> p r o", r=22),
                        wdT1.rearrange("(r p) o -> p r o", r=22))
            p.wait(VE, S["LG"], vwd)
            quant_w_rows(wdi1, 22, 1024, rr_d)
            p.barrier()
            last_se = {}
            sty_last = {}
            nloop = 0 if STOP_AFTER == "ffnBsetup" else (2 if STOP_AFTER == "ffnB2" else 2 * NT)
            for i in range(nloop):
                b, kt = divmod(i, NT)
                pr = i % 2
                if i >= 2 and (i - 2) in last_se:
                    p.wait(SY, S["ve"], last_se[i - 2])
                    p.wait(SY, S["SQ"], sty_last[i - 2])
                vtl = p.dma(SY, ttl[pr].full, t1_sp[b, kt])
                vx = p.dma(SY, xt[pr].full, xres[b, 128 * kt:128 * (kt + 1), :])
                p.wait(VE, S["LQ"], vtl)
                for j0 in range(0, IW, 512):
                    w_ = min(512, IW - j0)
                    p.opc(VE, lambda e, j0=j0, w_=w_, pr=pr: e.tensor_scalar(
                        q1s2[:, 0:w_], ttl[pr][:, j0:j0 + w_], cols[:, 0:1], MAGIC, ALU.mult, ALU.add))
                    p.opc(VE, lambda e, w_=w_: e.tensor_scalar(
                        q1s2[:, 0:w_], q1s2[:, 0:w_], -MAGIC, 127.0, ALU.add, ALU.min))
                    p.opc(VE, lambda e, j0=j0, w_=w_: e.tensor_scalar(
                        ti1[:, j0:j0 + w_], q1s2[:, 0:w_], -128.0, None, ALU.max, ALU.bypass))
                vq = p.opc(VE, lambda e: e.engine_nop())
                p.wait(PE, S["ve"], vq)
                for grp in range(3):
                    n8 = 8 if grp < 2 else 6
                    if grp > 0:
                        p.wait(PE, S["ve"])
                    for c8 in range(n8):
                        c = 8 * grp + c8
                        vt = p.opc(PE, lambda e, c=c, c8=c8: e.transpose(
                            pbt[:, 128 * c8:128 * (c8 + 1)], ti1[:, 128 * c:128 * (c + 1)], ident[:, :]))
                    p.wait(VE, S["pe"], vt)
                    p.opc(VE, lambda e, grp=grp, n8=n8: e.tensor_copy(
                        tif1[:, 1024 * grp:1024 * grp + 128 * n8], pbt[:, 0:128 * n8]))
                vtf = p.opc(VE, lambda e: e.engine_nop())
                p.wait(PE, S["ve"], vtf)
                for og in range(2):
                    bank = pb[4 + og]
                    p.wait(PE, S["ve"])
                    for r in range(22):
                        vpe = p.opc(PE, lambda e, r=r, og=og, bank=bank: e.matmul(
                            bank[:, 0:512],
                            tif1[:, 128 * r:128 * (r + 1)],
                            wdi1[:, 1024 * r + 512 * og:1024 * r + 512 * (og + 1)],
                            start=(r == 0), stop=(r == 21)))
                p.wait(VE, S["pe"], vpe)
                p.wait(VE, S["LQ"], vx)
                for og in range(2):
                    bank = pb[4 + og]
                    p.opc(VE, lambda e, og=og, bank=bank: e.tensor_tensor(
                        smal[:, 0:512], bank[:, 0:512], er_d[:, 512 * og:512 * (og + 1)], ALU.mult))
                    p.opc(VE, lambda e, og=og, b=b: e.tensor_tensor(
                        smal[:, 0:512], smal[:, 0:512], repg[(b, "gb")][:, 512 * og:512 * (og + 1)], ALU.mult))
                    p.opc(VE, lambda e, og=og: e.tensor_copy(q1s2[:, 0:512], smal[:, 0:512]))
                    vfin = p.opc(VE, lambda e, og=og, pr=pr: e.tensor_tensor(
                        xt[pr][:, 512 * og:512 * (og + 1)], q1s2[:, 0:512],
                        xt[pr][:, 512 * og:512 * (og + 1)], ALU.add))
                last_se[i] = vfin
                p.wait(SY, S["ve"], vfin)
                sty = p.store(y_out[b, 128 * kt:128 * (kt + 1), :], xt[pr].full)
                sty_last[i] = sty
            A16.release(); A32.release()
            p.barrier()

        def p_h1(b, rep, h1fm, h1tm, xnb):
            A32.mark()
            xt = [A32.alloc(1024) for _ in range(2)]
            stat = [A32.alloc(24) for _ in range(2)]
            last_se = {}
            for k in range(NT):
                if k >= 2 and (k - 2) in last_se:
                    p.wait(SY, S["se"], last_se[k - 2])
                vld = p.dma(SY, xt[k % 2].full, x_in[b, 128 * k:128 * (k + 1), :])
                p.wait(VE, S["LQ"], vld)
                vxn, _ = ln_h(xt[k % 2], stat[k % 2], h1tm, rep[(b, "ca1")], rep[(b, "sa")], xnb)
                last_se[k] = vxn
                tm_to_fm(h1tm.full, h1fm, k)
            A32.release()
            p.barrier()

        cst = {"negl": A32.alloc(4), "w1m": A16.alloc(64)}
        p_init()
        p_attn_consts(cst)
        p_mod(0, c0T, wmod0)
        p_mod(1, c1T, wmod1)
        for b in range(NB):
            A16.mark(); A32.mark()
            qfm = A16.alloc(8 * T); kfm = A16.alloc(8 * T)
            vab = A16.alloc(16 * 1040); qn = A16.alloc(8 * T)
            negkb = A32.alloc(32)
            A16.mark(); A32.mark()
            h1fm = A16.alloc(8 * T)
            A16.mark()
            rep1 = {(b, "ca1"): A16.alloc(D), (b, "sa"): A16.alloc(D)}
            build_rep(0, rep1)
            h1tm = A16.alloc(D); xnb = A16.alloc(D)
            p_h1(b, rep1, h1fm.full, h1tm, xnb)
            A16.release()
            p_qkv(b, h1fm.full, qfm, kfm, vab, qn, negkb)
            A16.release(); A32.release()
            p_core(b, qfm, kfm, vab, qn, negkb, cst)
            A16.release(); A32.release()
            A16.mark(); A32.mark()
            rep2 = {(b, "ga"): A16.alloc(D), (b, "cb1"): A16.alloc(D), (b, "sb"): A16.alloc(D)}
            build_rep(0, rep2)
            p_attn_out(b, rep2)
            A16.release(); A32.release()

        if DEBUG and "x1" in dbg:
            for b in range(NB):
                p.store(dbg["x1"][b], xres[b])
            p.barrier()

        p_ffn0_A()
        A16.mark()
        h3fm = [A16.alloc(8 * T) for _ in range(NB)]
        A16.mark(); A32.mark()
        repB = {}
        repL1 = {}
        for b in range(NB):
            repB[(b, "gb")] = A16.alloc(D)
            repL1[(b, "ca1")] = A16.alloc(D)
            repL1[(b, "sa")] = A16.alloc(D)
        build_rep(0, repB)
        build_rep(1, repL1)
        if DEBUG and "repgb" in dbg:
            p.barrier()
            for b in range(NB):
                p.store(dbg["repgb"][b], repB[(b, "gb")].full)
            for b in range(NB):
                p.store(dbg["repca1"][b], repL1[(b, "ca1")].full)
        p_ffn0_B(repB, repL1, h3fm)
        A16.release(); A32.release()

        if DEBUG and "t0" in dbg:
            for b in range(NB):
                p.store(dbg["t0"][b], t0_sp[b, 0])
        if DEBUG and "rmq" in dbg:
            for b in range(NB):
                for kt in range(NT):
                    p.store(dbg["rmq"][b, kt], rmq_sp[b, kt])
        if DEBUG and "h2q" in dbg:
            for b in range(NB):
                for kt in range(2):
                    p.store(dbg["h2q"][b, kt], h2q_sp[b, kt])
        if DEBUG and "x2" in dbg:
            for b in range(NB):
                p.store(dbg["x2"][b], xres[b])
        if DEBUG and "h3fm" in dbg:
            for c in range(8):
                p.store(dbg["h3fm"][c], h3fm[0][:, T * c:T * (c + 1)])
        if DEBUG and ("x2" in dbg or "h3fm" in dbg):
            p.barrier()

        gacc1 = A32.alloc(4)
        p.opc(VE, lambda e: e.memset(gacc1[:, 0:1], 0.0))
        p.drain(VE)
        A16.mark(); A32.mark()
        repL1b = {}
        for b in range(NB):
            repL1b[(b, "ga")] = A16.alloc(D)
            repL1b[(b, "cb1")] = A16.alloc(D)
            repL1b[(b, "sb")] = A16.alloc(D)
        build_rep(1, repL1b)
        p_l1_attn(repL1b, h3fm, gacc1)
        A16.release(); A32.release()
        A16.release()

        if DEBUG and "x3" in dbg:
            for b in range(NB):
                p.store(dbg["x3"][b], xres[b])
            p.barrier()

        if STOP_AFTER not in ("x3",):
            p_gmax_cc(gacc1, cc_in1, cc_out1, 21)
        if STOP_AFTER not in ("x3", "cc1"):
            p_ffn1_A(21)
        if STOP_AFTER not in ("x3", "cc1", "ffnA"):
            A16.mark(); A32.mark()
            repg1 = {}
            for b in range(NB):
                repg1[(b, "gb")] = A16.alloc(D)
            build_rep(1, repg1)
            p_ffn1_B(repg1)
            A16.release(); A32.release()
        if DEBUG and "attnsp" in dbg:
            for b in range(NB):
                p.store(dbg["attnsp"][b], attn_sp[b])
        if DEBUG and "scol" in dbg:
            p.store(dbg["scol"], persc[0:1, 0:32])
        p.barrier()
        p.wait(SY, S["SQ"])
        p.wait(SY, S["LQ"])

        with nc.Block() as block:
            @block.tensor
            def _(e):
                for f in p.streams[PE]: f(e)

            @block.vector
            def _(e):
                for f in p.streams[VE]: f(e)

            @block.scalar
            def _(e):
                for f in p.streams[SE]: f(e)

            @block.gpsimd
            def _(e):
                for f in p.streams[GE]: f(e)

            @block.sync
            def _(e):
                for f in p.streams[SY]: f(e)

    return nc


def prep_inputs(I):
    C = np.ascontiguousarray
    shared = {
        "wqT": C(I["L0_attn_Wq"].T), "wkT": C(I["L0_attn_Wk"].T),
        "wvT": C(I["L0_attn_Wv"].T), "woT": C(I["L0_attn_Wo"].T),
        "lamq1": C(I["L0_attn_lambda_q1"].reshape(32, 1)),
        "lamk1": C(I["L0_attn_lambda_k1"].reshape(32, 1)),
        "lamq2": C(I["L0_attn_lambda_q2"].reshape(32, 1)),
        "lamk2": C(I["L0_attn_lambda_k2"].reshape(32, 1)),
        "sublnw": C(I["L0_attn_subln_w"].reshape(1, 64)),
        "wgT0": C(I["L0_ffn_w_gate"].T), "wuT0": C(I["L0_ffn_w_up"].T),
        "wdT0": C(I["L0_ffn_w_down"].T),
        "wmod0": C(I["L0_wmod"]),
        "convw": C(I["L1_attn_conv_w"][:, 0, :].reshape(8, 128, 4).transpose(1, 0, 2).reshape(128, 32)),
        "convb": C(I["L1_attn_conv_b"].reshape(1, D)),
        "inprojT": C(I["L1_attn_in_proj_w"].T),
        "inprojb": C(I["L1_attn_in_proj_b"].reshape(1, 320)),
        "alog": C(I["L1_attn_A_log"].reshape(64, 1)),
        "outprojT": C(I["L1_attn_out_proj_w"].T),
        "outprojb": C(I["L1_attn_out_proj_b"].reshape(1, D)),
        "clipx": C(I["L1_ffn_clip_x"].reshape(1, 1)),
        "clipd": C(I["L1_ffn_clip_down"].reshape(1, 1)),
        "wgT1": C(I["L1_ffn_w_gate"].T), "wuT1": C(I["L1_ffn_w_up"].T),
        "wdT1": C(I["L1_ffn_w_down"].T),
        "wg1n": C(I["L1_ffn_w_gate"]), "wu1n": C(I["L1_ffn_w_up"]),
        "wd1n": C(I["L1_ffn_w_down"]),
        "wmod1": C(I["L1_wmod"]),
    }
    c0T = C(I["L0_c"].T)
    c1T = C(I["L1_c"].T)
    per_core = []
    for cid in range(8):
        m = dict(shared)
        m["x_in"] = C(I["L0_attn_x"][2 * cid:2 * cid + 2])
        m["c0T"] = C(c0T[:, 2 * cid:2 * cid + 2])
        m["c1T"] = C(c1T[:, 2 * cid:2 * cid + 2])
        per_core.append(m)
    return per_core


def run(I, debug_names=()):
    in_maps = prep_inputs(I)
    nc = build_program(debug_names=debug_names)
    return run_bass_kernel_spmd(nc, in_maps, core_ids=list(range(8)))

def kernel(**inputs):
    in_maps = prep_inputs(inputs)
    nc = build_program()
    res = run_bass_kernel_spmd(nc, in_maps, core_ids=list(range(8)))
    return np.concatenate([np.asarray(r["y"]) for r in res.results], axis=0)


# version 03
# speedup vs baseline: 1.0820x; 1.0779x over previous
import math
import contextlib
import numpy as np
import concourse.bass as bass
import concourse.mybir as mybir
from concourse.bass_utils import run_bass_kernel_spmd

F32 = mybir.dt.float32
BF16 = mybir.dt.bfloat16
AF = mybir.ActivationFunctionType
ALU = mybir.AluOpType
AX = mybir.AxisListType

ENGINES = ("tensor", "vector", "scalar", "gpsimd", "sync")
PE, VE, SE, GE, SY = ENGINES
NB = 2
T = 2048
D = 1024
NT = T // 128
LAMBDA_INIT = 0.8 - 0.6 * math.exp(-0.3 * 12)
MAGIC = float(np.float32(1.5 * 2 ** 23))

DEBUG = False
STOP_AFTER = None


class Sem:
    def __init__(self, name):
        self.name = name
        self.h = None
        self.n = 0


class Buf:
    def __init__(self, h, off, cols, dt):
        self.h = h; self.off = off; self.cols = cols; self.dt = dt

    def __getitem__(self, idx):
        ps, fs = idx
        a = fs.start or 0
        b = fs.stop if fs.stop is not None else self.cols
        assert fs.step is None and 0 <= a <= b <= self.cols, (a, b, self.cols)
        return self.h[ps, self.off + a: self.off + b]

    @property
    def full(self):
        return self.h[:, self.off: self.off + self.cols]

    def sub(self, a, b):
        return Buf(self.h, self.off + a, b - a, self.dt)


class Arena:
    def __init__(self, h, n, dt, name):
        self.h = h; self.n = n; self.dt = dt; self.off = 0; self.name = name
        self.marks = []; self.hi = 0

    def alloc(self, cols, align=4):
        o = (self.off + align - 1) // align * align
        assert o + cols <= self.n, f"arena {self.name} overflow: {o}+{cols} > {self.n}"
        self.off = o + cols
        self.hi = max(self.hi, self.off)
        return Buf(self.h, o, cols, self.dt)

    def mark(self):
        self.marks.append(self.off)

    def release(self):
        self.off = self.marks.pop()


class Prog:
    def __init__(self, nc):
        self.nc = nc
        self.streams = {e: [] for e in ENGINES}
        self.sems = {}
        for s in ("pe", "ve", "se", "ge", "sy", "LQ", "LG", "SQ", "CC"):
            self.sems[s] = Sem(s)
        self._emap = {PE: "pe", VE: "ve", SE: "se", GE: "ge", SY: "sy"}

    def em(self, eng, fn):
        self.streams[eng].append(fn)

    def esem(self, eng):
        return self.sems[self._emap[eng]]

    def op(self, eng, fn, sem=None):
        if sem is not None:
            sem.n += 1
            v = sem.n
            self.em(eng, lambda e, fn=fn, s=sem: fn(e).then_inc(s.h, 1))
            return v
        self.em(eng, fn)
        return None

    def opc(self, eng, fn):
        return self.op(eng, fn, self.esem(eng))

    def wait(self, eng, sem, val=None):
        v = sem.n if val is None else val
        self.em(eng, lambda e, s=sem, v=v: e.wait_ge(s.h, v))

    def drain(self, eng):
        self.em(eng, lambda e: e.drain())

    def dma(self, eng, out, in_, sem=None, **kw):
        sem = sem or (self.sems["LG"] if eng == GE else self.sems["LQ"])
        sem.n += 16
        v = sem.n
        self.em(eng, lambda e, o=out, i=in_, s=sem, kw=kw: e.dma_start(out=o, in_=i, **kw).then_inc(s.h, 16))
        return v

    def store(self, out, in_, eng=SY):
        return self.dma(eng, out, in_, sem=self.sems["SQ"])

    def barrier(self):
        self.wait(SY, self.sems["LQ"])
        self.wait(SY, self.sems["SQ"])
        self.wait(GE, self.sems["LG"])
        vals = {}
        for eng in ENGINES:
            s = self.esem(eng)
            s.n += 1
            vals[eng] = s.n
            if eng in (GE, VE):
                self.em(eng, lambda e, s=s: e.engine_nop().then_inc(s.h, 1))
            elif eng in (PE, SE):
                self.drain(eng)
                self.em(eng, lambda e, s=s: e.nop().then_inc(s.h, 1))
            else:
                self.em(eng, lambda e, s=s: e.nop().then_inc(s.h, 1))
        for eng in ENGINES:
            for oth in ENGINES:
                if oth != eng:
                    self.wait(eng, self.esem(oth), vals[oth])


def build_program(debug_names=()):
    nc = bass.Bass("TRN2", target_bir_lowering=False)
    p = Prog(nc)
    S = p.sems

    dr = {}
    def din(name, shape, dt=F32):
        dr[name] = nc.dram_tensor(name, list(shape), dt, kind="ExternalInput").ap()
        return dr[name]
    def dscr(name, shape, dt=F32):
        dr[name] = nc.dram_tensor(name, list(shape), dt).ap()
        return dr[name]

    x_in = din("x_in", (NB, T, D))
    wqT = din("wqT", (D, D)); wkT = din("wkT", (D, D)); wvT = din("wvT", (D, D)); woT = din("woT", (D, D))
    lamq1 = din("lamq1", (32, 1)); lamk1 = din("lamk1", (32, 1))
    lamq2 = din("lamq2", (32, 1)); lamk2 = din("lamk2", (32, 1))
    sublnw = din("sublnw", (1, 64))
    wgT0 = din("wgT0", (D, 4096)); wuT0 = din("wuT0", (D, 4096)); wdT0 = din("wdT0", (4096, D))
    c0T = din("c0T", (D, NB)); wmod0 = din("wmod0", (D, 6 * D))
    convw = din("convw", (128, 32)); convb = din("convb", (1, D))
    inprojT = din("inprojT", (D, 320)); inprojb = din("inprojb", (1, 320))
    alog = din("alog", (64, 1))
    outprojT = din("outprojT", (64, D)); outprojb = din("outprojb", (1, D))
    clipx = din("clipx", (1, 1)); clipd = din("clipd", (1, 1))
    wgT1 = din("wgT1", (D, 2816)); wuT1 = din("wuT1", (D, 2816)); wdT1 = din("wdT1", (2816, D))
    wg1n = din("wg1n", (2816, D)); wu1n = din("wu1n", (2816, D)); wd1n = din("wd1n", (D, 2816))
    c1T = din("c1T", (D, NB)); wmod1 = din("wmod1", (D, 6 * D))

    y_out = nc.dram_tensor("y", [NB, T, D], F32, kind="ExternalOutput").ap()

    xres = dscr("xres", (NB, T, D))
    modscr = dscr("modscr", (2, NB, 6 * D), BF16)
    h2q_sp = dscr("h2q_sp", (NB, NT, 128, 1024), BF16)
    h4_sp = dscr("h4_sp", (NB, NT, 128, 1024), BF16)
    t0_sp = dscr("t0_sp", (NB, NT, 128, 4096), BF16)
    t1_sp = dscr("t1_sp", (NB, NT, 128, 2816), BF16)
    cc_in1 = dscr("cc_in1", (1, 8)); cc_out1 = dscr("cc_out1", (1, 8))
    cc_in2 = dscr("cc_in2", (1, 8)); cc_out2 = dscr("cc_out2", (1, 8))
    negscr = dscr("negscr", (4, 8))
    scr2 = dscr("scr2", (128, 32))
    rmq_sp = dscr("rmq_sp", (NB, NT, 128, 1))
    attn_sp = dscr("attn_sp", (NB, T, D), BF16)

    dbg = {}
    if DEBUG:
        for name, shape, dt in debug_names:
            dbg[name] = nc.dram_tensor("dbg_" + name, list(shape), dt, kind="ExternalOutput").ap()

    es = contextlib.ExitStack()
    sb = lambda n, s, d: es.enter_context(nc.sbuf_tensor(n, s, d))
    psm = lambda n, s, d: es.enter_context(nc.psum_tensor(n, s, d))

    with es:
        A16 = Arena(sb("arena16", [128, 91300], BF16), 91300, BF16, "A16")
        A32 = Arena(sb("arena32", [128, 6200], F32), 6200, F32, "A32")
        ident = sb("ident", [128, 128], BF16)
        ones1 = sb("ones1", [1, 128], BF16)
        ones1f = sb("ones1f", [1, 128], F32)
        ones4 = sb("ones4", [128, 4], BF16)
        io4 = sb("io4", [128, 4], F32)
        row0 = sb("row0", [1, 1024], BF16)
        smal = sb("smal", [128, 512], F32)
        persc = sb("persc", [128, 40], F32)
        pA = psm("pA", [128, 1536], F32)
        pB = psm("pB", [128, 1536], F32)
        pC = psm("pC", [128, 512], F32)
        pbt = psm("pbt", [128, 1024], BF16)
        pb = [Buf(pA, 512 * i, 512, F32) for i in range(3)] + \
             [Buf(pB, 512 * i, 512, F32) for i in range(3)] + [Buf(pC, 0, 512, F32)]

        for name, s in S.items():
            s.h = es.enter_context(nc.semaphore(name))

        def p_init():
            p.op(GE, lambda e: e.memset(ident.ap(), 0.0))
            p.opc(GE, lambda e: e.affine_select(
                out=ident[:, :], in_=ident[:, :], compare_op=ALU.not_equal,
                fill=1.0, base=0, pattern=[[-1, 128]], channel_multiplier=1))
            p.opc(GE, lambda e: e.memset(ones1.ap(), 1.0))
            p.opc(GE, lambda e: e.memset(ones1f.ap(), 1.0))
            p.barrier()

        def bcast_row(src_row, n, dst, pbank=6, f32=False):
            one = ones1f if f32 else ones1
            for g0 in range(0, n, 512):
                w = min(512, n - g0)
                p.wait(PE, S["ve"])
                vpe = p.opc(PE, lambda e, g0=g0, w=w, one=one: e.matmul(
                    pb[pbank][:, 0:w], one[0:1, :], src_row[0:1, g0:g0 + w],
                    start=True, stop=True))
                p.wait(VE, S["pe"], vpe)
                p.opc(VE, lambda e, g0=g0, w=w: e.tensor_copy(dst[:, g0:g0 + w], pb[pbank][:, 0:w]))

        def p_mod(L, cT, wmod):
            A16.mark(); A32.mark()
            cs = A32.alloc(16)
            csf = A32.alloc(16)
            mrow = A32.alloc(1024)
            mrow2 = A32.alloc(1024)
            wm = A32.alloc(8 * 512)
            modsb = A16.alloc(6 * D)
            v = p.dma(SY, cs.full.rearrange("p (c t) -> p c t", c=8),
                      cT.rearrange("(c p) t -> p c t", c=8),
                      allow_slow_non_contiguous=True)
            p.wait(SE, S["LQ"], v)
            vsil = p.opc(SE, lambda e: e.activation(csf.full, cs.full, AF.Silu))
            p.wait(PE, S["se"], vsil)
            evac_v = {}
            for g in range(12):
                if g >= 1:
                    p.wait(SY, S["pe"])
                vld = p.dma(SY, wm.full, wmod[:, 512 * g:512 * (g + 1)].rearrange("(c p) m -> p c m", c=8))
                p.wait(PE, S["LQ"], vld)
                if g >= 2:
                    p.wait(PE, S["ve"], evac_v[g - 2])
                for c in range(8):
                    vpe = p.opc(PE, lambda e, c=c, g=g: e.matmul(
                        pb[g % 2][0:2, :], csf[:, 2 * c:2 * c + 2],
                        wm.full.rearrange("p (c m) -> p c m", c=8)[:, c, :],
                        start=(c == 0), stop=(c == 7)))
                p.wait(VE, S["pe"], vpe)
                evac_v[g] = p.opc(VE, lambda e, g=g: e.tensor_copy(
                    modsb[0:2, 512 * g:512 * (g + 1)], pb[g % 2][0:2, :]))
            for sl in (1, 4):
                p.opc(VE, lambda e, sl=sl: e.tensor_scalar(
                    modsb[0:2, sl * D:(sl + 1) * D], modsb[0:2, sl * D:(sl + 1) * D],
                    1.0, None, ALU.add, ALU.bypass))
            for sl in (2, 5):
                p.opc(VE, lambda e, sl=sl: e.tensor_scalar(
                    mrow[0:2, 0:D], modsb[0:2, sl * D:(sl + 1) * D], 0.0, None, ALU.is_ge, ALU.bypass))
                p.drain(VE)
                p.opc(VE, lambda e: e.tensor_scalar(
                    mrow2[0:2, 0:D], mrow[0:2, 0:D], 2.0, -1.0, ALU.mult, ALU.add))
                p.opc(VE, lambda e, sl=sl: e.tensor_scalar(
                    mrow[0:2, 0:D], modsb[0:2, sl * D:(sl + 1) * D], -1.0, None, ALU.mult, ALU.bypass))
                p.drain(VE)
                p.opc(VE, lambda e, sl=sl: e.tensor_tensor(
                    mrow[0:2, 0:D], mrow[0:2, 0:D], modsb[0:2, sl * D:(sl + 1) * D], ALU.max))
                p.drain(VE)
                p.opc(VE, lambda e: e.tensor_scalar(
                    mrow[0:2, 0:D], mrow[0:2, 0:D], 0.25, None, ALU.max, ALU.bypass))
                p.drain(VE)
                p.opc(VE, lambda e, sl=sl: e.tensor_tensor(
                    modsb[0:2, sl * D:(sl + 1) * D], mrow2[0:2, 0:D], mrow[0:2, 0:D], ALU.mult))
                p.drain(VE)
            vrows = p.opc(VE, lambda e: e.engine_nop())
            p.wait(SY, S["ve"], vrows)
            p.store(modscr[L], modsb[0:2, :])
            if DEBUG and L == 0 and "modrow" in dbg:
                p.store(dbg["modrow"], modsb[0:2, :])
            if DEBUG and L == 0 and "csb" in dbg:
                p.store(dbg["csb"], csf.full)
            A16.release(); A32.release()
            p.barrier()

        SLOT_OF = {"sa": 0, "ca1": 1, "ga": 2, "sb": 3, "cb1": 4, "gb": 5}

        def build_rep(L, rep):
            for (b, kind), dst in rep.items():
                sl = SLOT_OF[kind]
                p.wait(SY, S["pe"])
                vr = p.dma(SY, row0[0:1, 0:D], modscr[L, b, sl * D:(sl + 1) * D][None, :])
                p.wait(PE, S["LQ"], vr)
                bcast_row(row0, D, dst.full)
            p.barrier()

        def ln_h(xt, stat, htm, ca1, sa_, xnb):
            p.opc(VE, lambda e: e.bn_stats(stat[:, 0:6], xt[:, 0:512]))
            p.opc(VE, lambda e: e.bn_stats(stat[:, 6:12], xt[:, 512:1024]))
            p.drain(VE)
            p.opc(VE, lambda e: e.bn_aggr(stat[:, 12:14], stat[:, 0:12]))
            p.drain(VE)
            p.opc(VE, lambda e: e.tensor_scalar(stat[:, 14:15], stat[:, 13:14], 1e-5, None, ALU.add, ALU.bypass))
            p.drain(VE)
            vv = p.opc(VE, lambda e: e.reciprocal(stat[:, 15:16], stat[:, 14:15]))
            p.wait(SE, S["ve"], vv)
            p.drain(SE)
            vs = p.opc(SE, lambda e: e.activation(stat[:, 16:17], stat[:, 15:16], AF.Sqrt))
            p.wait(VE, S["se"], vs)
            p.opc(VE, lambda e: e.tensor_scalar(stat[:, 17:18], stat[:, 12:13], stat[:, 16:17], -1.0, ALU.mult, ALU.mult))
            p.drain(VE)
            vv2 = p.opc(VE, lambda e: e.engine_nop())
            p.wait(SE, S["ve"], vv2)
            p.drain(SE)
            vxn = p.opc(SE, lambda e: e.activation(xnb.full, xt[:, 0:D], AF.Identity,
                                                   bias=stat[:, 17:18], scale=stat[:, 16:17]))
            p.wait(VE, S["se"], vxn)
            p.opc(VE, lambda e: e.tensor_tensor(htm[:, 0:D], xnb.full, ca1.full, ALU.mult))
            vtt = p.opc(VE, lambda e: e.tensor_tensor(htm[:, 0:D], htm[:, 0:D], sa_.full, ALU.add))
            return vxn, vtt

        def tm_to_fm(src, dst_fm, k, nchunk=8, dst_stride=T):
            vv = p.opc(VE, lambda e: e.engine_nop())
            p.wait(PE, S["ve"], vv)
            for c in range(nchunk):
                p.opc(PE, lambda e, c=c: e.transpose(
                    pbt[:, 128 * c:128 * (c + 1)], src[:, 128 * c:128 * (c + 1)], ident[:, :]))
            p.wait(VE, S["pe"])
            for c in range(nchunk):
                p.opc(VE, lambda e, c=c: e.tensor_copy(
                    dst_fm[:, dst_stride * c + 128 * k: dst_stride * c + 128 * (k + 1)],
                    pbt[:, 128 * c:128 * (c + 1)]))

        def p_attn_consts(cst):
            p.opc(GE, lambda e: e.iota(io4.ap(), pattern=[[-32, 4]], base=0,
                                       channel_multiplier=1, allow_small_or_imprecise_dtypes=True))
            p.wait(VE, S["ge"])
            p.opc(VE, lambda e: e.tensor_scalar(smal[:, 0:4], io4[:, :], 0.0, None, ALU.is_ge, ALU.bypass))
            p.drain(VE)
            p.opc(VE, lambda e: e.tensor_scalar(smal[:, 4:8], io4[:, :], 31.5, None, ALU.is_le, ALU.bypass))
            p.drain(VE)
            p.opc(VE, lambda e: e.tensor_tensor(ones4[:, :], smal[:, 0:4], smal[:, 4:8], ALU.mult))
            p.drain(VE)
            v = p.dma(SY, smal[0:32, 8:9], lamq1)
            v = p.dma(SY, smal[0:32, 9:10], lamk1)
            v = p.dma(SY, smal[0:32, 10:11], lamq2)
            v = p.dma(SY, smal[0:32, 11:12], lamk2)
            sw = p.dma(SY, smal[0:1, 16:80], sublnw)
            p.wait(VE, S["LQ"], v)
            p.opc(VE, lambda e: e.tensor_tensor(smal[0:32, 12:13], smal[0:32, 8:9], smal[0:32, 9:10], ALU.mult))
            p.opc(VE, lambda e: e.tensor_tensor(smal[0:32, 13:14], smal[0:32, 10:11], smal[0:32, 11:12], ALU.mult))
            p.drain(VE)
            vv = p.opc(VE, lambda e: e.engine_nop())
            p.wait(GE, S["ve"], vv)
            p.opc(GE, lambda e: e.tensor_reduce(smal[0:1, 14:15], smal[0:32, 12:13], AX.C, ALU.add))
            vg = p.opc(GE, lambda e: e.tensor_reduce(smal[0:1, 15:16], smal[0:32, 13:14], AX.C, ALU.add))
            p.wait(SE, S["ge"], vg)
            p.drain(SE)
            p.opc(SE, lambda e: e.activation(smal[0:1, 14:15], smal[0:1, 14:15], AF.Exp))
            p.drain(SE)
            vs = p.opc(SE, lambda e: e.activation(smal[0:1, 15:16], smal[0:1, 15:16], AF.Exp))
            p.wait(VE, S["se"], vs)
            p.opc(VE, lambda e: e.tensor_tensor(smal[0:1, 14:15], smal[0:1, 14:15], smal[0:1, 15:16], ALU.subtract))
            p.drain(VE)
            p.opc(VE, lambda e: e.tensor_scalar(smal[0:1, 14:15], smal[0:1, 14:15],
                                                LAMBDA_INIT, -1.0, ALU.add, ALU.mult))
            p.drain(VE)
            p.wait(VE, S["LQ"], sw)
            p.opc(VE, lambda e: e.tensor_scalar(smal[0:1, 16:80], smal[0:1, 16:80],
                                                1.0 - LAMBDA_INIT, None, ALU.mult, ALU.bypass))
            p.drain(VE)
            vv2 = p.opc(VE, lambda e: e.engine_nop())
            p.wait(PE, S["ve"], vv2)
            vpe = p.opc(PE, lambda e: e.matmul(pC[:, 0:1], ones1f[0:1, :], smal[0:1, 14:15],
                                               start=True, stop=True))
            vpe = p.opc(PE, lambda e: e.matmul(pC[:, 64:128], ones1f[0:1, :], smal[0:1, 16:80],
                                               start=True, stop=True))
            p.wait(VE, S["pe"], vpe)
            p.opc(VE, lambda e: e.tensor_copy(cst["negl"][:, 0:1], pC[:, 0:1]))
            p.opc(VE, lambda e: e.tensor_copy(cst["w1m"][:, 0:64], pC[:, 64:128]))
            p.barrier()

        def p_qkv(b, h1fm, qfm, kfm, vab, qn, negkb):
            A16.mark(); A32.mark()
            wbuf = A16.alloc(8192)
            q2b = A16.alloc(512)
            kmx = A32.alloc(40)
            DINV = float(1.0 / math.sqrt(32.0))

            def proj_fm(dst, w_dram, scale, qn_dst):
                vw = p.dma(GE, wbuf.full.rearrange("p (r o) -> p r o", r=8),
                           w_dram.rearrange("(r p) o -> p r o", r=8))
                evac = {}
                sq_ev = {}
                for i, (c, g) in enumerate([(c, g) for c in range(8) for g in range(4)]):
                    bank = pb[i % 2]
                    p.wait(PE, S["LG"], vw)
                    if i >= 2:
                        p.wait(PE, S["se"], evac[i - 2])
                    for r in range(8):
                        vpe = p.opc(PE, lambda e, r=r, c=c, g=g, bank=bank: e.matmul(
                            bank[:, 0:512],
                            wbuf[:, 1024 * r + 128 * c: 1024 * r + 128 * (c + 1)],
                            h1fm[:, 2048 * r + 512 * g: 2048 * r + 512 * (g + 1)],
                            start=(r == 0), stop=(r == 7)))
                    p.wait(SE, S["pe"], vpe)
                    evac[i] = p.opc(SE, lambda e, c=c, g=g, bank=bank: e.activation(
                        dst[:, 2048 * c + 512 * g: 2048 * c + 512 * (g + 1)],
                        bank[:, 0:512], AF.Identity, scale=scale))
                    if qn_dst is not None:
                        sq_ev[i] = p.opc(SE, lambda e, c=c, g=g: e.activation(
                            q2b[:, 0:512],
                            dst[:, 2048 * c + 512 * g: 2048 * c + 512 * (g + 1)], AF.Square))
                        p.wait(PE, S["se"], sq_ev[i])
                        if qn_dst == "kmx":
                            p.wait(PE, S["ve"])
                        elif i >= 1:
                            p.wait(PE, S["se"])
                        vn = p.opc(PE, lambda e: e.matmul(
                            pC[0:4, 0:512], ones4[:, :], q2b[:, 0:512], start=True, stop=True))
                        if qn_dst == "kmx":
                            p.wait(VE, S["pe"], vn)
                            p.opc(VE, lambda e, i=i: e.tensor_reduce(
                                kmx[0:4, i:i + 1], pC[0:4, 0:512], AX.X, ALU.max))
                        else:
                            p.wait(SE, S["pe"], vn)
                            p.opc(SE, lambda e, c=c, g=g: e.activation(
                                qn_dst[0:4, 2048 * c + 512 * g: 2048 * c + 512 * (g + 1)],
                                pC[0:4, 0:512], AF.Sqrt))
                p.barrier()

            proj_fm(qfm.full, wqT, DINV, qn)
            proj_fm(kfm.full, wkT, 1.0, "kmx")
            p.drain(VE)
            p.opc(VE, lambda e: e.tensor_reduce(
                kmx[0:4, 32:40],
                kmx[0:4, 0:32].rearrange("p (c g) -> p c g", c=8),
                AX.X, ALU.max))
            p.drain(VE)
            vv = p.opc(VE, lambda e: e.engine_nop())
            p.wait(SE, S["ve"], vv)
            p.drain(SE)
            vs = p.opc(SE, lambda e: e.activation(kmx[0:4, 32:40], kmx[0:4, 32:40], AF.Sqrt))
            p.wait(VE, S["se"], vs)
            p.opc(VE, lambda e: e.tensor_scalar(kmx[0:4, 32:40], kmx[0:4, 32:40], -1.0, None, ALU.mult, ALU.bypass))
            p.drain(VE)
            vneg = p.opc(VE, lambda e: e.engine_nop())
            p.wait(SY, S["ve"], vneg)
            vst = p.store(negscr, kmx[0:4, 32:40])
            p.wait(SY, S["SQ"], vst)
            vrow = p.dma(SY, smal[0:1, 96:128], negscr.rearrange("r c -> c r"),
                         allow_slow_non_contiguous=True)
            p.wait(PE, S["LQ"], vrow)
            vpe = p.opc(PE, lambda e: e.matmul(pC[:, 128:160], ones1f[0:1, :], smal[0:1, 96:128],
                                               start=True, stop=True))
            p.wait(VE, S["pe"], vpe)
            p.opc(VE, lambda e: e.tensor_copy(negkb[:, 0:32], pC[:, 128:160]))
            vw = p.dma(GE, wbuf.full.rearrange("p (r o) -> p r o", r=8),
                       wvT.rearrange("(r p) o -> p r o", r=8))
            p.opc(VE, lambda e: e.memset(
                vab.full.rearrange("p (k h o) -> p k h o", k=16, h=16)[:, :, :, 64:65], 1.0))
            evac = {}
            for i, (kt, dg) in enumerate([(kt, dg) for kt in range(16) for dg in range(2)]):
                bank = pb[2 + i % 2]
                p.wait(PE, S["LG"], vw)
                if i >= 2:
                    p.wait(PE, S["ve"], evac[i - 2])
                for r in range(8):
                    vpe = p.opc(PE, lambda e, r=r, kt=kt, dg=dg, bank=bank: e.matmul(
                        bank[:, 0:512],
                        h1fm[:, 2048 * r + 128 * kt: 2048 * r + 128 * (kt + 1)],
                        wbuf[:, 1024 * r + 512 * dg: 1024 * r + 512 * (dg + 1)],
                        start=(r == 0), stop=(r == 7)))
                p.wait(VE, S["pe"], vpe)
                evac[i] = p.opc(VE, lambda e, kt=kt, dg=dg, bank=bank: e.tensor_copy(
                    vab.full.rearrange("p (k h o) -> p k h o", k=16, h=16)[:, kt, 8 * dg:8 * (dg + 1), 0:64],
                    bank[:, 0:512].rearrange("p (h o) -> p h o", h=8)))
            if DEBUG and b == 0:
                p.barrier()
                if "qfm" in dbg: p.store(dbg["qfm"], qfm.full)
                if "kfm" in dbg: p.store(dbg["kfm"], kfm.full)
                if "vab" in dbg: p.store(dbg["vab"], vab.full)
                if "qn" in dbg: p.store(dbg["qn"][0:4, :], qn[0:4, 0:8 * T])
                if "negkb" in dbg: p.store(dbg["negkb"], negkb.full)
            A16.release(); A32.release()
            p.barrier()

        def p_core(b, qfm, kfm, vab, qn, negkb, cst):
            A16.mark(); A32.mark()
            E2 = [A16.alloc(8192) for _ in range(2)]
            kpair = A16.alloc(2048)
            qpair = A16.alloc(2048)
            stage = A16.alloc(4096)
            dump = A16.alloc(64)
            o0sb = A32.alloc(280)
            comb = A32.alloc(200)
            vcols = A32.alloc(256)
            p.opc(VE, lambda e: e.memset(kpair[32:33, 0:2048], 1.0))
            p.opc(VE, lambda e: e.memset(kpair[96:97, 0:2048], 1.0))
            last_pair_use = 0
            last_pC_use = 0
            exp_slot = {0: 0, 1: 0}
            scount = 0
            st_done = 0
            for qg in range(4):
                for vh in range(16):
                    h2a, h2b = 2 * vh, 2 * vh + 1
                    ca_, ra = h2a // 4, h2a % 4
                    cb_, rb = h2b // 4, h2b % 4
                    p.wait(GE, S["pe"], last_pair_use)
                    vp = p.dma(GE, kpair[0:32, 0:2048], kfm.h[32 * ra:32 * ra + 32, kfm.off + 2048 * ca_: kfm.off + 2048 * (ca_ + 1)])
                    vp = p.dma(GE, kpair[64:96, 0:2048], kfm.h[32 * rb:32 * rb + 32, kfm.off + 2048 * cb_: kfm.off + 2048 * (cb_ + 1)])
                    vp = p.dma(GE, qpair[0:32, 0:2048], qfm.h[32 * ra:32 * ra + 32, qfm.off + 2048 * ca_: qfm.off + 2048 * (ca_ + 1)])
                    vp = p.dma(GE, qpair[64:96, 0:2048], qfm.h[32 * rb:32 * rb + 32, qfm.off + 2048 * cb_: qfm.off + 2048 * (cb_ + 1)])
                    vp = p.dma(GE, qpair[32:33, 0:2048], qn.h[ra:ra + 1, qn.off + 2048 * ca_: qn.off + 2048 * (ca_ + 1)])
                    vp = p.dma(GE, qpair[96:97, 0:2048], qn.h[rb:rb + 1, qn.off + 2048 * cb_: qn.off + 2048 * (cb_ + 1)])
                    p.wait(VE, S["LG"], vp)
                    p.opc(VE, lambda e, h2a=h2a: e.tensor_scalar(
                        qpair[32:33, 0:2048], qpair[32:33, 0:2048],
                        negkb[32:33, h2a:h2a + 1], None, ALU.mult, ALU.bypass))
                    p.opc(VE, lambda e, h2b=h2b: e.tensor_scalar(
                        qpair[96:97, 0:2048], qpair[96:97, 0:2048],
                        negkb[96:97, h2b:h2b + 1], None, ALU.mult, ALU.bypass))
                    p.drain(VE)
                    vqp = p.opc(VE, lambda e: e.engine_nop())
                    for sh in (0, 1):
                        E = E2[sh]
                        jbase = 0
                        for si, sw_ in enumerate((3, 3, 3, 3, 3, 1)):
                            ps = (pA, pB)[scount % 2]
                            p.wait(PE, S["se"], exp_slot[scount % 2])
                            p.wait(PE, S["ve"], vqp)
                            p.wait(PE, S["LG"], vp)
                            for jj in range(sw_):
                                j = jbase + jj
                                vpe = p.opc(PE, lambda e, jj=jj, j=j, sh=sh, ps=ps, qg=qg: e.matmul(
                                    ps[:, 512 * jj:512 * (jj + 1)],
                                    kpair[64 * sh:64 * sh + 33, 128 * j:128 * (j + 1)],
                                    qpair[64 * sh:64 * sh + 33, 512 * qg:512 * (qg + 1)],
                                    start=True, stop=True))
                            last_pair_use = vpe
                            p.wait(SE, S["pe"], vpe)
                            exp_slot[scount % 2] = p.opc(SE, lambda e, ps=ps, jbase=jbase, sw_=sw_, E=E: e.activation(
                                E[:, 512 * jbase:512 * (jbase + sw_)], ps[:, 0:512 * sw_], AF.Exp))
                            jbase += sw_
                            scount += 1
                    for sh in (0, 1):
                        E = E2[sh]
                        p.wait(PE, S["se"], exp_slot[0])
                        p.wait(PE, S["se"], exp_slot[1])
                        p.wait(PE, S["ve"], last_pC_use)
                        for qs in range(4):
                            for j in range(16):
                                vpe = p.opc(PE, lambda e, j=j, qs=qs, vh=vh, E=E: e.matmul(
                                    pC[:, 128 * qs:128 * qs + 65],
                                    E[:, 512 * j + 128 * qs:512 * j + 128 * (qs + 1)],
                                    vab[:, 1040 * j + 65 * vh:1040 * j + 65 * (vh + 1)],
                                    start=(j == 0), stop=(j == 15), skip_group_check=True))
                        p.wait(VE, S["pe"], vpe)
                        if sh == 0:
                            for qs in range(4):
                                p.opc(VE, lambda e, qs=qs: e.tensor_copy(
                                    o0sb[:, 70 * qs:70 * qs + 65], pC[:, 128 * qs:128 * qs + 65]))
                            vo0 = p.opc(VE, lambda e: e.engine_nop())
                            last_pC_use = vo0
                        else:
                            for qs in range(4):
                                p.opc(VE, lambda e, qs=qs: e.reciprocal(
                                    comb[:, 128:129], o0sb[:, 70 * qs + 64:70 * qs + 65]))
                                p.opc(VE, lambda e, qs=qs: e.reciprocal(
                                    comb[:, 129:130], pC[:, 128 * qs + 64:128 * qs + 65]))
                                p.drain(VE)
                                p.opc(VE, lambda e, qs=qs: e.tensor_scalar(
                                    comb[:, 0:64], o0sb[:, 70 * qs:70 * qs + 64],
                                    comb[:, 128:129], None, ALU.mult, ALU.bypass))
                                p.opc(VE, lambda e, qs=qs: e.tensor_scalar(
                                    comb[:, 64:128], pC[:, 128 * qs:128 * qs + 64],
                                    comb[:, 129:130], cst["negl"][:, 0:1], ALU.mult, ALU.mult))
                                p.drain(VE)
                                vA = p.opc(VE, lambda e, qs=qs, vh=vh: e.tensor_tensor(
                                    stage[:, 1024 * qs + 64 * vh:1024 * qs + 64 * (vh + 1)],
                                    comb[:, 0:64], comb[:, 64:128], ALU.add))
                                p.wait(SE, S["ve"], vA)
                                p.opc(SE, lambda e, qs=qs, vh=vh: e.activation(
                                    dump[:, 0:64],
                                    stage[:, 1024 * qs + 64 * vh:1024 * qs + 64 * (vh + 1)],
                                    AF.Square, accum_out=vcols[:, 16 * qs + vh:16 * qs + vh + 1]))
                            vcomb = p.opc(VE, lambda e: e.engine_nop())
                            last_pC_use = vcomb
                p.wait(VE, S["se"])
                p.opc(VE, lambda e: e.tensor_scalar(
                    vcols[:, 64:128], vcols[:, 0:64], 1.0 / 64.0, 1e-5, ALU.mult, ALU.add))
                p.drain(VE)
                vu = p.opc(VE, lambda e: e.engine_nop())
                p.wait(SE, S["ve"], vu)
                p.drain(SE)
                vsq = p.opc(SE, lambda e: e.activation(
                    vcols[:, 192:256], vcols[:, 64:128], AF.Sqrt))
                p.wait(VE, S["se"], vsq)
                vrc = p.opc(VE, lambda e: e.reciprocal(vcols[:, 128:192], vcols[:, 192:256]))
                p.drain(VE)
                for qs in range(4):
                    for vh in range(16):
                        idx = 16 * qs + vh
                        p.opc(VE, lambda e, qs=qs, vh=vh, idx=idx: e.scalar_tensor_tensor(
                            stage[:, 1024 * qs + 64 * vh:1024 * qs + 64 * (vh + 1)],
                            stage[:, 1024 * qs + 64 * vh:1024 * qs + 64 * (vh + 1)],
                            vcols[:, 128 + idx:128 + idx + 1],
                            cst["w1m"][:, 0:64], ALU.mult, ALU.mult))
                vnrm = p.opc(VE, lambda e: e.engine_nop())
                p.wait(SY, S["ve"], vnrm)
                for qs in range(4):
                    st_done = p.store(attn_sp[b, 512 * qg + 128 * qs:512 * qg + 128 * (qs + 1), :],
                                      stage[:, 1024 * qs:1024 * (qs + 1)])
                p.wait(VE, S["SQ"], st_done)
            A16.release(); A32.release()
            p.barrier()

        def p_attn_out(b, rep):
            A16.mark(); A32.mark()
            wo = A16.alloc(8192)
            afm = [A16.alloc(1024) for _ in range(2)]
            h2tm = A16.alloc(1024)
            h2i = A16.alloc(1024)
            hqs = A16.alloc(1024)
            xnb2 = A16.alloc(1024)
            xt = [A32.alloc(1024) for _ in range(2)]
            xw = [A32.alloc(1024) for _ in range(2)]
            stat = [A32.alloc(24) for _ in range(2)]
            q1f = A32.alloc(1024)
            vw = p.dma(GE, wo.full.rearrange("p (r o) -> p r o", r=8),
                       woT.rearrange("(r p) o -> p r o", r=8))
            last_se = {}
            last_tr = {}
            st_last = {}
            ao_ev = {}
            for kt in range(NT):
                pr = kt % 2
                if kt >= 1:
                    p.wait(VE, S["SQ"], st_last[kt - 1])
                if kt >= 2:
                    p.wait(SY, S["pe"], last_tr[kt - 2])
                va_ = None
                for c in range(8):
                    va_ = p.dma(SY, afm[pr][:, 128 * c:128 * (c + 1)],
                                attn_sp[b, 128 * kt:128 * (kt + 1), 128 * c:128 * (c + 1)],
                                transpose=True)
                if kt >= 2 and (kt - 2) in last_se:
                    p.wait(SY, S["se"], last_se[kt - 2])
                vx = p.dma(SY, xt[pr].full, x_in[b, 128 * kt:128 * (kt + 1), :])
                if DEBUG and b == 0 and kt == 0 and "afm0" in dbg:
                    p.wait(SY, S["LQ"], va_)
                    p.store(dbg["afm0"], afm[pr].full)
                p.wait(PE, S["LQ"], va_)
                p.wait(PE, S["LG"], vw)
                for dg in range(2):
                    bank = pb[dg]
                    if ao_ev.get(dg):
                        p.wait(PE, S["ve"], ao_ev[dg])
                    for r in range(8):
                        vpe = p.opc(PE, lambda e, r=r, dg=dg, pr=pr, bank=bank: e.matmul(
                            bank[:, 0:512],
                            afm[pr][:, 128 * r:128 * (r + 1)],
                            wo[:, 1024 * r + 512 * dg:1024 * r + 512 * (dg + 1)],
                            start=(r == 0), stop=(r == 7)))
                last_tr[kt] = vpe
                p.wait(VE, S["pe"], last_tr[kt])
                p.wait(VE, S["LQ"], vx)
                if DEBUG and b == 0 and kt == 0 and "op0" in dbg:
                    p.opc(VE, lambda e: e.tensor_copy(q1f[:, 0:512], pb[0][:, 0:512]))
                    p.opc(VE, lambda e: e.tensor_copy(q1f[:, 512:1024], pb[1][:, 0:512]))
                    vtap = p.opc(VE, lambda e: e.engine_nop())
                    p.wait(SY, S["ve"], vtap)
                    p.store(dbg["op0"], q1f.full)
                for dg in range(2):
                    bank = pb[dg]
                    ao_ev[dg] = p.opc(VE, lambda e, dg=dg, pr=pr, bank=bank: e.tensor_tensor(
                        smal[:, 0:512], bank[:, 0:512],
                        rep[(b, "ga")][:, 512 * dg:512 * (dg + 1)], ALU.mult))
                    p.opc(VE, lambda e, dg=dg, pr=pr: e.tensor_tensor(
                        xw[pr][:, 512 * dg:512 * (dg + 1)], smal[:, 0:512],
                        xt[pr][:, 512 * dg:512 * (dg + 1)], ALU.add))
                vxw = p.opc(VE, lambda e: e.engine_nop())
                p.wait(SY, S["ve"], vxw)
                p.store(xres[b, 128 * kt:128 * (kt + 1), :], xw[pr].full)
                if DEBUG and b == 0 and kt == 0 and "rep_ga" in dbg:
                    p.store(dbg["rep_ga"], rep[(b, "ga")].full)
                vxn, _ = ln_h(xw[pr], stat[pr], h2tm, rep[(b, "cb1")], rep[(b, "sb")], xnb2)
                last_se[kt] = vxn
                p.opc(VE, lambda e, pr=pr: e.tensor_reduce(stat[pr][:, 20:21], h2tm[:, 0:D], AX.X, ALU.max,
                                                    apply_absolute_value=True))
                p.drain(VE)
                p.opc(VE, lambda e, pr=pr: e.tensor_scalar(stat[pr][:, 21:22], stat[pr][:, 20:21],
                                                    1e-5, None, ALU.max, ALU.bypass))
                p.drain(VE)
                p.opc(VE, lambda e, pr=pr: e.reciprocal(stat[pr][:, 22:23], stat[pr][:, 21:22]))
                p.drain(VE)
                p.opc(VE, lambda e, pr=pr: e.tensor_scalar(stat[pr][:, 23:24], stat[pr][:, 22:23],
                                                    127.0, None, ALU.mult, ALU.bypass))
                p.drain(VE)
                p.opc(VE, lambda e, pr=pr: e.tensor_scalar(q1f.full, h2tm[:, 0:D],
                                                    stat[pr][:, 23:24], MAGIC, ALU.mult, ALU.add))
                p.opc(VE, lambda e: e.tensor_scalar(q1f.full, q1f.full, -MAGIC, 127.0, ALU.add, ALU.min))
                vq = p.opc(VE, lambda e: e.tensor_scalar(h2i.full, q1f.full, -128.0, None, ALU.max, ALU.bypass))
                p.wait(PE, S["ve"], vq)
                for c in range(8):
                    vt = p.opc(PE, lambda e, c=c: e.transpose(
                        pbt[:, 128 * c:128 * (c + 1)], h2i[:, 128 * c:128 * (c + 1)], ident[:, :]))
                p.wait(VE, S["pe"], vt)
                vh2 = p.opc(VE, lambda e: e.tensor_copy(hqs.full, pbt[:, 0:1024]))
                p.wait(SY, S["ve"], vh2)
                p.store(h2q_sp[b, kt], hqs.full)
                st_last[kt] = p.store(rmq_sp[b, kt], stat[pr][:, 21:22])
            A16.release(); A32.release()
            p.barrier()

        def load_quant_bitnet(wdst, w_dram, nr, nelem, mcol):
            ncols = nelem // 128
            A32.mark()
            q1s = A32.alloc(512)
            vw = p.dma(GE, wdst.full.rearrange("p (r o) -> p r o", r=nr),
                       w_dram.rearrange("(r p) o -> p r o", r=nr))
            p.wait(VE, S["LG"], vw)
            p.opc(VE, lambda e: e.memset(mcol[:, 0:1], 0.0))
            p.drain(VE)
            for s0 in range(0, ncols, 512):
                w_ = min(512, ncols - s0)
                p.opc(VE, lambda e, s0=s0, w_=w_: e.tensor_reduce(
                    smal[:, 8:9], wdst[:, s0:s0 + w_], AX.X, ALU.add, apply_absolute_value=True))
                p.drain(VE)
                p.opc(VE, lambda e: e.tensor_tensor(mcol[:, 0:1], mcol[:, 0:1], smal[:, 8:9], ALU.add))
                p.drain(VE)
            vv = p.opc(VE, lambda e: e.engine_nop())
            p.wait(GE, S["ve"], vv)
            vg = p.opc(GE, lambda e: e.tensor_reduce(smal[0:1, 12:13], mcol[:, 0:1], AX.C, ALU.add))
            p.wait(VE, S["ge"], vg)
            p.opc(VE, lambda e: e.tensor_scalar(smal[0:1, 13:14], smal[0:1, 12:13],
                                                1.0 / nelem, 1e-5, ALU.mult, ALU.max))
            p.drain(VE)
            p.opc(VE, lambda e: e.reciprocal(smal[0:1, 14:15], smal[0:1, 13:14]))
            p.drain(VE)
            p.opc(VE, lambda e: e.tensor_scalar(smal[0:1, 15:16], smal[0:1, 13:14],
                                                1.0 / 127.0, None, ALU.mult, ALU.bypass))
            p.drain(VE)
            vv2 = p.opc(VE, lambda e: e.engine_nop())
            p.wait(PE, S["ve"], vv2)
            vpe = p.opc(PE, lambda e: e.matmul(pC[:, 0:2], ones1f[0:1, :], smal[0:1, 14:16],
                                               start=True, stop=True))
            p.wait(VE, S["pe"], vpe)
            p.opc(VE, lambda e: e.tensor_copy(mcol[:, 1:3], pC[:, 0:2]))
            p.drain(VE)
            for s0 in range(0, ncols, 512):
                w_ = min(512, ncols - s0)
                p.opc(VE, lambda e, s0=s0, w_=w_, q1s=q1s: e.tensor_scalar(
                    q1s[:, 0:w_], wdst[:, s0:s0 + w_],
                    mcol[:, 1:2], MAGIC, ALU.mult, ALU.add))
                p.opc(VE, lambda e, s0=s0, w_=w_, q1s=q1s: e.tensor_scalar(
                    q1s[:, 0:w_], q1s[:, 0:w_], -MAGIC, 1.0, ALU.add, ALU.min))
                p.opc(VE, lambda e, s0=s0, w_=w_, q1s=q1s: e.tensor_scalar(
                    wdst[:, s0:s0 + w_], q1s[:, 0:w_], -1.0, None, ALU.max, ALU.bypass))
            vq = p.opc(VE, lambda e: e.engine_nop())
            A32.release()
            return vq

        def p_ffn0_A():
            A16.mark(); A32.mark()
            wgi = A16.alloc(32768)
            wui = A16.alloc(32768)
            h2qt = [A16.alloc(1024) for _ in range(2)]
            gt = A16.alloc(4096)
            ut = A16.alloc(4096)
            tt = A16.alloc(4096)
            mcolg = A32.alloc(4)
            mcolu = A32.alloc(4)
            rmc = [A32.alloc(4) for _ in range(2)]
            last_vt = 0
            load_quant_bitnet(wgi, wgT0, 8, 4096 * 1024, mcolg)
            load_quant_bitnet(wui, wuT0, 8, 4096 * 1024, mcolu)
            p.barrier()
            last_tt = {}
            bank_ev = {}
            for i in range(2 * NT):
                b, kt = divmod(i, NT)
                pr = i % 2
                vh_ = p.dma(SY, h2qt[pr].full, h2q_sp[b, kt])
                vr = p.dma(SY, rmc[pr][:, 0:1], rmq_sp[b, kt])
                p.wait(VE, S["LQ"], vr)
                p.opc(VE, lambda e, pr=pr: e.tensor_scalar(
                    rmc[pr][:, 1:2], rmc[pr][:, 0:1], mcolg[:, 2:3], None, ALU.mult, ALU.bypass))
                p.opc(VE, lambda e, pr=pr: e.tensor_scalar(
                    rmc[pr][:, 2:3], rmc[pr][:, 0:1], mcolu[:, 2:3], None, ALU.mult, ALU.bypass))
                p.drain(VE)
                vsc = p.opc(VE, lambda e: e.engine_nop())
                if i >= 1:
                    p.wait(SE, S["ve"], last_vt)
                p.wait(PE, S["LQ"], vh_)
                for which, wsrc, dstb, col in ((0, wgi, gt, 1), (1, wui, ut, 2)):
                    for ig in range(8):
                        gidx = which * 8 + ig
                        bank = pb[gidx % 4]
                        if bank_ev.get(gidx % 4):
                            p.wait(PE, S["se"], bank_ev[gidx % 4])
                        for r in range(8):
                            vpe = p.opc(PE, lambda e, r=r, ig=ig, pr=pr, wsrc=wsrc, bank=bank: e.matmul(
                                bank[:, 0:512],
                                h2qt[pr][:, 128 * r:128 * (r + 1)],
                                wsrc[:, 4096 * r + 512 * ig:4096 * r + 512 * (ig + 1)],
                                start=(r == 0), stop=(r == 7)))
                        p.wait(SE, S["pe"], vpe)
                        p.wait(SE, S["ve"], vsc)
                        if which == 0:
                            bank_ev[gidx % 4] = p.opc(SE, lambda e, ig=ig, pr=pr, bank=bank, col=col: e.activation(
                                gt[:, 512 * ig:512 * (ig + 1)], bank[:, 0:512],
                                AF.Silu, scale=rmc[pr][:, col:col + 1]))
                        else:
                            bank_ev[gidx % 4] = p.opc(SE, lambda e, ig=ig, pr=pr, bank=bank, col=col: e.activation(
                                ut[:, 512 * ig:512 * (ig + 1)], bank[:, 0:512],
                                AF.Identity, scale=rmc[pr][:, col:col + 1]))
                p.drain(SE)
                vse = p.opc(SE, lambda e: e.nop())
                p.wait(VE, S["se"], vse)
                if i >= 1:
                    p.wait(VE, S["SQ"], last_tt[i - 1])
                vt = p.opc(VE, lambda e: e.tensor_tensor(tt.full, gt.full, ut.full, ALU.mult))
                last_vt = vt
                p.wait(SY, S["ve"], vt)
                last_tt[i] = p.store(t0_sp[b, kt], tt.full)
            A16.release(); A32.release()
            p.barrier()

        def p_ffn0_B(rep, repL1, h3fm):
            A16.mark(); A32.mark()
            wdi = A16.alloc(32768)
            ttl = [A16.alloc(4096) for _ in range(2)]
            ti = A16.alloc(4096)
            tif = A16.alloc(4096)
            h3tm = A16.alloc(1024)
            xnb3 = A16.alloc(1024)
            mcold = A32.alloc(4)
            q1l = A32.alloc(1024)
            xt = [A32.alloc(1024) for _ in range(2)]
            xw = [A32.alloc(1024) for _ in range(2)]
            stat = [A32.alloc(24) for _ in range(2)]
            load_quant_bitnet(wdi, wdT0, 32, 4096 * 1024, mcold)
            p.barrier()
            last_se = {}
            stx_last = {}
            dn_ev = {}
            for i in range(2 * NT):
                b, kt = divmod(i, NT)
                pr = i % 2
                if i >= 2 and (i - 2) in last_se:
                    p.wait(SY, S["se"], last_se[i - 2])
                if i >= 2:
                    p.wait(VE, S["SQ"], stx_last[i - 2])
                vtl = p.dma(SY, ttl[pr].full, t0_sp[b, kt])
                vx = p.dma(SY, xt[pr].full, xres[b, 128 * kt:128 * (kt + 1), :])
                p.wait(VE, S["LQ"], vtl)
                p.opc(VE, lambda e, pr=pr: e.tensor_reduce(
                    stat[pr][:, 20:21], ttl[pr].full, AX.X, ALU.max, apply_absolute_value=True))
                p.drain(VE)
                p.opc(VE, lambda e, pr=pr: e.tensor_scalar(
                    stat[pr][:, 21:22], stat[pr][:, 20:21], 1e-5, None, ALU.max, ALU.bypass))
                p.drain(VE)
                p.opc(VE, lambda e, pr=pr: e.reciprocal(stat[pr][:, 22:23], stat[pr][:, 21:22]))
                p.drain(VE)
                p.opc(VE, lambda e, pr=pr: e.tensor_scalar(
                    stat[pr][:, 23:24], stat[pr][:, 22:23], 127.0, None, ALU.mult, ALU.bypass))
                p.opc(VE, lambda e, pr=pr: e.tensor_scalar(
                    stat[pr][:, 19:20], stat[pr][:, 21:22], mcold[:, 2:3], None, ALU.mult, ALU.bypass))
                p.drain(VE)
                for hf in range(4):
                    p.opc(VE, lambda e, hf=hf, pr=pr: e.tensor_scalar(
                        q1l.full, ttl[pr][:, 1024 * hf:1024 * (hf + 1)],
                        stat[pr][:, 23:24], MAGIC, ALU.mult, ALU.add))
                    p.opc(VE, lambda e, hf=hf: e.tensor_scalar(
                        q1l.full, q1l.full, -MAGIC, 127.0, ALU.add, ALU.min))
                    p.opc(VE, lambda e, hf=hf: e.tensor_scalar(
                        ti[:, 1024 * hf:1024 * (hf + 1)], q1l.full, -128.0, None, ALU.max, ALU.bypass))
                vq = p.opc(VE, lambda e: e.engine_nop())
                p.wait(PE, S["ve"], vq)
                tr_ev = 0
                for grp in range(4):
                    if grp > 0:
                        p.wait(PE, S["ve"], tr_ev)
                    for c8 in range(8):
                        c = 8 * grp + c8
                        vt = p.opc(PE, lambda e, c=c, c8=c8: e.transpose(
                            pbt[:, 128 * c8:128 * (c8 + 1)], ti[:, 128 * c:128 * (c + 1)], ident[:, :]))
                    p.wait(VE, S["pe"], vt)
                    tr_ev = p.opc(VE, lambda e, grp=grp: e.tensor_copy(
                        tif[:, 1024 * grp:1024 * (grp + 1)], pbt[:, 0:1024]))
                vtf = p.opc(VE, lambda e: e.engine_nop())
                p.wait(PE, S["ve"], vtf)
                for og in range(2):
                    bank = pb[4 + og]
                    if dn_ev.get(og):
                        p.wait(PE, S["ve"], dn_ev[og])
                    for r in range(32):
                        vpe = p.opc(PE, lambda e, r=r, og=og, bank=bank: e.matmul(
                            bank[:, 0:512],
                            tif[:, 128 * r:128 * (r + 1)],
                            wdi[:, 1024 * r + 512 * og:1024 * r + 512 * (og + 1)],
                            start=(r == 0), stop=(r == 31)))
                p.wait(VE, S["pe"], vpe)
                p.wait(VE, S["LQ"], vx)
                for og in range(2):
                    bank = pb[4 + og]
                    p.opc(VE, lambda e, og=og, pr=pr, bank=bank: e.tensor_scalar(
                        smal[:, 0:512], bank[:, 0:512], stat[pr][:, 19:20], None, ALU.mult, ALU.bypass))
                    p.opc(VE, lambda e, og=og, bank=bank, b=b: e.tensor_tensor(
                        smal[:, 0:512], smal[:, 0:512], rep[(b, "gb")][:, 512 * og:512 * (og + 1)], ALU.mult))
                    p.opc(VE, lambda e, og=og, pr=pr: e.tensor_tensor(
                        xw[pr][:, 512 * og:512 * (og + 1)], smal[:, 0:512],
                        xt[pr][:, 512 * og:512 * (og + 1)], ALU.add))
                vxw = p.opc(VE, lambda e: e.engine_nop())
                p.wait(SY, S["ve"], vxw)
                stx_last[i] = p.store(xres[b, 128 * kt:128 * (kt + 1), :], xw[pr].full)
                vxn, _ = ln_h(xw[pr], stat[pr], h3tm, repL1[(b, "ca1")], repL1[(b, "sa")], xnb3)
                last_se[i] = vxn
                tm_to_fm(h3tm.full, h3fm[b].full, kt)
            A16.release(); A32.release()
            p.barrier()

        def p_l1_attn(repL1, h3fm, gacc):
            A16.mark(); A32.mark()
            xc = A16.alloc(16 * D)
            proj = A16.alloc(16 * 320)
            dtm = A16.alloc(1024)
            atm = A16.alloc(1024)
            btm = A16.alloc(1024)
            ctm = A16.alloc(1024)
            dsm = A16.alloc(2048)
            bsm = A16.alloc(2048)
            csm = A16.alloc(2048)
            abar = A16.alloc(2048)
            bbar = A16.alloc(2048)
            ssc1 = A16.alloc(2048)
            ssc2 = A16.alloc(2048)
            hsm = ssc1
            ysm = ssc2
            stm = A16.alloc(1024)
            h4tm = A16.alloc(1024)
            xnb4 = A16.alloc(1024)
            h4s = A16.alloc(1024)
            cw = A32.alloc(32)
            cvb = A32.alloc(8)
            ipw = A16.alloc(8 * 320)
            ipbrep = A32.alloc(320)
            woutb = A16.alloc(1024)
            opbrep = A16.alloc(1024)
            acol = A32.alloc(8)
            xt = [A32.alloc(1024) for _ in range(2)]
            xw = [A32.alloc(1024) for _ in range(2)]
            stat = [A32.alloc(24) for _ in range(2)]
            v1 = p.dma(SY, cw.full, convw)
            v2 = p.dma(SY, cvb.full.rearrange("p (q c) -> p q c", q=1),
                       convb.rearrange("q (c p) -> p q c", c=8), allow_slow_non_contiguous=True)
            v3 = p.dma(GE, ipw.full.rearrange("p (r o) -> p r o", r=8),
                       inprojT.rearrange("(r p) o -> p r o", r=8))
            v4 = p.dma(GE, woutb[0:64, 0:1024], outprojT)
            v5 = p.dma(SY, smal[0:1, 20:340], inprojb)
            valog = p.dma(SY, smal[0:64, 344:345], alog)
            p.wait(PE, S["LQ"], v5)
            vpe = p.opc(PE, lambda e: e.matmul(pC[:, 0:320], ones1f[0:1, :], smal[0:1, 20:340],
                                               start=True, stop=True))
            p.wait(VE, S["pe"], vpe)
            p.opc(VE, lambda e: e.tensor_copy(ipbrep.full, pC[:, 0:320]))
            vop = p.dma(GE, row0[0:1, 0:1024], outprojb)
            p.wait(PE, S["LG"], vop)
            p.wait(PE, S["ve"])
            bcast_row(row0, D, opbrep.full)
            p.wait(SE, S["LQ"], valog)
            p.drain(SE)
            vex = p.opc(SE, lambda e: e.activation(acol[0:64, 0:1], smal[0:64, 344:345], AF.Exp))
            p.wait(VE, S["se"], vex)
            p.opc(VE, lambda e: e.tensor_scalar(acol[0:64, 0:1], acol[0:64, 0:1], -1.0, None, ALU.mult, ALU.bypass))
            p.drain(VE)
            p.opc(VE, lambda e: e.tensor_scalar(acol[0:64, 3:4], acol[0:64, 0:1], -1.0, None, ALU.mult, ALU.bypass))
            p.drain(VE)
            p.opc(VE, lambda e: e.tensor_scalar(acol[0:64, 3:4], acol[0:64, 3:4], 1e-4, None, ALU.is_lt, ALU.bypass))
            p.drain(VE)
            p.opc(VE, lambda e: e.tensor_scalar(acol[0:64, 4:5], acol[0:64, 3:4], -1.0, 1.0, ALU.mult, ALU.add))
            p.drain(VE)
            p.opc(VE, lambda e: e.tensor_scalar(acol[0:64, 1:2], acol[0:64, 0:1], acol[0:64, 4:5], None, ALU.mult, ALU.bypass))
            p.drain(VE)
            p.opc(VE, lambda e: e.tensor_tensor(acol[0:64, 1:2], acol[0:64, 1:2], acol[0:64, 3:4], ALU.add))
            p.drain(VE)
            p.opc(VE, lambda e: e.reciprocal(acol[0:64, 2:3], acol[0:64, 1:2]))
            p.drain(VE)
            p.barrier()
            for b in range(NB):
                hb = h3fm[b]
                p.wait(VE, S["LQ"], v2)
                for c in range(8):
                    p.opc(VE, lambda e, c=c, hb=hb: e.tensor_scalar(
                        xc[:, 2048 * c: 2048 * c + 2047], hb[:, 2048 * c: 2048 * c + 2047],
                        cw[:, 4 * c + 1:4 * c + 2], cvb[:, c:c + 1], ALU.mult, ALU.add))
                    p.opc(VE, lambda e, c=c, hb=hb: e.scalar_tensor_tensor(
                        xc[:, 2048 * c: 2048 * c + 2047], hb[:, 2048 * c + 1: 2048 * c + 2048],
                        cw[:, 4 * c + 2:4 * c + 3], xc[:, 2048 * c: 2048 * c + 2047], ALU.mult, ALU.add))
                    p.opc(VE, lambda e, c=c, hb=hb: e.scalar_tensor_tensor(
                        xc[:, 2048 * c + 1: 2048 * c + 2047], hb[:, 2048 * c: 2048 * c + 2046],
                        cw[:, 4 * c:4 * c + 1], xc[:, 2048 * c + 1: 2048 * c + 2047], ALU.mult, ALU.add))
                    p.opc(VE, lambda e, c=c, hb=hb: e.scalar_tensor_tensor(
                        xc[:, 2048 * c: 2048 * c + 2046], hb[:, 2048 * c + 2: 2048 * c + 2048],
                        cw[:, 4 * c + 3:4 * c + 4], xc[:, 2048 * c: 2048 * c + 2046], ALU.mult, ALU.add))
                    p.opc(VE, lambda e, c=c: e.memset(xc[:, 2048 * c + 2047: 2048 * c + 2048], 0.0))
                vconv = p.opc(VE, lambda e: e.engine_nop())
                p.wait(PE, S["ve"], vconv)
                p.wait(PE, S["LG"], v3)
                for kt in range(NT):
                    bank = pb[kt % 2]
                    p.wait(PE, S["ve"])
                    for r in range(8):
                        vpe = p.opc(PE, lambda e, r=r, kt=kt, bank=bank: e.matmul(
                            bank[:, 0:320],
                            xc[:, 2048 * r + 128 * kt: 2048 * r + 128 * (kt + 1)],
                            ipw[:, 320 * r: 320 * (r + 1)],
                            start=(r == 0), stop=(r == 7)))
                    p.wait(VE, S["pe"], vpe)
                    p.opc(VE, lambda e, kt=kt, bank=bank: e.tensor_tensor(
                        proj[:, 320 * kt:320 * (kt + 1)], bank[:, 0:320], ipbrep.full, ALU.add))
                vproj = p.opc(VE, lambda e: e.engine_nop())
                prv = proj.full.rearrange("p (k o) -> p k o", k=16)
                p.opc(VE, lambda e: e.tensor_scalar(atm.full, prv[:, :, 0:64], -1.0, None, ALU.mult, ALU.bypass))
                p.drain(VE)
                p.opc(VE, lambda e: e.tensor_tensor(stm.full, atm.full.rearrange("p (k o) -> p k o", k=16), prv[:, :, 0:64], ALU.max))
                p.drain(VE)
                vabs = p.opc(VE, lambda e: e.engine_nop())
                p.wait(SE, S["ve"], vabs)
                p.drain(SE)
                p.opc(SE, lambda e: e.activation(atm.full, stm.full, AF.Exp, scale=-1.0))
                p.drain(SE)
                vln = p.opc(SE, lambda e: e.activation(stm.full, atm.full, AF.Ln, bias=1.0))
                p.wait(VE, S["se"], vln)
                p.opc(VE, lambda e: e.tensor_scalar(atm.full, prv[:, :, 0:64], 0.0, None, ALU.max, ALU.bypass))
                p.drain(VE)
                p.opc(VE, lambda e: e.tensor_tensor(dtm.full, atm.full, stm.full, ALU.add))
                p.opc(VE, lambda e: e.tensor_tensor(
                    btm.full, prv[:, :, 64:192].rearrange("p k (o two) -> p k o two", two=2)[:, :, :, 0],
                    prv[:, :, 64:192].rearrange("p k (o two) -> p k o two", two=2)[:, :, :, 1], ALU.add))
                p.opc(VE, lambda e: e.tensor_scalar(btm.full, btm.full, 0.5, None, ALU.mult, ALU.bypass))
                p.opc(VE, lambda e: e.tensor_tensor(
                    ctm.full, prv[:, :, 192:320].rearrange("p k (o two) -> p k o two", two=2)[:, :, :, 0],
                    prv[:, :, 192:320].rearrange("p k (o two) -> p k o two", two=2)[:, :, :, 1], ALU.add))
                p.opc(VE, lambda e: e.tensor_scalar(ctm.full, ctm.full, 0.5, None, ALU.mult, ALU.bypass))
                p.drain(VE)
                vtmq = p.opc(VE, lambda e: e.engine_nop())
                p.wait(PE, S["ve"], vtmq)
                for qi, (src, dst) in enumerate(((dtm, dsm), (btm, bsm), (ctm, csm))):
                    for kt in range(NT):
                        slot = kt % 8
                        if kt % 8 == 0:
                            p.wait(PE, S["ve"])
                        vt = p.opc(PE, lambda e, kt=kt, slot=slot, src=src: e.transpose(
                            pbt[0:64, 128 * slot:128 * (slot + 1)],
                            src[:, 64 * kt:64 * (kt + 1)], ident[:, :]))
                        if kt % 8 == 7:
                            p.wait(VE, S["pe"], vt)
                            g8 = kt // 8
                            p.opc(VE, lambda e, g8=g8, dst=dst: e.tensor_copy(
                                dst[0:64, 1024 * g8:1024 * (g8 + 1)], pbt[0:64, 0:1024]))
                vsm = p.opc(VE, lambda e: e.engine_nop())
                p.wait(SE, S["ve"], vsm)
                p.drain(SE)
                vab_ = p.opc(SE, lambda e: e.activation(abar[0:64, 0:2048], dsm[0:64, 0:2048],
                                                        AF.Exp, scale=acol[0:64, 0:1]))
                p.wait(VE, S["se"], vab_)
                p.opc(VE, lambda e: e.tensor_scalar(dsm[0:64, 0:2048], dsm[0:64, 0:2048],
                                                    acol[0:64, 0:1], None, ALU.mult, ALU.bypass))
                p.drain(VE)
                p.opc(VE, lambda e: e.tensor_scalar(ssc1[0:64, 0:2048], abar[0:64, 0:2048],
                                                    -1.0, acol[0:64, 2:3], ALU.add, ALU.mult))
                p.opc(VE, lambda e: e.tensor_scalar(ssc2[0:64, 0:2048], dsm[0:64, 0:2048],
                                                    1.0 / 6.0, 0.5, ALU.mult, ALU.add))
                p.drain(VE)
                p.opc(VE, lambda e: e.tensor_tensor(ssc2[0:64, 0:2048], ssc2[0:64, 0:2048],
                                                    dsm[0:64, 0:2048], ALU.mult))
                p.opc(VE, lambda e: e.tensor_scalar(ssc2[0:64, 0:2048], ssc2[0:64, 0:2048],
                                                    1.0, None, ALU.add, ALU.bypass))
                p.drain(VE)
                p.opc(VE, lambda e: e.tensor_scalar(ssc1[0:64, 0:2048], ssc1[0:64, 0:2048],
                                                    acol[0:64, 4:5], None, ALU.mult, ALU.bypass))
                p.drain(VE)
                p.opc(VE, lambda e: e.scalar_tensor_tensor(ssc1[0:64, 0:2048], ssc2[0:64, 0:2048],
                                                           acol[0:64, 3:4], ssc1[0:64, 0:2048], ALU.mult, ALU.add))
                p.opc(VE, lambda e: e.tensor_tensor(bbar[0:64, 0:2048], ssc1[0:64, 0:2048],
                                                    bsm[0:64, 0:2048], ALU.mult))
                p.drain(VE)
                p.opc(VE, lambda e: e.tensor_tensor_scan(hsm[0:64, 0:2048], abar[0:64, 0:2048],
                                                         bbar[0:64, 0:2048], 0.0, ALU.mult, ALU.add))
                p.drain(VE)
                p.opc(VE, lambda e: e.tensor_tensor(ysm[0:64, 0:2048], csm[0:64, 0:2048],
                                                    hsm[0:64, 0:2048], ALU.mult))
                p.drain(VE)
                vy = p.opc(VE, lambda e: e.engine_nop())
                p.wait(PE, S["ve"], vy)
                p.wait(PE, S["LG"], v4)
                last_se = {}
                st_last = {}
                for kt in range(NT):
                    pr = kt % 2
                    if kt >= 2 and (kt - 2) in last_se:
                        p.wait(SY, S["se"], last_se[kt - 2])
                    if kt >= 2:
                        p.wait(VE, S["SQ"], st_last[kt - 2])
                    vx = p.dma(SY, xt[pr].full, xres[b, 128 * kt:128 * (kt + 1), :])
                    for og in range(2):
                        bank = pb[2 + og]
                        p.wait(PE, S["ve"])
                        vpe = p.opc(PE, lambda e, kt=kt, og=og, bank=bank: e.matmul(
                            bank[:, 0:512],
                            ysm[0:64, 128 * kt:128 * (kt + 1)],
                            woutb[0:64, 512 * og:512 * (og + 1)],
                            start=True, stop=True))
                    p.wait(VE, S["pe"], vpe)
                    p.wait(VE, S["LQ"], vx)
                    for og in range(2):
                        bank = pb[2 + og]
                        p.opc(VE, lambda e, og=og, bank=bank: e.tensor_tensor(
                            smal[:, 0:512], bank[:, 0:512], opbrep[:, 512 * og:512 * (og + 1)], ALU.add))
                        p.opc(VE, lambda e, og=og, b=b: e.tensor_tensor(
                            smal[:, 0:512], smal[:, 0:512], repL1[(b, "ga")][:, 512 * og:512 * (og + 1)], ALU.mult))
                        p.opc(VE, lambda e, og=og, pr=pr: e.tensor_tensor(
                            xw[pr][:, 512 * og:512 * (og + 1)], smal[:, 0:512],
                            xt[pr][:, 512 * og:512 * (og + 1)], ALU.add))
                    vxw = p.opc(VE, lambda e: e.engine_nop())
                    p.wait(SY, S["ve"], vxw)
                    p.store(xres[b, 128 * kt:128 * (kt + 1), :], xw[pr].full)
                    vxn, _ = ln_h(xw[pr], stat[pr], h4tm, repL1[(b, "cb1")], repL1[(b, "sb")], xnb4)
                    last_se[kt] = vxn
                    p.opc(VE, lambda e, pr=pr: e.tensor_reduce(
                        stat[pr][:, 20:21], h4tm[:, 0:D], AX.X, ALU.max, apply_absolute_value=True))
                    p.drain(VE)
                    p.opc(VE, lambda e, pr=pr: e.tensor_tensor(
                        gacc[:, 0:1], gacc[:, 0:1], stat[pr][:, 20:21], ALU.max))
                    p.drain(VE)
                    vq = p.opc(VE, lambda e: e.engine_nop())
                    p.wait(PE, S["ve"], vq)
                    for c in range(8):
                        vt = p.opc(PE, lambda e, c=c: e.transpose(
                            pbt[:, 128 * c:128 * (c + 1)], h4tm[:, 128 * c:128 * (c + 1)], ident[:, :]))
                    p.wait(VE, S["pe"], vt)
                    vh4 = p.opc(VE, lambda e: e.tensor_copy(h4s.full, pbt[:, 0:1024]))
                    p.wait(SY, S["ve"], vh4)
                    st_last[kt] = p.store(h4_sp[b, kt], h4s.full)
            A16.release(); A32.release()
            p.barrier()

        def p_gmax_cc(gacc, ccin, ccout, dst_col):
            vv = p.opc(VE, lambda e: e.engine_nop())
            p.wait(GE, S["ve"], vv)
            p.opc(GE, lambda e: e.tensor_reduce(persc[0:1, 0:1], gacc[:, 0:1], AX.C, ALU.max))
            p.opc(GE, lambda e: e.memset(persc[0:1, 4:12], 0.0))
            vg = p.opc(GE, lambda e: e.tensor_copy(persc[0:1, 4:5], persc[0:1, 0:1]))
            vd = p.dma(GE, ccin, persc[0:1, 4:12])
            p.wait(GE, S["LG"], vd)
            vcc = p.op(GE, lambda e: e.collective_compute(
                "AllReduce", ALU.max, replica_groups=[list(range(8))],
                ins=[ccin], outs=[ccout]), S["CC"])
            p.wait(GE, S["CC"], vcc)
            vb = p.dma(GE, persc[0:1, 4:12], ccout)
            p.wait(VE, S["LG"], vb)
            p.opc(VE, lambda e, dst_col=dst_col: e.tensor_copy(persc[0:1, dst_col:dst_col + 1], persc[0:1, 4:5]))
            p.drain(VE)
            p.barrier()

        def rowmax_to_row(w_nat, nrows, width, rowf):
            A16.mark(); A32.mark()
            wt = A16.alloc(width)
            nt_ = nrows // 128
            sqc = A32.alloc(nt_ + 1)
            for r in range(nt_):
                vw = p.dma(GE, wt.full, w_nat[128 * r:128 * (r + 1), :])
                p.wait(VE, S["LG"], vw)
                p.opc(VE, lambda e, r=r: e.tensor_reduce(
                    sqc[:, r:r + 1], wt.full, AX.X, ALU.max, apply_absolute_value=True))
                p.drain(VE)
                vv = p.opc(VE, lambda e: e.engine_nop())
                p.wait(GE, S["ve"], vv)
            vv = p.opc(VE, lambda e: e.engine_nop())
            p.wait(SY, S["ve"], vv)
            vst = p.store(scr2[:, 0:nt_], sqc[:, 0:nt_])
            p.wait(SY, S["SQ"], vst)
            vrb = p.dma(SY, rowf.h[0:1, rowf.off:rowf.off + nrows].rearrange("q (r p) -> q r p", r=nt_),
                        scr2[:, 0:nt_].rearrange("p r -> r p"), allow_slow_non_contiguous=True)
            p.wait(VE, S["LQ"], vrb)
            A16.release(); A32.release()

        def quant_w_rows(wi, nr, width, rrg):
            A32.mark()
            q1s = A32.alloc(512)
            for r in range(nr):
                for j0 in range(0, width, 512):
                    w_ = min(512, width - j0)
                    p.opc(VE, lambda e, r=r, j0=j0, w_=w_, q1s=q1s: e.tensor_tensor(
                        q1s[:, 0:w_], wi[:, width * r + j0:width * r + j0 + w_],
                        rrg[:, j0:j0 + w_], ALU.mult))
                    p.opc(VE, lambda e, w_=w_, q1s=q1s: e.tensor_scalar(
                        q1s[:, 0:w_], q1s[:, 0:w_], MAGIC, -MAGIC, ALU.add, ALU.add))
                    p.opc(VE, lambda e, w_=w_, q1s=q1s: e.tensor_scalar(
                        q1s[:, 0:w_], q1s[:, 0:w_], 127.0, -128.0, ALU.min, ALU.max))
                    p.opc(VE, lambda e, r=r, j0=j0, w_=w_, q1s=q1s: e.tensor_copy(
                        wi[:, width * r + j0:width * r + j0 + w_], q1s[:, 0:w_]))
            return p.opc(VE, lambda e: e.engine_nop())

        IW = 2816
        IGW = [512, 512, 512, 512, 512, 256]

        def p_ffn1_A(gcol):
            A16.mark(); A32.mark()
            wgi1 = A16.alloc(8 * IW)
            wui1 = A16.alloc(8 * IW)
            rr_g = A16.alloc(IW)
            er_g = A16.alloc(IW)
            rr_u = A16.alloc(IW)
            er_u = A16.alloc(IW)
            h4t = [A16.alloc(1024) for _ in range(2)]
            h4q = A16.alloc(1024)
            gt1 = A16.alloc(IW)
            ut1 = A16.alloc(IW)
            tt1 = A16.alloc(IW)
            cols = A32.alloc(8)
            gacc2 = A32.alloc(4)
            A32.mark()
            rowf = A32.alloc(IW)
            rowr = A32.alloc(IW)
            vcx = p.dma(SY, persc[0:1, 16:17], clipx)
            p.wait(SE, S["LQ"], vcx)
            p.drain(SE)
            vsg = p.opc(SE, lambda e: e.activation(persc[0:1, 17:18], persc[0:1, 16:17], AF.Sigmoid))
            p.wait(VE, S["se"], vsg)
            p.opc(VE, lambda e, gcol=gcol: e.tensor_tensor(
                persc[0:1, 18:19], persc[0:1, 17:18], persc[0:1, gcol:gcol + 1], ALU.mult))
            p.drain(VE)
            p.opc(VE, lambda e: e.tensor_scalar(persc[0:1, 19:20], persc[0:1, 18:19],
                                                1.0 / 127.0, None, ALU.mult, ALU.bypass))
            p.drain(VE)
            p.opc(VE, lambda e: e.reciprocal(persc[0:1, 20:21], persc[0:1, 19:20]))
            p.drain(VE)
            vs4 = p.opc(VE, lambda e: e.engine_nop())
            p.wait(PE, S["ve"], vs4)
            vpe = p.opc(PE, lambda e: e.matmul(pC[:, 0:1], ones1f[0:1, :], persc[0:1, 20:21],
                                               start=True, stop=True))
            p.wait(VE, S["pe"], vpe)
            p.opc(VE, lambda e: e.tensor_copy(cols[:, 0:1], pC[:, 0:1]))
            p.drain(VE)
            for (wn, wT, rr_, er_) in ((wg1n, wgT1, rr_g, er_g), (wu1n, wuT1, rr_u, er_u)):
                rowmax_to_row(wn, IW, 1024, rowf)
                p.opc(VE, lambda e: e.reciprocal(rowr[0:1, 0:IW], rowf[0:1, 0:IW]))
                p.drain(VE)
                p.opc(VE, lambda e: e.tensor_scalar(rowr[0:1, 0:IW], rowr[0:1, 0:IW], 127.0, None, ALU.mult, ALU.bypass))
                p.drain(VE)
                vrr = p.opc(VE, lambda e: e.engine_nop())
                p.wait(PE, S["ve"], vrr)
                for g0 in range(0, IW, 512):
                    w_ = min(512, IW - g0)
                    p.wait(PE, S["ve"])
                    vpe = p.opc(PE, lambda e, g0=g0, w_=w_: e.matmul(
                        pC[:, 0:w_], ones1f[0:1, :], rowr.h[0:1, rowr.off + g0:rowr.off + g0 + w_],
                        start=True, stop=True))
                    p.wait(VE, S["pe"], vpe)
                    p.opc(VE, lambda e, g0=g0, w_=w_, rr_=rr_: e.tensor_copy(rr_[:, g0:g0 + w_], pC[:, 0:w_]))
                p.opc(VE, lambda e: e.tensor_scalar(rowr[0:1, 0:IW], rowf[0:1, 0:IW], persc[0:1, 19:20],
                                                    1.0 / 127.0, ALU.mult, ALU.mult))
                p.drain(VE)
                ver = p.opc(VE, lambda e: e.engine_nop())
                p.wait(PE, S["ve"], ver)
                for g0 in range(0, IW, 512):
                    w_ = min(512, IW - g0)
                    p.wait(PE, S["ve"])
                    vpe = p.opc(PE, lambda e, g0=g0, w_=w_: e.matmul(
                        pC[:, 0:w_], ones1f[0:1, :], rowr.h[0:1, rowr.off + g0:rowr.off + g0 + w_],
                        start=True, stop=True))
                    p.wait(VE, S["pe"], vpe)
                    p.opc(VE, lambda e, g0=g0, w_=w_, er_=er_: e.tensor_copy(er_[:, g0:g0 + w_], pC[:, 0:w_]))
            A32.release()
            q1h = A32.alloc(1024)
            vwg = p.dma(GE, wgi1.full.rearrange("p (r o) -> p r o", r=8),
                        wgT1.rearrange("(r p) o -> p r o", r=8))
            p.wait(VE, S["LG"], vwg)
            quant_w_rows(wgi1, 8, IW, rr_g)
            vwu = p.dma(GE, wui1.full.rearrange("p (r o) -> p r o", r=8),
                        wuT1.rearrange("(r p) o -> p r o", r=8))
            p.wait(VE, S["LG"], vwu)
            vqu = quant_w_rows(wui1, 8, IW, rr_u)
            p.opc(VE, lambda e: e.memset(gacc2[:, 0:1], 0.0))
            p.drain(VE)
            if DEBUG and "erg" in dbg:
                p.store(dbg["erg"], er_g[:, 0:IW])
                p.store(dbg["rrg"], rr_g[:, 0:IW])
            p.barrier()
            last_tt = {}
            last_vt = 0
            bank_ev1 = {}
            last_silu = 0
            for i in range(2 * NT):
                b, kt = divmod(i, NT)
                pr = i % 2
                vh_ = p.dma(SY, h4t[pr].full, h4_sp[b, kt])
                p.wait(VE, S["LQ"], vh_)
                if i >= 1:
                    p.wait(VE, S["pe"], last_mm)
                p.opc(VE, lambda e, pr=pr: e.tensor_scalar(
                    q1h.full, h4t[pr].full, cols[:, 0:1], MAGIC, ALU.mult, ALU.add))
                p.opc(VE, lambda e: e.tensor_scalar(q1h.full, q1h.full, -MAGIC, 127.0, ALU.add, ALU.min))
                vq = p.opc(VE, lambda e: e.tensor_scalar(h4q.full, q1h.full, -128.0, None, ALU.max, ALU.bypass))
                if i >= 1:
                    p.wait(SE, S["ve"], last_vt)
                p.wait(PE, S["ve"], vq)
                for which, wsrc, er_ in ((0, wgi1, er_g), (1, wui1, er_u)):
                    j0 = 0
                    for gi, gw in enumerate(IGW):
                        gidx = which * 6 + gi
                        bank = pb[gidx % 4]
                        if bank_ev1.get(gidx % 4):
                            ee, vv_ = bank_ev1[gidx % 4]
                            p.wait(PE, ee, vv_)
                        for r in range(8):
                            vpe = p.opc(PE, lambda e, r=r, j0=j0, gw=gw, wsrc=wsrc, bank=bank: e.matmul(
                                bank[:, 0:gw],
                                h4q[:, 128 * r:128 * (r + 1)],
                                wsrc[:, IW * r + j0:IW * r + j0 + gw],
                                start=(r == 0), stop=(r == 7)))
                        p.wait(VE, S["pe"], vpe)
                        if which == 0:
                            p.wait(VE, S["se"], last_silu)
                            vv_ = p.opc(VE, lambda e, j0=j0, gw=gw, bank=bank: e.tensor_tensor(
                                smal[:, 0:gw], bank[:, 0:gw], er_g[:, j0:j0 + gw], ALU.mult))
                            bank_ev1[gidx % 4] = (S["ve"], vv_)
                            vsm = p.opc(VE, lambda e: e.engine_nop())
                            p.wait(SE, S["ve"], vsm)
                            last_silu = p.opc(SE, lambda e, j0=j0, gw=gw: e.activation(
                                gt1[:, j0:j0 + gw], smal[:, 0:gw], AF.Silu))
                        else:
                            vv_ = p.opc(VE, lambda e, j0=j0, gw=gw, bank=bank: e.tensor_tensor(
                                ut1[:, j0:j0 + gw], bank[:, 0:gw], er_u[:, j0:j0 + gw], ALU.mult))
                            bank_ev1[gidx % 4] = (S["ve"], vv_)
                        j0 += gw
                last_mm = vpe
                p.drain(SE)
                vse = p.opc(SE, lambda e: e.nop())
                p.wait(VE, S["se"], vse)
                if i >= 1:
                    p.wait(VE, S["SQ"], last_tt[i - 1])
                vt = p.opc(VE, lambda e: e.tensor_tensor(tt1.full, gt1.full, ut1.full, ALU.mult))
                last_vt = vt
                p.drain(VE)
                p.opc(VE, lambda e: e.tensor_reduce(smal[:, 8:9], tt1.full, AX.X, ALU.max,
                                                    apply_absolute_value=True))
                p.drain(VE)
                p.opc(VE, lambda e: e.tensor_tensor(gacc2[:, 0:1], gacc2[:, 0:1], smal[:, 8:9], ALU.max))
                p.drain(VE)
                vt2 = p.opc(VE, lambda e: e.engine_nop())
                p.wait(SY, S["ve"], vt2)
                last_tt[i] = p.store(t1_sp[b, kt], tt1.full)
                if DEBUG and i == 0 and "tt1" in dbg:
                    p.store(dbg["tt1"], tt1.full)
                    p.store(dbg["h4q0"], h4q.full)
            p.barrier()
            p_gmax_cc(gacc2, cc_in2, cc_out2, 30)
            A16.release(); A32.release()
            p.barrier()

        def p_ffn1_B(repg):
            A16.mark(); A32.mark()
            wdi1 = A16.alloc(22 * 1024)
            rr_d = A16.alloc(1024)
            er_d = A16.alloc(1024)
            ttl = [A16.alloc(IW) for _ in range(2)]
            ti1 = A16.alloc(IW)
            tif1 = A16.alloc(IW)
            q1s2 = A32.alloc(512)
            cols = A32.alloc(8)
            xt = [A32.alloc(1024) for _ in range(2)]
            A32.mark()
            rowf = A32.alloc(1024)
            rowr = A32.alloc(1024)
            vcx = p.dma(SY, persc[0:1, 16:17], clipd)
            p.wait(SE, S["LQ"], vcx)
            p.drain(SE)
            vsg = p.opc(SE, lambda e: e.activation(persc[0:1, 17:18], persc[0:1, 16:17], AF.Sigmoid))
            p.wait(VE, S["se"], vsg)
            p.opc(VE, lambda e: e.tensor_tensor(
                persc[0:1, 18:19], persc[0:1, 17:18], persc[0:1, 30:31], ALU.mult))
            p.drain(VE)
            p.opc(VE, lambda e: e.tensor_scalar(persc[0:1, 19:20], persc[0:1, 18:19],
                                                1.0 / 127.0, None, ALU.mult, ALU.bypass))
            p.drain(VE)
            p.opc(VE, lambda e: e.reciprocal(persc[0:1, 20:21], persc[0:1, 19:20]))
            p.drain(VE)
            vs5 = p.opc(VE, lambda e: e.engine_nop())
            p.wait(PE, S["ve"], vs5)
            vpe = p.opc(PE, lambda e: e.matmul(pC[:, 0:1], ones1f[0:1, :], persc[0:1, 20:21],
                                               start=True, stop=True))
            p.wait(VE, S["pe"], vpe)
            p.opc(VE, lambda e: e.tensor_copy(cols[:, 0:1], pC[:, 0:1]))
            p.drain(VE)
            rowmax_to_row(wd1n, 1024, IW, rowf)
            p.opc(VE, lambda e: e.reciprocal(rowr[0:1, 0:1024], rowf[0:1, 0:1024]))
            p.drain(VE)
            p.opc(VE, lambda e: e.tensor_scalar(rowr[0:1, 0:1024], rowr[0:1, 0:1024], 127.0, None, ALU.mult, ALU.bypass))
            p.drain(VE)
            vrr = p.opc(VE, lambda e: e.engine_nop())
            p.wait(PE, S["ve"], vrr)
            for g0 in range(0, 1024, 512):
                p.wait(PE, S["ve"])
                vpe = p.opc(PE, lambda e, g0=g0: e.matmul(
                    pC[:, 0:512], ones1f[0:1, :], rowr.h[0:1, rowr.off + g0:rowr.off + g0 + 512],
                    start=True, stop=True))
                p.wait(VE, S["pe"], vpe)
                p.opc(VE, lambda e, g0=g0: e.tensor_copy(rr_d[:, g0:g0 + 512], pC[:, 0:512]))
            p.opc(VE, lambda e: e.tensor_scalar(rowr[0:1, 0:1024], rowf[0:1, 0:1024], persc[0:1, 19:20],
                                                1.0 / 127.0, ALU.mult, ALU.mult))
            p.drain(VE)
            ver = p.opc(VE, lambda e: e.engine_nop())
            p.wait(PE, S["ve"], ver)
            for g0 in range(0, 1024, 512):
                p.wait(PE, S["ve"])
                vpe = p.opc(PE, lambda e, g0=g0: e.matmul(
                    pC[:, 0:512], ones1f[0:1, :], rowr.h[0:1, rowr.off + g0:rowr.off + g0 + 512],
                    start=True, stop=True))
                p.wait(VE, S["pe"], vpe)
                p.opc(VE, lambda e, g0=g0: e.tensor_copy(er_d[:, g0:g0 + 512], pC[:, 0:512]))
            A32.release()
            vwd = p.dma(GE, wdi1.full.rearrange("p (r o) -> p r o", r=22),
                        wdT1.rearrange("(r p) o -> p r o", r=22))
            p.wait(VE, S["LG"], vwd)
            quant_w_rows(wdi1, 22, 1024, rr_d)
            p.barrier()
            last_se = {}
            sty_last = {}
            dn_ev = {}
            nloop = 0 if STOP_AFTER == "ffnBsetup" else (2 if STOP_AFTER == "ffnB2" else 2 * NT)
            for i in range(nloop):
                b, kt = divmod(i, NT)
                pr = i % 2
                if i >= 2 and (i - 2) in last_se:
                    p.wait(SY, S["ve"], last_se[i - 2])
                    p.wait(SY, S["SQ"], sty_last[i - 2])
                vtl = p.dma(SY, ttl[pr].full, t1_sp[b, kt])
                vx = p.dma(SY, xt[pr].full, xres[b, 128 * kt:128 * (kt + 1), :])
                p.wait(VE, S["LQ"], vtl)
                for j0 in range(0, IW, 512):
                    w_ = min(512, IW - j0)
                    p.opc(VE, lambda e, j0=j0, w_=w_, pr=pr: e.tensor_scalar(
                        q1s2[:, 0:w_], ttl[pr][:, j0:j0 + w_], cols[:, 0:1], MAGIC, ALU.mult, ALU.add))
                    p.opc(VE, lambda e, w_=w_: e.tensor_scalar(
                        q1s2[:, 0:w_], q1s2[:, 0:w_], -MAGIC, 127.0, ALU.add, ALU.min))
                    p.opc(VE, lambda e, j0=j0, w_=w_: e.tensor_scalar(
                        ti1[:, j0:j0 + w_], q1s2[:, 0:w_], -128.0, None, ALU.max, ALU.bypass))
                vq = p.opc(VE, lambda e: e.engine_nop())
                p.wait(PE, S["ve"], vq)
                tr_ev = 0
                for grp in range(3):
                    n8 = 8 if grp < 2 else 6
                    if grp > 0:
                        p.wait(PE, S["ve"], tr_ev)
                    for c8 in range(n8):
                        c = 8 * grp + c8
                        vt = p.opc(PE, lambda e, c=c, c8=c8: e.transpose(
                            pbt[:, 128 * c8:128 * (c8 + 1)], ti1[:, 128 * c:128 * (c + 1)], ident[:, :]))
                    p.wait(VE, S["pe"], vt)
                    tr_ev = p.opc(VE, lambda e, grp=grp, n8=n8: e.tensor_copy(
                        tif1[:, 1024 * grp:1024 * grp + 128 * n8], pbt[:, 0:128 * n8]))
                vtf = p.opc(VE, lambda e: e.engine_nop())
                p.wait(PE, S["ve"], vtf)
                for og in range(2):
                    bank = pb[4 + og]
                    if dn_ev.get(og):
                        p.wait(PE, S["ve"], dn_ev[og])
                    for r in range(22):
                        vpe = p.opc(PE, lambda e, r=r, og=og, bank=bank: e.matmul(
                            bank[:, 0:512],
                            tif1[:, 128 * r:128 * (r + 1)],
                            wdi1[:, 1024 * r + 512 * og:1024 * r + 512 * (og + 1)],
                            start=(r == 0), stop=(r == 21)))
                p.wait(VE, S["pe"], vpe)
                p.wait(VE, S["LQ"], vx)
                for og in range(2):
                    bank = pb[4 + og]
                    dn_ev[og] = p.opc(VE, lambda e, og=og, bank=bank: e.tensor_tensor(
                        smal[:, 0:512], bank[:, 0:512], er_d[:, 512 * og:512 * (og + 1)], ALU.mult))
                    p.opc(VE, lambda e, og=og, b=b: e.tensor_tensor(
                        smal[:, 0:512], smal[:, 0:512], repg[(b, "gb")][:, 512 * og:512 * (og + 1)], ALU.mult))
                    p.opc(VE, lambda e, og=og: e.tensor_copy(q1s2[:, 0:512], smal[:, 0:512]))
                    vfin = p.opc(VE, lambda e, og=og, pr=pr: e.tensor_tensor(
                        xt[pr][:, 512 * og:512 * (og + 1)], q1s2[:, 0:512],
                        xt[pr][:, 512 * og:512 * (og + 1)], ALU.add))
                last_se[i] = vfin
                p.wait(SY, S["ve"], vfin)
                sty = p.store(y_out[b, 128 * kt:128 * (kt + 1), :], xt[pr].full)
                sty_last[i] = sty
            A16.release(); A32.release()
            p.barrier()

        def p_h1(b, rep, h1fm, h1tm, xnb):
            A32.mark()
            xt = [A32.alloc(1024) for _ in range(2)]
            stat = [A32.alloc(24) for _ in range(2)]
            last_se = {}
            for k in range(NT):
                if k >= 2 and (k - 2) in last_se:
                    p.wait(SY, S["se"], last_se[k - 2])
                vld = p.dma(SY, xt[k % 2].full, x_in[b, 128 * k:128 * (k + 1), :])
                p.wait(VE, S["LQ"], vld)
                vxn, _ = ln_h(xt[k % 2], stat[k % 2], h1tm, rep[(b, "ca1")], rep[(b, "sa")], xnb)
                last_se[k] = vxn
                tm_to_fm(h1tm.full, h1fm, k)
            A32.release()
            p.barrier()

        cst = {"negl": A32.alloc(4), "w1m": A16.alloc(64)}
        p_init()
        p_attn_consts(cst)
        p_mod(0, c0T, wmod0)
        p_mod(1, c1T, wmod1)
        for b in range(NB):
            A16.mark(); A32.mark()
            qfm = A16.alloc(8 * T); kfm = A16.alloc(8 * T)
            vab = A16.alloc(16 * 1040); qn = A16.alloc(8 * T)
            negkb = A32.alloc(32)
            A16.mark(); A32.mark()
            h1fm = A16.alloc(8 * T)
            A16.mark()
            rep1 = {(b, "ca1"): A16.alloc(D), (b, "sa"): A16.alloc(D)}
            build_rep(0, rep1)
            h1tm = A16.alloc(D); xnb = A16.alloc(D)
            p_h1(b, rep1, h1fm.full, h1tm, xnb)
            A16.release()
            p_qkv(b, h1fm.full, qfm, kfm, vab, qn, negkb)
            A16.release(); A32.release()
            p_core(b, qfm, kfm, vab, qn, negkb, cst)
            A16.release(); A32.release()
            A16.mark(); A32.mark()
            rep2 = {(b, "ga"): A16.alloc(D), (b, "cb1"): A16.alloc(D), (b, "sb"): A16.alloc(D)}
            build_rep(0, rep2)
            p_attn_out(b, rep2)
            A16.release(); A32.release()

        if DEBUG and "x1" in dbg:
            for b in range(NB):
                p.store(dbg["x1"][b], xres[b])
            p.barrier()

        p_ffn0_A()
        A16.mark()
        h3fm = [A16.alloc(8 * T) for _ in range(NB)]
        A16.mark(); A32.mark()
        repB = {}
        repL1 = {}
        for b in range(NB):
            repB[(b, "gb")] = A16.alloc(D)
            repL1[(b, "ca1")] = A16.alloc(D)
            repL1[(b, "sa")] = A16.alloc(D)
        build_rep(0, repB)
        build_rep(1, repL1)
        if DEBUG and "repgb" in dbg:
            p.barrier()
            for b in range(NB):
                p.store(dbg["repgb"][b], repB[(b, "gb")].full)
            for b in range(NB):
                p.store(dbg["repca1"][b], repL1[(b, "ca1")].full)
        p_ffn0_B(repB, repL1, h3fm)
        A16.release(); A32.release()

        if DEBUG and "t0" in dbg:
            for b in range(NB):
                p.store(dbg["t0"][b], t0_sp[b, 0])
        if DEBUG and "rmq" in dbg:
            for b in range(NB):
                for kt in range(NT):
                    p.store(dbg["rmq"][b, kt], rmq_sp[b, kt])
        if DEBUG and "h2q" in dbg:
            for b in range(NB):
                for kt in range(2):
                    p.store(dbg["h2q"][b, kt], h2q_sp[b, kt])
        if DEBUG and "x2" in dbg:
            for b in range(NB):
                p.store(dbg["x2"][b], xres[b])
        if DEBUG and "h3fm" in dbg:
            for c in range(8):
                p.store(dbg["h3fm"][c], h3fm[0][:, T * c:T * (c + 1)])
        if DEBUG and ("x2" in dbg or "h3fm" in dbg):
            p.barrier()

        gacc1 = A32.alloc(4)
        p.opc(VE, lambda e: e.memset(gacc1[:, 0:1], 0.0))
        p.drain(VE)
        A16.mark(); A32.mark()
        repL1b = {}
        for b in range(NB):
            repL1b[(b, "ga")] = A16.alloc(D)
            repL1b[(b, "cb1")] = A16.alloc(D)
            repL1b[(b, "sb")] = A16.alloc(D)
        build_rep(1, repL1b)
        p_l1_attn(repL1b, h3fm, gacc1)
        A16.release(); A32.release()
        A16.release()

        if DEBUG and "x3" in dbg:
            for b in range(NB):
                p.store(dbg["x3"][b], xres[b])
            p.barrier()

        if STOP_AFTER not in ("x3",):
            p_gmax_cc(gacc1, cc_in1, cc_out1, 21)
        if STOP_AFTER not in ("x3", "cc1"):
            p_ffn1_A(21)
        if STOP_AFTER not in ("x3", "cc1", "ffnA"):
            A16.mark(); A32.mark()
            repg1 = {}
            for b in range(NB):
                repg1[(b, "gb")] = A16.alloc(D)
            build_rep(1, repg1)
            p_ffn1_B(repg1)
            A16.release(); A32.release()
        if DEBUG and "attnsp" in dbg:
            for b in range(NB):
                p.store(dbg["attnsp"][b], attn_sp[b])
        if DEBUG and "scol" in dbg:
            p.store(dbg["scol"], persc[0:1, 0:32])
        p.barrier()
        p.wait(SY, S["SQ"])
        p.wait(SY, S["LQ"])

        with nc.Block() as block:
            @block.tensor
            def _(e):
                for f in p.streams[PE]: f(e)

            @block.vector
            def _(e):
                for f in p.streams[VE]: f(e)

            @block.scalar
            def _(e):
                for f in p.streams[SE]: f(e)

            @block.gpsimd
            def _(e):
                for f in p.streams[GE]: f(e)

            @block.sync
            def _(e):
                for f in p.streams[SY]: f(e)

    return nc


def prep_inputs(I):
    C = np.ascontiguousarray
    shared = {
        "wqT": C(I["L0_attn_Wq"].T), "wkT": C(I["L0_attn_Wk"].T),
        "wvT": C(I["L0_attn_Wv"].T), "woT": C(I["L0_attn_Wo"].T),
        "lamq1": C(I["L0_attn_lambda_q1"].reshape(32, 1)),
        "lamk1": C(I["L0_attn_lambda_k1"].reshape(32, 1)),
        "lamq2": C(I["L0_attn_lambda_q2"].reshape(32, 1)),
        "lamk2": C(I["L0_attn_lambda_k2"].reshape(32, 1)),
        "sublnw": C(I["L0_attn_subln_w"].reshape(1, 64)),
        "wgT0": C(I["L0_ffn_w_gate"].T), "wuT0": C(I["L0_ffn_w_up"].T),
        "wdT0": C(I["L0_ffn_w_down"].T),
        "wmod0": C(I["L0_wmod"]),
        "convw": C(I["L1_attn_conv_w"][:, 0, :].reshape(8, 128, 4).transpose(1, 0, 2).reshape(128, 32)),
        "convb": C(I["L1_attn_conv_b"].reshape(1, D)),
        "inprojT": C(I["L1_attn_in_proj_w"].T),
        "inprojb": C(I["L1_attn_in_proj_b"].reshape(1, 320)),
        "alog": C(I["L1_attn_A_log"].reshape(64, 1)),
        "outprojT": C(I["L1_attn_out_proj_w"].T),
        "outprojb": C(I["L1_attn_out_proj_b"].reshape(1, D)),
        "clipx": C(I["L1_ffn_clip_x"].reshape(1, 1)),
        "clipd": C(I["L1_ffn_clip_down"].reshape(1, 1)),
        "wgT1": C(I["L1_ffn_w_gate"].T), "wuT1": C(I["L1_ffn_w_up"].T),
        "wdT1": C(I["L1_ffn_w_down"].T),
        "wg1n": C(I["L1_ffn_w_gate"]), "wu1n": C(I["L1_ffn_w_up"]),
        "wd1n": C(I["L1_ffn_w_down"]),
        "wmod1": C(I["L1_wmod"]),
    }
    c0T = C(I["L0_c"].T)
    c1T = C(I["L1_c"].T)
    per_core = []
    for cid in range(8):
        m = dict(shared)
        m["x_in"] = C(I["L0_attn_x"][2 * cid:2 * cid + 2])
        m["c0T"] = C(c0T[:, 2 * cid:2 * cid + 2])
        m["c1T"] = C(c1T[:, 2 * cid:2 * cid + 2])
        per_core.append(m)
    return per_core


def run(I, debug_names=()):
    in_maps = prep_inputs(I)
    nc = build_program(debug_names=debug_names)
    return run_bass_kernel_spmd(nc, in_maps, core_ids=list(range(8)))

def kernel(**inputs):
    in_maps = prep_inputs(inputs)
    nc = build_program()
    res = run_bass_kernel_spmd(nc, in_maps, core_ids=list(range(8)))
    return np.concatenate([np.asarray(r["y"]) for r in res.results], axis=0)


# version 04 (final)
# speedup vs baseline: 1.0863x; 1.0040x over previous
import math
import contextlib
import numpy as np
import concourse.bass as bass
import concourse.mybir as mybir
from concourse.bass_utils import run_bass_kernel_spmd

F32 = mybir.dt.float32
BF16 = mybir.dt.bfloat16
AF = mybir.ActivationFunctionType
ALU = mybir.AluOpType
AX = mybir.AxisListType

ENGINES = ("tensor", "vector", "scalar", "gpsimd", "sync")
PE, VE, SE, GE, SY = ENGINES
NB = 2
T = 2048
D = 1024
NT = T // 128
LAMBDA_INIT = 0.8 - 0.6 * math.exp(-0.3 * 12)
MAGIC = float(np.float32(1.5 * 2 ** 23))

DEBUG = False
STOP_AFTER = None


class Sem:
    def __init__(self, name):
        self.name = name
        self.h = None
        self.n = 0


class Buf:
    def __init__(self, h, off, cols, dt):
        self.h = h; self.off = off; self.cols = cols; self.dt = dt

    def __getitem__(self, idx):
        ps, fs = idx
        a = fs.start or 0
        b = fs.stop if fs.stop is not None else self.cols
        assert fs.step is None and 0 <= a <= b <= self.cols, (a, b, self.cols)
        return self.h[ps, self.off + a: self.off + b]

    @property
    def full(self):
        return self.h[:, self.off: self.off + self.cols]

    def sub(self, a, b):
        return Buf(self.h, self.off + a, b - a, self.dt)


class Arena:
    def __init__(self, h, n, dt, name):
        self.h = h; self.n = n; self.dt = dt; self.off = 0; self.name = name
        self.marks = []; self.hi = 0

    def alloc(self, cols, align=4):
        o = (self.off + align - 1) // align * align
        assert o + cols <= self.n, f"arena {self.name} overflow: {o}+{cols} > {self.n}"
        self.off = o + cols
        self.hi = max(self.hi, self.off)
        return Buf(self.h, o, cols, self.dt)

    def mark(self):
        self.marks.append(self.off)

    def release(self):
        self.off = self.marks.pop()


class Prog:
    def __init__(self, nc):
        self.nc = nc
        self.streams = {e: [] for e in ENGINES}
        self.sems = {}
        for s in ("pe", "ve", "se", "ge", "sy", "LQ", "LG", "SQ", "CC"):
            self.sems[s] = Sem(s)
        self._emap = {PE: "pe", VE: "ve", SE: "se", GE: "ge", SY: "sy"}

    def em(self, eng, fn):
        self.streams[eng].append(fn)

    def esem(self, eng):
        return self.sems[self._emap[eng]]

    def op(self, eng, fn, sem=None):
        if sem is not None:
            sem.n += 1
            v = sem.n
            self.em(eng, lambda e, fn=fn, s=sem: fn(e).then_inc(s.h, 1))
            return v
        self.em(eng, fn)
        return None

    def opc(self, eng, fn):
        return self.op(eng, fn, self.esem(eng))

    def wait(self, eng, sem, val=None):
        v = sem.n if val is None else val
        self.em(eng, lambda e, s=sem, v=v: e.wait_ge(s.h, v))

    def drain(self, eng):
        self.em(eng, lambda e: e.drain())

    def dma(self, eng, out, in_, sem=None, **kw):
        sem = sem or (self.sems["LG"] if eng == GE else self.sems["LQ"])
        sem.n += 16
        v = sem.n
        self.em(eng, lambda e, o=out, i=in_, s=sem, kw=kw: e.dma_start(out=o, in_=i, **kw).then_inc(s.h, 16))
        return v

    def store(self, out, in_, eng=SY):
        return self.dma(eng, out, in_, sem=self.sems["SQ"])

    def barrier(self):
        self.wait(SY, self.sems["LQ"])
        self.wait(SY, self.sems["SQ"])
        self.wait(GE, self.sems["LG"])
        vals = {}
        for eng in ENGINES:
            s = self.esem(eng)
            s.n += 1
            vals[eng] = s.n
            if eng in (GE, VE):
                self.em(eng, lambda e, s=s: e.engine_nop().then_inc(s.h, 1))
            elif eng in (PE, SE):
                self.drain(eng)
                self.em(eng, lambda e, s=s: e.nop().then_inc(s.h, 1))
            else:
                self.em(eng, lambda e, s=s: e.nop().then_inc(s.h, 1))
        for eng in ENGINES:
            for oth in ENGINES:
                if oth != eng:
                    self.wait(eng, self.esem(oth), vals[oth])


def build_program(debug_names=()):
    nc = bass.Bass("TRN2", target_bir_lowering=False)
    p = Prog(nc)
    S = p.sems

    dr = {}
    def din(name, shape, dt=F32):
        dr[name] = nc.dram_tensor(name, list(shape), dt, kind="ExternalInput").ap()
        return dr[name]
    def dscr(name, shape, dt=F32):
        dr[name] = nc.dram_tensor(name, list(shape), dt).ap()
        return dr[name]

    x_in = din("x_in", (NB, T, D))
    wqT = din("wqT", (D, D)); wkT = din("wkT", (D, D)); wvT = din("wvT", (D, D)); woT = din("woT", (D, D))
    lamq1 = din("lamq1", (32, 1)); lamk1 = din("lamk1", (32, 1))
    lamq2 = din("lamq2", (32, 1)); lamk2 = din("lamk2", (32, 1))
    sublnw = din("sublnw", (1, 64))
    wgT0 = din("wgT0", (D, 4096)); wuT0 = din("wuT0", (D, 4096)); wdT0 = din("wdT0", (4096, D))
    c0T = din("c0T", (D, NB)); wmod0 = din("wmod0", (D, 6 * D))
    convw = din("convw", (128, 32)); convb = din("convb", (1, D))
    inprojT = din("inprojT", (D, 320)); inprojb = din("inprojb", (1, 320))
    alog = din("alog", (64, 1))
    outprojT = din("outprojT", (64, D)); outprojb = din("outprojb", (1, D))
    clipx = din("clipx", (1, 1)); clipd = din("clipd", (1, 1))
    wgT1 = din("wgT1", (D, 2816)); wuT1 = din("wuT1", (D, 2816)); wdT1 = din("wdT1", (2816, D))
    wg1n = din("wg1n", (2816, D)); wu1n = din("wu1n", (2816, D)); wd1n = din("wd1n", (D, 2816))
    c1T = din("c1T", (D, NB)); wmod1 = din("wmod1", (D, 6 * D))

    y_out = nc.dram_tensor("y", [NB, T, D], F32, kind="ExternalOutput").ap()

    xres = dscr("xres", (NB, T, D))
    modscr = dscr("modscr", (2, NB, 6 * D), BF16)
    h2q_sp = dscr("h2q_sp", (NB, NT, 128, 1024), BF16)
    h4_sp = dscr("h4_sp", (NB, NT, 128, 1024), BF16)
    t0_sp = dscr("t0_sp", (NB, NT, 128, 4096), BF16)
    t1_sp = dscr("t1_sp", (NB, NT, 128, 2816), BF16)
    cc_in1 = dscr("cc_in1", (1, 8)); cc_out1 = dscr("cc_out1", (1, 8))
    cc_in2 = dscr("cc_in2", (1, 8)); cc_out2 = dscr("cc_out2", (1, 8))
    negscr = dscr("negscr", (4, 8))
    scr2 = dscr("scr2", (128, 32))
    rmq_sp = dscr("rmq_sp", (NB, NT, 128, 1))
    attn_sp = dscr("attn_sp", (NB, T, D), BF16)

    dbg = {}
    if DEBUG:
        for name, shape, dt in debug_names:
            dbg[name] = nc.dram_tensor("dbg_" + name, list(shape), dt, kind="ExternalOutput").ap()

    es = contextlib.ExitStack()
    sb = lambda n, s, d: es.enter_context(nc.sbuf_tensor(n, s, d))
    psm = lambda n, s, d: es.enter_context(nc.psum_tensor(n, s, d))

    with es:
        A16 = Arena(sb("arena16", [128, 91300], BF16), 91300, BF16, "A16")
        A32 = Arena(sb("arena32", [128, 6200], F32), 6200, F32, "A32")
        ident = sb("ident", [128, 128], BF16)
        ones1 = sb("ones1", [1, 128], BF16)
        ones1f = sb("ones1f", [1, 128], F32)
        ones4 = sb("ones4", [128, 4], BF16)
        io4 = sb("io4", [128, 4], F32)
        row0 = sb("row0", [1, 1024], BF16)
        smal = sb("smal", [128, 512], F32)
        persc = sb("persc", [128, 40], F32)
        pA = psm("pA", [128, 1536], F32)
        pB = psm("pB", [128, 1536], F32)
        pC = psm("pC", [128, 512], F32)
        pbt = psm("pbt", [128, 1024], BF16)
        pb = [Buf(pA, 512 * i, 512, F32) for i in range(3)] + \
             [Buf(pB, 512 * i, 512, F32) for i in range(3)] + [Buf(pC, 0, 512, F32)]

        for name, s in S.items():
            s.h = es.enter_context(nc.semaphore(name))

        def p_init():
            p.op(GE, lambda e: e.memset(ident.ap(), 0.0))
            p.opc(GE, lambda e: e.affine_select(
                out=ident[:, :], in_=ident[:, :], compare_op=ALU.not_equal,
                fill=1.0, base=0, pattern=[[-1, 128]], channel_multiplier=1))
            p.opc(GE, lambda e: e.memset(ones1.ap(), 1.0))
            p.opc(GE, lambda e: e.memset(ones1f.ap(), 1.0))
            p.barrier()

        def bcast_row(src_row, n, dst, pbank=6, f32=False):
            one = ones1f if f32 else ones1
            for g0 in range(0, n, 512):
                w = min(512, n - g0)
                p.wait(PE, S["ve"])
                vpe = p.opc(PE, lambda e, g0=g0, w=w, one=one: e.matmul(
                    pb[pbank][:, 0:w], one[0:1, :], src_row[0:1, g0:g0 + w],
                    start=True, stop=True))
                p.wait(VE, S["pe"], vpe)
                p.opc(VE, lambda e, g0=g0, w=w: e.tensor_copy(dst[:, g0:g0 + w], pb[pbank][:, 0:w]))

        def p_mod(L, cT, wmod):
            A16.mark(); A32.mark()
            cs = A32.alloc(16)
            csf = A32.alloc(16)
            mrow = A32.alloc(1024)
            mrow2 = A32.alloc(1024)
            wm = A32.alloc(8 * 512)
            modsb = A16.alloc(6 * D)
            v = p.dma(SY, cs.full.rearrange("p (c t) -> p c t", c=8),
                      cT.rearrange("(c p) t -> p c t", c=8),
                      allow_slow_non_contiguous=True)
            p.wait(SE, S["LQ"], v)
            vsil = p.opc(SE, lambda e: e.activation(csf.full, cs.full, AF.Silu))
            p.wait(PE, S["se"], vsil)
            evac_v = {}
            for g in range(12):
                if g >= 1:
                    p.wait(SY, S["pe"])
                vld = p.dma(SY, wm.full, wmod[:, 512 * g:512 * (g + 1)].rearrange("(c p) m -> p c m", c=8))
                p.wait(PE, S["LQ"], vld)
                if g >= 2:
                    p.wait(PE, S["ve"], evac_v[g - 2])
                for c in range(8):
                    vpe = p.opc(PE, lambda e, c=c, g=g: e.matmul(
                        pb[g % 2][0:2, :], csf[:, 2 * c:2 * c + 2],
                        wm.full.rearrange("p (c m) -> p c m", c=8)[:, c, :],
                        start=(c == 0), stop=(c == 7)))
                p.wait(VE, S["pe"], vpe)
                evac_v[g] = p.opc(VE, lambda e, g=g: e.tensor_copy(
                    modsb[0:2, 512 * g:512 * (g + 1)], pb[g % 2][0:2, :]))
            for sl in (1, 4):
                p.opc(VE, lambda e, sl=sl: e.tensor_scalar(
                    modsb[0:2, sl * D:(sl + 1) * D], modsb[0:2, sl * D:(sl + 1) * D],
                    1.0, None, ALU.add, ALU.bypass))
            for sl in (2, 5):
                p.opc(VE, lambda e, sl=sl: e.tensor_scalar(
                    mrow[0:2, 0:D], modsb[0:2, sl * D:(sl + 1) * D], 0.0, None, ALU.is_ge, ALU.bypass))
                p.drain(VE)
                p.opc(VE, lambda e: e.tensor_scalar(
                    mrow2[0:2, 0:D], mrow[0:2, 0:D], 2.0, -1.0, ALU.mult, ALU.add))
                p.opc(VE, lambda e, sl=sl: e.tensor_scalar(
                    mrow[0:2, 0:D], modsb[0:2, sl * D:(sl + 1) * D], -1.0, None, ALU.mult, ALU.bypass))
                p.drain(VE)
                p.opc(VE, lambda e, sl=sl: e.tensor_tensor(
                    mrow[0:2, 0:D], mrow[0:2, 0:D], modsb[0:2, sl * D:(sl + 1) * D], ALU.max))
                p.drain(VE)
                p.opc(VE, lambda e: e.tensor_scalar(
                    mrow[0:2, 0:D], mrow[0:2, 0:D], 0.25, None, ALU.max, ALU.bypass))
                p.drain(VE)
                p.opc(VE, lambda e, sl=sl: e.tensor_tensor(
                    modsb[0:2, sl * D:(sl + 1) * D], mrow2[0:2, 0:D], mrow[0:2, 0:D], ALU.mult))
                p.drain(VE)
            vrows = p.opc(VE, lambda e: e.engine_nop())
            p.wait(SY, S["ve"], vrows)
            p.store(modscr[L], modsb[0:2, :])
            if DEBUG and L == 0 and "modrow" in dbg:
                p.store(dbg["modrow"], modsb[0:2, :])
            if DEBUG and L == 0 and "csb" in dbg:
                p.store(dbg["csb"], csf.full)
            A16.release(); A32.release()
            p.barrier()

        SLOT_OF = {"sa": 0, "ca1": 1, "ga": 2, "sb": 3, "cb1": 4, "gb": 5}

        def build_rep(L, rep):
            for (b, kind), dst in rep.items():
                sl = SLOT_OF[kind]
                p.wait(SY, S["pe"])
                vr = p.dma(SY, row0[0:1, 0:D], modscr[L, b, sl * D:(sl + 1) * D][None, :])
                p.wait(PE, S["LQ"], vr)
                bcast_row(row0, D, dst.full)
            p.barrier()

        def ln_h(xt, stat, htm, ca1, sa_, xnb):
            p.opc(VE, lambda e: e.bn_stats(stat[:, 0:6], xt[:, 0:512]))
            p.opc(VE, lambda e: e.bn_stats(stat[:, 6:12], xt[:, 512:1024]))
            p.drain(VE)
            p.opc(VE, lambda e: e.bn_aggr(stat[:, 12:14], stat[:, 0:12]))
            p.drain(VE)
            p.opc(VE, lambda e: e.tensor_scalar(stat[:, 14:15], stat[:, 13:14], 1e-5, None, ALU.add, ALU.bypass))
            p.drain(VE)
            vv = p.opc(VE, lambda e: e.reciprocal(stat[:, 15:16], stat[:, 14:15]))
            p.wait(SE, S["ve"], vv)
            vs = p.opc(SE, lambda e: e.activation(stat[:, 16:17], stat[:, 15:16], AF.Sqrt))
            p.wait(VE, S["se"], vs)
            p.opc(VE, lambda e: e.tensor_scalar(stat[:, 17:18], stat[:, 12:13], stat[:, 16:17], -1.0, ALU.mult, ALU.mult))
            p.drain(VE)
            vv2 = p.opc(VE, lambda e: e.engine_nop())
            p.wait(SE, S["ve"], vv2)
            p.drain(SE)
            vxn = p.opc(SE, lambda e: e.activation(xnb.full, xt[:, 0:D], AF.Identity,
                                                   bias=stat[:, 17:18], scale=stat[:, 16:17]))
            p.wait(VE, S["se"], vxn)
            p.opc(VE, lambda e: e.tensor_tensor(htm[:, 0:D], xnb.full, ca1.full, ALU.mult))
            vtt = p.opc(VE, lambda e: e.tensor_tensor(htm[:, 0:D], htm[:, 0:D], sa_.full, ALU.add))
            return vxn, vtt

        def tm_to_fm(src, dst_fm, k, nchunk=8, dst_stride=T):
            vv = p.opc(VE, lambda e: e.engine_nop())
            p.wait(PE, S["ve"], vv)
            for c in range(nchunk):
                p.opc(PE, lambda e, c=c: e.transpose(
                    pbt[:, 128 * c:128 * (c + 1)], src[:, 128 * c:128 * (c + 1)], ident[:, :]))
            p.wait(VE, S["pe"])
            for c in range(nchunk):
                p.opc(VE, lambda e, c=c: e.tensor_copy(
                    dst_fm[:, dst_stride * c + 128 * k: dst_stride * c + 128 * (k + 1)],
                    pbt[:, 128 * c:128 * (c + 1)]))

        def p_attn_consts(cst):
            p.opc(GE, lambda e: e.iota(io4.ap(), pattern=[[-32, 4]], base=0,
                                       channel_multiplier=1, allow_small_or_imprecise_dtypes=True))
            p.wait(VE, S["ge"])
            p.opc(VE, lambda e: e.tensor_scalar(smal[:, 0:4], io4[:, :], 0.0, None, ALU.is_ge, ALU.bypass))
            p.drain(VE)
            p.opc(VE, lambda e: e.tensor_scalar(smal[:, 4:8], io4[:, :], 31.5, None, ALU.is_le, ALU.bypass))
            p.drain(VE)
            p.opc(VE, lambda e: e.tensor_tensor(ones4[:, :], smal[:, 0:4], smal[:, 4:8], ALU.mult))
            p.drain(VE)
            v = p.dma(SY, smal[0:32, 8:9], lamq1)
            v = p.dma(SY, smal[0:32, 9:10], lamk1)
            v = p.dma(SY, smal[0:32, 10:11], lamq2)
            v = p.dma(SY, smal[0:32, 11:12], lamk2)
            sw = p.dma(SY, smal[0:1, 16:80], sublnw)
            p.wait(VE, S["LQ"], v)
            p.opc(VE, lambda e: e.tensor_tensor(smal[0:32, 12:13], smal[0:32, 8:9], smal[0:32, 9:10], ALU.mult))
            p.opc(VE, lambda e: e.tensor_tensor(smal[0:32, 13:14], smal[0:32, 10:11], smal[0:32, 11:12], ALU.mult))
            p.drain(VE)
            vv = p.opc(VE, lambda e: e.engine_nop())
            p.wait(GE, S["ve"], vv)
            p.opc(GE, lambda e: e.tensor_reduce(smal[0:1, 14:15], smal[0:32, 12:13], AX.C, ALU.add))
            vg = p.opc(GE, lambda e: e.tensor_reduce(smal[0:1, 15:16], smal[0:32, 13:14], AX.C, ALU.add))
            p.wait(SE, S["ge"], vg)
            p.drain(SE)
            p.opc(SE, lambda e: e.activation(smal[0:1, 14:15], smal[0:1, 14:15], AF.Exp))
            p.drain(SE)
            vs = p.opc(SE, lambda e: e.activation(smal[0:1, 15:16], smal[0:1, 15:16], AF.Exp))
            p.wait(VE, S["se"], vs)
            p.opc(VE, lambda e: e.tensor_tensor(smal[0:1, 14:15], smal[0:1, 14:15], smal[0:1, 15:16], ALU.subtract))
            p.drain(VE)
            p.opc(VE, lambda e: e.tensor_scalar(smal[0:1, 14:15], smal[0:1, 14:15],
                                                LAMBDA_INIT, -1.0, ALU.add, ALU.mult))
            p.drain(VE)
            p.wait(VE, S["LQ"], sw)
            p.opc(VE, lambda e: e.tensor_scalar(smal[0:1, 16:80], smal[0:1, 16:80],
                                                1.0 - LAMBDA_INIT, None, ALU.mult, ALU.bypass))
            p.drain(VE)
            vv2 = p.opc(VE, lambda e: e.engine_nop())
            p.wait(PE, S["ve"], vv2)
            vpe = p.opc(PE, lambda e: e.matmul(pC[:, 0:1], ones1f[0:1, :], smal[0:1, 14:15],
                                               start=True, stop=True))
            vpe = p.opc(PE, lambda e: e.matmul(pC[:, 64:128], ones1f[0:1, :], smal[0:1, 16:80],
                                               start=True, stop=True))
            p.wait(VE, S["pe"], vpe)
            p.opc(VE, lambda e: e.tensor_copy(cst["negl"][:, 0:1], pC[:, 0:1]))
            p.opc(VE, lambda e: e.tensor_copy(cst["w1m"][:, 0:64], pC[:, 64:128]))
            p.barrier()

        def p_qkv(b, h1fm, qfm, kfm, vab, qn, negkb):
            A16.mark(); A32.mark()
            wbuf = A16.alloc(8192)
            q2b = A16.alloc(512)
            kmx = A32.alloc(40)
            DINV = float(1.0 / math.sqrt(32.0))

            def proj_fm(dst, w_dram, scale, qn_dst):
                vw = p.dma(GE, wbuf.full.rearrange("p (r o) -> p r o", r=8),
                           w_dram.rearrange("(r p) o -> p r o", r=8))
                evac = {}
                sq_ev = {}
                for i, (c, g) in enumerate([(c, g) for c in range(8) for g in range(4)]):
                    bank = pb[i % 2]
                    p.wait(PE, S["LG"], vw)
                    if i >= 2:
                        p.wait(PE, S["se"], evac[i - 2])
                    for r in range(8):
                        vpe = p.opc(PE, lambda e, r=r, c=c, g=g, bank=bank: e.matmul(
                            bank[:, 0:512],
                            wbuf[:, 1024 * r + 128 * c: 1024 * r + 128 * (c + 1)],
                            h1fm[:, 2048 * r + 512 * g: 2048 * r + 512 * (g + 1)],
                            start=(r == 0), stop=(r == 7)))
                    p.wait(SE, S["pe"], vpe)
                    evac[i] = p.opc(SE, lambda e, c=c, g=g, bank=bank: e.activation(
                        dst[:, 2048 * c + 512 * g: 2048 * c + 512 * (g + 1)],
                        bank[:, 0:512], AF.Identity, scale=scale))
                    if qn_dst is not None:
                        sq_ev[i] = p.opc(SE, lambda e, c=c, g=g: e.activation(
                            q2b[:, 0:512],
                            dst[:, 2048 * c + 512 * g: 2048 * c + 512 * (g + 1)], AF.Square))
                        p.wait(PE, S["se"], sq_ev[i])
                        if qn_dst == "kmx":
                            p.wait(PE, S["ve"])
                        elif i >= 1:
                            p.wait(PE, S["se"])
                        vn = p.opc(PE, lambda e: e.matmul(
                            pC[0:4, 0:512], ones4[:, :], q2b[:, 0:512], start=True, stop=True))
                        if qn_dst == "kmx":
                            p.wait(VE, S["pe"], vn)
                            p.opc(VE, lambda e, i=i: e.tensor_reduce(
                                kmx[0:4, i:i + 1], pC[0:4, 0:512], AX.X, ALU.max))
                        else:
                            p.wait(SE, S["pe"], vn)
                            p.opc(SE, lambda e, c=c, g=g: e.activation(
                                qn_dst[0:4, 2048 * c + 512 * g: 2048 * c + 512 * (g + 1)],
                                pC[0:4, 0:512], AF.Sqrt))
                p.barrier()

            proj_fm(qfm.full, wqT, DINV, qn)
            proj_fm(kfm.full, wkT, 1.0, "kmx")
            p.drain(VE)
            p.opc(VE, lambda e: e.tensor_reduce(
                kmx[0:4, 32:40],
                kmx[0:4, 0:32].rearrange("p (c g) -> p c g", c=8),
                AX.X, ALU.max))
            p.drain(VE)
            vv = p.opc(VE, lambda e: e.engine_nop())
            p.wait(SE, S["ve"], vv)
            p.drain(SE)
            vs = p.opc(SE, lambda e: e.activation(kmx[0:4, 32:40], kmx[0:4, 32:40], AF.Sqrt))
            p.wait(VE, S["se"], vs)
            p.opc(VE, lambda e: e.tensor_scalar(kmx[0:4, 32:40], kmx[0:4, 32:40], -1.0, None, ALU.mult, ALU.bypass))
            p.drain(VE)
            vneg = p.opc(VE, lambda e: e.engine_nop())
            p.wait(SY, S["ve"], vneg)
            vst = p.store(negscr, kmx[0:4, 32:40])
            p.wait(SY, S["SQ"], vst)
            vrow = p.dma(SY, smal[0:1, 96:128], negscr.rearrange("r c -> c r"),
                         allow_slow_non_contiguous=True)
            p.wait(PE, S["LQ"], vrow)
            vpe = p.opc(PE, lambda e: e.matmul(pC[:, 128:160], ones1f[0:1, :], smal[0:1, 96:128],
                                               start=True, stop=True))
            p.wait(VE, S["pe"], vpe)
            p.opc(VE, lambda e: e.tensor_copy(negkb[:, 0:32], pC[:, 128:160]))
            vw = p.dma(GE, wbuf.full.rearrange("p (r o) -> p r o", r=8),
                       wvT.rearrange("(r p) o -> p r o", r=8))
            p.opc(VE, lambda e: e.memset(
                vab.full.rearrange("p (k h o) -> p k h o", k=16, h=16)[:, :, :, 64:65], 1.0))
            evac = {}
            for i, (kt, dg) in enumerate([(kt, dg) for kt in range(16) for dg in range(2)]):
                bank = pb[2 + i % 2]
                p.wait(PE, S["LG"], vw)
                if i >= 2:
                    p.wait(PE, S["ve"], evac[i - 2])
                for r in range(8):
                    vpe = p.opc(PE, lambda e, r=r, kt=kt, dg=dg, bank=bank: e.matmul(
                        bank[:, 0:512],
                        h1fm[:, 2048 * r + 128 * kt: 2048 * r + 128 * (kt + 1)],
                        wbuf[:, 1024 * r + 512 * dg: 1024 * r + 512 * (dg + 1)],
                        start=(r == 0), stop=(r == 7)))
                p.wait(VE, S["pe"], vpe)
                evac[i] = p.opc(VE, lambda e, kt=kt, dg=dg, bank=bank: e.tensor_copy(
                    vab.full.rearrange("p (k h o) -> p k h o", k=16, h=16)[:, kt, 8 * dg:8 * (dg + 1), 0:64],
                    bank[:, 0:512].rearrange("p (h o) -> p h o", h=8)))
            if DEBUG and b == 0:
                p.barrier()
                if "qfm" in dbg: p.store(dbg["qfm"], qfm.full)
                if "kfm" in dbg: p.store(dbg["kfm"], kfm.full)
                if "vab" in dbg: p.store(dbg["vab"], vab.full)
                if "qn" in dbg: p.store(dbg["qn"][0:4, :], qn[0:4, 0:8 * T])
                if "negkb" in dbg: p.store(dbg["negkb"], negkb.full)
            A16.release(); A32.release()
            p.barrier()

        def p_core(b, qfm, kfm, vab, qn, negkb, cst):
            A16.mark(); A32.mark()
            E2 = [A16.alloc(8192) for _ in range(2)]
            kpair = A16.alloc(2048)
            qpair = A16.alloc(2048)
            stage = A16.alloc(4096)
            dump = A16.alloc(64)
            o0sb = A32.alloc(280)
            comb = A32.alloc(200)
            vcols = A32.alloc(256)
            p.opc(VE, lambda e: e.memset(kpair[32:33, 0:2048], 1.0))
            p.opc(VE, lambda e: e.memset(kpair[96:97, 0:2048], 1.0))
            last_pair_use = 0
            last_pC_use = 0
            exp_slot = {0: 0, 1: 0}
            scount = 0
            st_done = 0
            for qg in range(4):
                for vh in range(16):
                    h2a, h2b = 2 * vh, 2 * vh + 1
                    ca_, ra = h2a // 4, h2a % 4
                    cb_, rb = h2b // 4, h2b % 4
                    p.wait(GE, S["pe"], last_pair_use)
                    vp = p.dma(GE, kpair[0:32, 0:2048], kfm.h[32 * ra:32 * ra + 32, kfm.off + 2048 * ca_: kfm.off + 2048 * (ca_ + 1)])
                    vp = p.dma(GE, kpair[64:96, 0:2048], kfm.h[32 * rb:32 * rb + 32, kfm.off + 2048 * cb_: kfm.off + 2048 * (cb_ + 1)])
                    vp = p.dma(GE, qpair[0:32, 0:2048], qfm.h[32 * ra:32 * ra + 32, qfm.off + 2048 * ca_: qfm.off + 2048 * (ca_ + 1)])
                    vp = p.dma(GE, qpair[64:96, 0:2048], qfm.h[32 * rb:32 * rb + 32, qfm.off + 2048 * cb_: qfm.off + 2048 * (cb_ + 1)])
                    vp = p.dma(GE, qpair[32:33, 0:2048], qn.h[ra:ra + 1, qn.off + 2048 * ca_: qn.off + 2048 * (ca_ + 1)])
                    vp = p.dma(GE, qpair[96:97, 0:2048], qn.h[rb:rb + 1, qn.off + 2048 * cb_: qn.off + 2048 * (cb_ + 1)])
                    p.wait(VE, S["LG"], vp)
                    p.opc(VE, lambda e, h2a=h2a: e.tensor_scalar(
                        qpair[32:33, 0:2048], qpair[32:33, 0:2048],
                        negkb[32:33, h2a:h2a + 1], None, ALU.mult, ALU.bypass))
                    p.opc(VE, lambda e, h2b=h2b: e.tensor_scalar(
                        qpair[96:97, 0:2048], qpair[96:97, 0:2048],
                        negkb[96:97, h2b:h2b + 1], None, ALU.mult, ALU.bypass))
                    p.drain(VE)
                    vqp = p.opc(VE, lambda e: e.engine_nop())
                    for sh in (0, 1):
                        E = E2[sh]
                        jbase = 0
                        for si, sw_ in enumerate((3, 3, 3, 3, 3, 1)):
                            ps = (pA, pB)[scount % 2]
                            p.wait(PE, S["se"], exp_slot[scount % 2])
                            p.wait(PE, S["ve"], vqp)
                            p.wait(PE, S["LG"], vp)
                            for jj in range(sw_):
                                j = jbase + jj
                                vpe = p.opc(PE, lambda e, jj=jj, j=j, sh=sh, ps=ps, qg=qg: e.matmul(
                                    ps[:, 512 * jj:512 * (jj + 1)],
                                    kpair[64 * sh:64 * sh + 33, 128 * j:128 * (j + 1)],
                                    qpair[64 * sh:64 * sh + 33, 512 * qg:512 * (qg + 1)],
                                    start=True, stop=True))
                            last_pair_use = vpe
                            p.wait(SE, S["pe"], vpe)
                            exp_slot[scount % 2] = p.opc(SE, lambda e, ps=ps, jbase=jbase, sw_=sw_, E=E: e.activation(
                                E[:, 512 * jbase:512 * (jbase + sw_)], ps[:, 0:512 * sw_], AF.Exp))
                            jbase += sw_
                            scount += 1
                    for sh in (0, 1):
                        E = E2[sh]
                        p.wait(PE, S["se"], exp_slot[0])
                        p.wait(PE, S["se"], exp_slot[1])
                        p.wait(PE, S["ve"], last_pC_use)
                        for qs in range(4):
                            for j in range(16):
                                vpe = p.opc(PE, lambda e, j=j, qs=qs, vh=vh, E=E: e.matmul(
                                    pC[:, 128 * qs:128 * qs + 65],
                                    E[:, 512 * j + 128 * qs:512 * j + 128 * (qs + 1)],
                                    vab[:, 1040 * j + 65 * vh:1040 * j + 65 * (vh + 1)],
                                    start=(j == 0), stop=(j == 15), skip_group_check=True))
                        p.wait(VE, S["pe"], vpe)
                        if sh == 0:
                            for qs in range(4):
                                p.opc(VE, lambda e, qs=qs: e.tensor_copy(
                                    o0sb[:, 70 * qs:70 * qs + 65], pC[:, 128 * qs:128 * qs + 65]))
                            vo0 = p.opc(VE, lambda e: e.engine_nop())
                            last_pC_use = vo0
                        else:
                            for qs in range(4):
                                p.opc(VE, lambda e, qs=qs: e.reciprocal(
                                    comb[:, 128 + 2 * qs:129 + 2 * qs], o0sb[:, 70 * qs + 64:70 * qs + 65]))
                                p.opc(VE, lambda e, qs=qs: e.reciprocal(
                                    comb[:, 129 + 2 * qs:130 + 2 * qs], pC[:, 128 * qs + 64:128 * qs + 65]))
                            p.drain(VE)
                            for qs in range(4):
                                p.opc(VE, lambda e, qs=qs: e.tensor_scalar(
                                    comb[:, 0:64], o0sb[:, 70 * qs:70 * qs + 64],
                                    comb[:, 128 + 2 * qs:129 + 2 * qs], None, ALU.mult, ALU.bypass))
                                p.opc(VE, lambda e, qs=qs: e.tensor_scalar(
                                    comb[:, 64:128], pC[:, 128 * qs:128 * qs + 64],
                                    comb[:, 129 + 2 * qs:130 + 2 * qs], cst["negl"][:, 0:1], ALU.mult, ALU.mult))
                                vA = p.opc(VE, lambda e, qs=qs, vh=vh: e.tensor_tensor(
                                    stage[:, 1024 * qs + 64 * vh:1024 * qs + 64 * (vh + 1)],
                                    comb[:, 0:64], comb[:, 64:128], ALU.add))
                                p.wait(SE, S["ve"], vA)
                                p.opc(SE, lambda e, qs=qs, vh=vh: e.activation(
                                    dump[:, 0:64],
                                    stage[:, 1024 * qs + 64 * vh:1024 * qs + 64 * (vh + 1)],
                                    AF.Square, accum_out=vcols[:, 16 * qs + vh:16 * qs + vh + 1]))
                            vcomb = p.opc(VE, lambda e: e.engine_nop())
                            last_pC_use = vcomb
                p.wait(VE, S["se"])
                p.opc(VE, lambda e: e.tensor_scalar(
                    vcols[:, 64:128], vcols[:, 0:64], 1.0 / 64.0, 1e-5, ALU.mult, ALU.add))
                p.drain(VE)
                vu = p.opc(VE, lambda e: e.engine_nop())
                p.wait(SE, S["ve"], vu)
                p.drain(SE)
                vsq = p.opc(SE, lambda e: e.activation(
                    vcols[:, 192:256], vcols[:, 64:128], AF.Sqrt))
                p.wait(VE, S["se"], vsq)
                vrc = p.opc(VE, lambda e: e.reciprocal(vcols[:, 128:192], vcols[:, 192:256]))
                p.drain(VE)
                for qs in range(4):
                    for vh in range(16):
                        idx = 16 * qs + vh
                        p.opc(VE, lambda e, qs=qs, vh=vh, idx=idx: e.scalar_tensor_tensor(
                            stage[:, 1024 * qs + 64 * vh:1024 * qs + 64 * (vh + 1)],
                            stage[:, 1024 * qs + 64 * vh:1024 * qs + 64 * (vh + 1)],
                            vcols[:, 128 + idx:128 + idx + 1],
                            cst["w1m"][:, 0:64], ALU.mult, ALU.mult))
                vnrm = p.opc(VE, lambda e: e.engine_nop())
                p.wait(SY, S["ve"], vnrm)
                for qs in range(4):
                    st_done = p.store(attn_sp[b, 512 * qg + 128 * qs:512 * qg + 128 * (qs + 1), :],
                                      stage[:, 1024 * qs:1024 * (qs + 1)])
                p.wait(VE, S["SQ"], st_done)
            A16.release(); A32.release()
            p.barrier()

        def p_attn_out(b, rep):
            A16.mark(); A32.mark()
            wo = A16.alloc(8192)
            afm = [A16.alloc(1024) for _ in range(2)]
            h2tm = A16.alloc(1024)
            h2i = A16.alloc(1024)
            hqs = A16.alloc(1024)
            xnb2 = A16.alloc(1024)
            xt = [A32.alloc(1024) for _ in range(2)]
            xw = [A32.alloc(1024) for _ in range(2)]
            stat = [A32.alloc(24) for _ in range(2)]
            q1f = A32.alloc(1024)
            vw = p.dma(GE, wo.full.rearrange("p (r o) -> p r o", r=8),
                       woT.rearrange("(r p) o -> p r o", r=8))
            last_se = {}
            last_tr = {}
            st_last = {}
            ao_ev = {}
            for kt in range(NT):
                pr = kt % 2
                if kt >= 1:
                    p.wait(VE, S["SQ"], st_last[kt - 1])
                if kt >= 2:
                    p.wait(SY, S["pe"], last_tr[kt - 2])
                va_ = None
                for c in range(8):
                    va_ = p.dma(SY, afm[pr][:, 128 * c:128 * (c + 1)],
                                attn_sp[b, 128 * kt:128 * (kt + 1), 128 * c:128 * (c + 1)],
                                transpose=True)
                if kt >= 2 and (kt - 2) in last_se:
                    p.wait(SY, S["se"], last_se[kt - 2])
                vx = p.dma(SY, xt[pr].full, x_in[b, 128 * kt:128 * (kt + 1), :])
                if DEBUG and b == 0 and kt == 0 and "afm0" in dbg:
                    p.wait(SY, S["LQ"], va_)
                    p.store(dbg["afm0"], afm[pr].full)
                p.wait(PE, S["LQ"], va_)
                p.wait(PE, S["LG"], vw)
                for dg in range(2):
                    bank = pb[dg]
                    if ao_ev.get(dg):
                        p.wait(PE, S["ve"], ao_ev[dg])
                    for r in range(8):
                        vpe = p.opc(PE, lambda e, r=r, dg=dg, pr=pr, bank=bank: e.matmul(
                            bank[:, 0:512],
                            afm[pr][:, 128 * r:128 * (r + 1)],
                            wo[:, 1024 * r + 512 * dg:1024 * r + 512 * (dg + 1)],
                            start=(r == 0), stop=(r == 7)))
                last_tr[kt] = vpe
                p.wait(VE, S["pe"], last_tr[kt])
                p.wait(VE, S["LQ"], vx)
                if DEBUG and b == 0 and kt == 0 and "op0" in dbg:
                    p.opc(VE, lambda e: e.tensor_copy(q1f[:, 0:512], pb[0][:, 0:512]))
                    p.opc(VE, lambda e: e.tensor_copy(q1f[:, 512:1024], pb[1][:, 0:512]))
                    vtap = p.opc(VE, lambda e: e.engine_nop())
                    p.wait(SY, S["ve"], vtap)
                    p.store(dbg["op0"], q1f.full)
                for dg in range(2):
                    bank = pb[dg]
                    ao_ev[dg] = p.opc(VE, lambda e, dg=dg, pr=pr, bank=bank: e.tensor_tensor(
                        smal[:, 0:512], bank[:, 0:512],
                        rep[(b, "ga")][:, 512 * dg:512 * (dg + 1)], ALU.mult))
                    p.opc(VE, lambda e, dg=dg, pr=pr: e.tensor_tensor(
                        xw[pr][:, 512 * dg:512 * (dg + 1)], smal[:, 0:512],
                        xt[pr][:, 512 * dg:512 * (dg + 1)], ALU.add))
                vxw = p.opc(VE, lambda e: e.engine_nop())
                p.wait(SY, S["ve"], vxw)
                p.store(xres[b, 128 * kt:128 * (kt + 1), :], xw[pr].full)
                if DEBUG and b == 0 and kt == 0 and "rep_ga" in dbg:
                    p.store(dbg["rep_ga"], rep[(b, "ga")].full)
                vxn, _ = ln_h(xw[pr], stat[pr], h2tm, rep[(b, "cb1")], rep[(b, "sb")], xnb2)
                last_se[kt] = vxn
                p.opc(VE, lambda e, pr=pr: e.tensor_reduce(stat[pr][:, 20:21], h2tm[:, 0:D], AX.X, ALU.max,
                                                    apply_absolute_value=True))
                p.drain(VE)
                p.opc(VE, lambda e, pr=pr: e.tensor_scalar(stat[pr][:, 21:22], stat[pr][:, 20:21],
                                                    1e-5, None, ALU.max, ALU.bypass))
                p.drain(VE)
                p.opc(VE, lambda e, pr=pr: e.reciprocal(stat[pr][:, 22:23], stat[pr][:, 21:22]))
                p.drain(VE)
                p.opc(VE, lambda e, pr=pr: e.tensor_scalar(stat[pr][:, 23:24], stat[pr][:, 22:23],
                                                    127.0, None, ALU.mult, ALU.bypass))
                p.drain(VE)
                p.opc(VE, lambda e, pr=pr: e.tensor_scalar(q1f.full, h2tm[:, 0:D],
                                                    stat[pr][:, 23:24], MAGIC, ALU.mult, ALU.add))
                p.opc(VE, lambda e: e.tensor_scalar(q1f.full, q1f.full, -MAGIC, 127.0, ALU.add, ALU.min))
                vq = p.opc(VE, lambda e: e.tensor_scalar(h2i.full, q1f.full, -128.0, None, ALU.max, ALU.bypass))
                p.wait(PE, S["ve"], vq)
                for c in range(8):
                    vt = p.opc(PE, lambda e, c=c: e.transpose(
                        pbt[:, 128 * c:128 * (c + 1)], h2i[:, 128 * c:128 * (c + 1)], ident[:, :]))
                p.wait(VE, S["pe"], vt)
                vh2 = p.opc(VE, lambda e: e.tensor_copy(hqs.full, pbt[:, 0:1024]))
                p.wait(SY, S["ve"], vh2)
                p.store(h2q_sp[b, kt], hqs.full)
                st_last[kt] = p.store(rmq_sp[b, kt], stat[pr][:, 21:22])
            A16.release(); A32.release()
            p.barrier()

        def load_quant_bitnet(wdst, w_dram, nr, nelem, mcol):
            ncols = nelem // 128
            A32.mark()
            q1s = A32.alloc(512)
            vw = p.dma(GE, wdst.full.rearrange("p (r o) -> p r o", r=nr),
                       w_dram.rearrange("(r p) o -> p r o", r=nr))
            p.wait(VE, S["LG"], vw)
            p.opc(VE, lambda e: e.memset(mcol[:, 0:1], 0.0))
            p.drain(VE)
            for s0 in range(0, ncols, 512):
                w_ = min(512, ncols - s0)
                p.opc(VE, lambda e, s0=s0, w_=w_: e.tensor_reduce(
                    smal[:, 8:9], wdst[:, s0:s0 + w_], AX.X, ALU.add, apply_absolute_value=True))
                p.drain(VE)
                p.opc(VE, lambda e: e.tensor_tensor(mcol[:, 0:1], mcol[:, 0:1], smal[:, 8:9], ALU.add))
                p.drain(VE)
            vv = p.opc(VE, lambda e: e.engine_nop())
            p.wait(GE, S["ve"], vv)
            vg = p.opc(GE, lambda e: e.tensor_reduce(smal[0:1, 12:13], mcol[:, 0:1], AX.C, ALU.add))
            p.wait(VE, S["ge"], vg)
            p.opc(VE, lambda e: e.tensor_scalar(smal[0:1, 13:14], smal[0:1, 12:13],
                                                1.0 / nelem, 1e-5, ALU.mult, ALU.max))
            p.drain(VE)
            p.opc(VE, lambda e: e.reciprocal(smal[0:1, 14:15], smal[0:1, 13:14]))
            p.drain(VE)
            p.opc(VE, lambda e: e.tensor_scalar(smal[0:1, 15:16], smal[0:1, 13:14],
                                                1.0 / 127.0, None, ALU.mult, ALU.bypass))
            p.drain(VE)
            vv2 = p.opc(VE, lambda e: e.engine_nop())
            p.wait(PE, S["ve"], vv2)
            vpe = p.opc(PE, lambda e: e.matmul(pC[:, 0:2], ones1f[0:1, :], smal[0:1, 14:16],
                                               start=True, stop=True))
            p.wait(VE, S["pe"], vpe)
            p.opc(VE, lambda e: e.tensor_copy(mcol[:, 1:3], pC[:, 0:2]))
            p.drain(VE)
            for s0 in range(0, ncols, 512):
                w_ = min(512, ncols - s0)
                p.opc(VE, lambda e, s0=s0, w_=w_, q1s=q1s: e.tensor_scalar(
                    q1s[:, 0:w_], wdst[:, s0:s0 + w_],
                    mcol[:, 1:2], MAGIC, ALU.mult, ALU.add))
                p.opc(VE, lambda e, s0=s0, w_=w_, q1s=q1s: e.tensor_scalar(
                    q1s[:, 0:w_], q1s[:, 0:w_], -MAGIC, 1.0, ALU.add, ALU.min))
                p.opc(VE, lambda e, s0=s0, w_=w_, q1s=q1s: e.tensor_scalar(
                    wdst[:, s0:s0 + w_], q1s[:, 0:w_], -1.0, None, ALU.max, ALU.bypass))
            vq = p.opc(VE, lambda e: e.engine_nop())
            A32.release()
            return vq

        def p_ffn0_A():
            A16.mark(); A32.mark()
            wgi = A16.alloc(32768)
            wui = A16.alloc(32768)
            h2qt = [A16.alloc(1024) for _ in range(2)]
            gt = A16.alloc(4096)
            ut = A16.alloc(4096)
            tt = A16.alloc(4096)
            mcolg = A32.alloc(4)
            mcolu = A32.alloc(4)
            rmc = [A32.alloc(4) for _ in range(2)]
            last_vt = 0
            load_quant_bitnet(wgi, wgT0, 8, 4096 * 1024, mcolg)
            load_quant_bitnet(wui, wuT0, 8, 4096 * 1024, mcolu)
            p.barrier()
            last_tt = {}
            bank_ev = {}
            for i in range(2 * NT):
                b, kt = divmod(i, NT)
                pr = i % 2
                vh_ = p.dma(SY, h2qt[pr].full, h2q_sp[b, kt])
                vr = p.dma(SY, rmc[pr][:, 0:1], rmq_sp[b, kt])
                p.wait(VE, S["LQ"], vr)
                p.opc(VE, lambda e, pr=pr: e.tensor_scalar(
                    rmc[pr][:, 1:2], rmc[pr][:, 0:1], mcolg[:, 2:3], None, ALU.mult, ALU.bypass))
                p.opc(VE, lambda e, pr=pr: e.tensor_scalar(
                    rmc[pr][:, 2:3], rmc[pr][:, 0:1], mcolu[:, 2:3], None, ALU.mult, ALU.bypass))
                p.drain(VE)
                vsc = p.opc(VE, lambda e: e.engine_nop())
                if i >= 1:
                    p.wait(SE, S["ve"], last_vt)
                p.wait(PE, S["LQ"], vh_)
                for which, wsrc, dstb, col in ((0, wgi, gt, 1), (1, wui, ut, 2)):
                    for ig in range(8):
                        gidx = which * 8 + ig
                        bank = pb[gidx % 4]
                        if bank_ev.get(gidx % 4):
                            p.wait(PE, S["se"], bank_ev[gidx % 4])
                        for r in range(8):
                            vpe = p.opc(PE, lambda e, r=r, ig=ig, pr=pr, wsrc=wsrc, bank=bank: e.matmul(
                                bank[:, 0:512],
                                h2qt[pr][:, 128 * r:128 * (r + 1)],
                                wsrc[:, 4096 * r + 512 * ig:4096 * r + 512 * (ig + 1)],
                                start=(r == 0), stop=(r == 7)))
                        p.wait(SE, S["pe"], vpe)
                        p.wait(SE, S["ve"], vsc)
                        if which == 0:
                            bank_ev[gidx % 4] = p.opc(SE, lambda e, ig=ig, pr=pr, bank=bank, col=col: e.activation(
                                gt[:, 512 * ig:512 * (ig + 1)], bank[:, 0:512],
                                AF.Silu, scale=rmc[pr][:, col:col + 1]))
                        else:
                            bank_ev[gidx % 4] = p.opc(SE, lambda e, ig=ig, pr=pr, bank=bank, col=col: e.activation(
                                ut[:, 512 * ig:512 * (ig + 1)], bank[:, 0:512],
                                AF.Identity, scale=rmc[pr][:, col:col + 1]))
                p.drain(SE)
                vse = p.opc(SE, lambda e: e.nop())
                p.wait(VE, S["se"], vse)
                if i >= 1:
                    p.wait(VE, S["SQ"], last_tt[i - 1])
                vt = p.opc(VE, lambda e: e.tensor_tensor(tt.full, gt.full, ut.full, ALU.mult))
                last_vt = vt
                p.wait(SY, S["ve"], vt)
                last_tt[i] = p.store(t0_sp[b, kt], tt.full)
            A16.release(); A32.release()
            p.barrier()

        def p_ffn0_B(rep, repL1, h3fm):
            A16.mark(); A32.mark()
            wdi = A16.alloc(32768)
            ttl = [A16.alloc(4096) for _ in range(2)]
            ti = A16.alloc(4096)
            tif = A16.alloc(4096)
            h3tm = A16.alloc(1024)
            xnb3 = A16.alloc(1024)
            mcold = A32.alloc(4)
            q1l = A32.alloc(1024)
            xt = [A32.alloc(1024) for _ in range(2)]
            xw = [A32.alloc(1024) for _ in range(2)]
            stat = [A32.alloc(24) for _ in range(2)]
            load_quant_bitnet(wdi, wdT0, 32, 4096 * 1024, mcold)
            p.barrier()
            last_se = {}
            stx_last = {}
            dn_ev = {}
            for i in range(2 * NT):
                b, kt = divmod(i, NT)
                pr = i % 2
                if i >= 2 and (i - 2) in last_se:
                    p.wait(SY, S["se"], last_se[i - 2])
                if i >= 2:
                    p.wait(VE, S["SQ"], stx_last[i - 2])
                vtl = p.dma(SY, ttl[pr].full, t0_sp[b, kt])
                vx = p.dma(SY, xt[pr].full, xres[b, 128 * kt:128 * (kt + 1), :])
                p.wait(VE, S["LQ"], vtl)
                p.opc(VE, lambda e, pr=pr: e.tensor_reduce(
                    stat[pr][:, 20:21], ttl[pr].full, AX.X, ALU.max, apply_absolute_value=True))
                p.drain(VE)
                p.opc(VE, lambda e, pr=pr: e.tensor_scalar(
                    stat[pr][:, 21:22], stat[pr][:, 20:21], 1e-5, None, ALU.max, ALU.bypass))
                p.drain(VE)
                p.opc(VE, lambda e, pr=pr: e.reciprocal(stat[pr][:, 22:23], stat[pr][:, 21:22]))
                p.drain(VE)
                p.opc(VE, lambda e, pr=pr: e.tensor_scalar(
                    stat[pr][:, 23:24], stat[pr][:, 22:23], 127.0, None, ALU.mult, ALU.bypass))
                p.opc(VE, lambda e, pr=pr: e.tensor_scalar(
                    stat[pr][:, 19:20], stat[pr][:, 21:22], mcold[:, 2:3], None, ALU.mult, ALU.bypass))
                p.drain(VE)
                for hf in range(4):
                    p.opc(VE, lambda e, hf=hf, pr=pr: e.tensor_scalar(
                        q1l.full, ttl[pr][:, 1024 * hf:1024 * (hf + 1)],
                        stat[pr][:, 23:24], MAGIC, ALU.mult, ALU.add))
                    p.opc(VE, lambda e, hf=hf: e.tensor_scalar(
                        q1l.full, q1l.full, -MAGIC, 127.0, ALU.add, ALU.min))
                    p.opc(VE, lambda e, hf=hf: e.tensor_scalar(
                        ti[:, 1024 * hf:1024 * (hf + 1)], q1l.full, -128.0, None, ALU.max, ALU.bypass))
                vq = p.opc(VE, lambda e: e.engine_nop())
                p.wait(PE, S["ve"], vq)
                tr_ev = 0
                for grp in range(4):
                    if grp > 0:
                        p.wait(PE, S["ve"], tr_ev)
                    for c8 in range(8):
                        c = 8 * grp + c8
                        vt = p.opc(PE, lambda e, c=c, c8=c8: e.transpose(
                            pbt[:, 128 * c8:128 * (c8 + 1)], ti[:, 128 * c:128 * (c + 1)], ident[:, :]))
                    p.wait(VE, S["pe"], vt)
                    tr_ev = p.opc(VE, lambda e, grp=grp: e.tensor_copy(
                        tif[:, 1024 * grp:1024 * (grp + 1)], pbt[:, 0:1024]))
                vtf = p.opc(VE, lambda e: e.engine_nop())
                p.wait(PE, S["ve"], vtf)
                for og in range(2):
                    bank = pb[4 + og]
                    if dn_ev.get(og):
                        p.wait(PE, S["ve"], dn_ev[og])
                    for r in range(32):
                        vpe = p.opc(PE, lambda e, r=r, og=og, bank=bank: e.matmul(
                            bank[:, 0:512],
                            tif[:, 128 * r:128 * (r + 1)],
                            wdi[:, 1024 * r + 512 * og:1024 * r + 512 * (og + 1)],
                            start=(r == 0), stop=(r == 31)))
                p.wait(VE, S["pe"], vpe)
                p.wait(VE, S["LQ"], vx)
                for og in range(2):
                    bank = pb[4 + og]
                    p.opc(VE, lambda e, og=og, pr=pr, bank=bank: e.tensor_scalar(
                        smal[:, 0:512], bank[:, 0:512], stat[pr][:, 19:20], None, ALU.mult, ALU.bypass))
                    p.opc(VE, lambda e, og=og, bank=bank, b=b: e.tensor_tensor(
                        smal[:, 0:512], smal[:, 0:512], rep[(b, "gb")][:, 512 * og:512 * (og + 1)], ALU.mult))
                    p.opc(VE, lambda e, og=og, pr=pr: e.tensor_tensor(
                        xw[pr][:, 512 * og:512 * (og + 1)], smal[:, 0:512],
                        xt[pr][:, 512 * og:512 * (og + 1)], ALU.add))
                vxw = p.opc(VE, lambda e: e.engine_nop())
                p.wait(SY, S["ve"], vxw)
                stx_last[i] = p.store(xres[b, 128 * kt:128 * (kt + 1), :], xw[pr].full)
                vxn, _ = ln_h(xw[pr], stat[pr], h3tm, repL1[(b, "ca1")], repL1[(b, "sa")], xnb3)
                last_se[i] = vxn
                tm_to_fm(h3tm.full, h3fm[b].full, kt)
            A16.release(); A32.release()
            p.barrier()

        def p_l1_attn(repL1, h3fm, gacc):
            A16.mark(); A32.mark()
            xc = A16.alloc(16 * D)
            proj = A16.alloc(16 * 320)
            dtm = A16.alloc(1024)
            atm = A16.alloc(1024)
            btm = A16.alloc(1024)
            ctm = A16.alloc(1024)
            dsm = A16.alloc(2048)
            bsm = A16.alloc(2048)
            csm = A16.alloc(2048)
            abar = A16.alloc(2048)
            bbar = A16.alloc(2048)
            ssc1 = A16.alloc(2048)
            ssc2 = A16.alloc(2048)
            hsm = ssc1
            ysm = ssc2
            stm = A16.alloc(1024)
            h4tm = A16.alloc(1024)
            xnb4 = A16.alloc(1024)
            h4s = A16.alloc(1024)
            cw = A32.alloc(32)
            cvb = A32.alloc(8)
            ipw = A16.alloc(8 * 320)
            ipbrep = A32.alloc(320)
            woutb = A16.alloc(1024)
            opbrep = A16.alloc(1024)
            acol = A32.alloc(8)
            xt = [A32.alloc(1024) for _ in range(2)]
            xw = [A32.alloc(1024) for _ in range(2)]
            stat = [A32.alloc(24) for _ in range(2)]
            v1 = p.dma(SY, cw.full, convw)
            v2 = p.dma(SY, cvb.full.rearrange("p (q c) -> p q c", q=1),
                       convb.rearrange("q (c p) -> p q c", c=8), allow_slow_non_contiguous=True)
            v3 = p.dma(GE, ipw.full.rearrange("p (r o) -> p r o", r=8),
                       inprojT.rearrange("(r p) o -> p r o", r=8))
            v4 = p.dma(GE, woutb[0:64, 0:1024], outprojT)
            v5 = p.dma(SY, smal[0:1, 20:340], inprojb)
            valog = p.dma(SY, smal[0:64, 344:345], alog)
            p.wait(PE, S["LQ"], v5)
            vpe = p.opc(PE, lambda e: e.matmul(pC[:, 0:320], ones1f[0:1, :], smal[0:1, 20:340],
                                               start=True, stop=True))
            p.wait(VE, S["pe"], vpe)
            p.opc(VE, lambda e: e.tensor_copy(ipbrep.full, pC[:, 0:320]))
            vop = p.dma(GE, row0[0:1, 0:1024], outprojb)
            p.wait(PE, S["LG"], vop)
            p.wait(PE, S["ve"])
            bcast_row(row0, D, opbrep.full)
            p.wait(SE, S["LQ"], valog)
            p.drain(SE)
            vex = p.opc(SE, lambda e: e.activation(acol[0:64, 0:1], smal[0:64, 344:345], AF.Exp))
            p.wait(VE, S["se"], vex)
            p.opc(VE, lambda e: e.tensor_scalar(acol[0:64, 0:1], acol[0:64, 0:1], -1.0, None, ALU.mult, ALU.bypass))
            p.drain(VE)
            p.opc(VE, lambda e: e.tensor_scalar(acol[0:64, 3:4], acol[0:64, 0:1], -1.0, None, ALU.mult, ALU.bypass))
            p.drain(VE)
            p.opc(VE, lambda e: e.tensor_scalar(acol[0:64, 3:4], acol[0:64, 3:4], 1e-4, None, ALU.is_lt, ALU.bypass))
            p.drain(VE)
            p.opc(VE, lambda e: e.tensor_scalar(acol[0:64, 4:5], acol[0:64, 3:4], -1.0, 1.0, ALU.mult, ALU.add))
            p.drain(VE)
            p.opc(VE, lambda e: e.tensor_scalar(acol[0:64, 1:2], acol[0:64, 0:1], acol[0:64, 4:5], None, ALU.mult, ALU.bypass))
            p.drain(VE)
            p.opc(VE, lambda e: e.tensor_tensor(acol[0:64, 1:2], acol[0:64, 1:2], acol[0:64, 3:4], ALU.add))
            p.drain(VE)
            p.opc(VE, lambda e: e.reciprocal(acol[0:64, 2:3], acol[0:64, 1:2]))
            p.drain(VE)
            p.barrier()
            for b in range(NB):
                hb = h3fm[b]
                p.wait(VE, S["LQ"], v2)
                for c in range(8):
                    p.opc(VE, lambda e, c=c, hb=hb: e.tensor_scalar(
                        xc[:, 2048 * c: 2048 * c + 2047], hb[:, 2048 * c: 2048 * c + 2047],
                        cw[:, 4 * c + 1:4 * c + 2], cvb[:, c:c + 1], ALU.mult, ALU.add))
                    p.opc(VE, lambda e, c=c, hb=hb: e.scalar_tensor_tensor(
                        xc[:, 2048 * c: 2048 * c + 2047], hb[:, 2048 * c + 1: 2048 * c + 2048],
                        cw[:, 4 * c + 2:4 * c + 3], xc[:, 2048 * c: 2048 * c + 2047], ALU.mult, ALU.add))
                    p.opc(VE, lambda e, c=c, hb=hb: e.scalar_tensor_tensor(
                        xc[:, 2048 * c + 1: 2048 * c + 2047], hb[:, 2048 * c: 2048 * c + 2046],
                        cw[:, 4 * c:4 * c + 1], xc[:, 2048 * c + 1: 2048 * c + 2047], ALU.mult, ALU.add))
                    p.opc(VE, lambda e, c=c, hb=hb: e.scalar_tensor_tensor(
                        xc[:, 2048 * c: 2048 * c + 2046], hb[:, 2048 * c + 2: 2048 * c + 2048],
                        cw[:, 4 * c + 3:4 * c + 4], xc[:, 2048 * c: 2048 * c + 2046], ALU.mult, ALU.add))
                    p.opc(VE, lambda e, c=c: e.memset(xc[:, 2048 * c + 2047: 2048 * c + 2048], 0.0))
                vconv = p.opc(VE, lambda e: e.engine_nop())
                p.wait(PE, S["ve"], vconv)
                p.wait(PE, S["LG"], v3)
                for kt in range(NT):
                    bank = pb[kt % 2]
                    p.wait(PE, S["ve"])
                    for r in range(8):
                        vpe = p.opc(PE, lambda e, r=r, kt=kt, bank=bank: e.matmul(
                            bank[:, 0:320],
                            xc[:, 2048 * r + 128 * kt: 2048 * r + 128 * (kt + 1)],
                            ipw[:, 320 * r: 320 * (r + 1)],
                            start=(r == 0), stop=(r == 7)))
                    p.wait(VE, S["pe"], vpe)
                    p.opc(VE, lambda e, kt=kt, bank=bank: e.tensor_tensor(
                        proj[:, 320 * kt:320 * (kt + 1)], bank[:, 0:320], ipbrep.full, ALU.add))
                vproj = p.opc(VE, lambda e: e.engine_nop())
                prv = proj.full.rearrange("p (k o) -> p k o", k=16)
                p.opc(VE, lambda e: e.tensor_scalar(atm.full, prv[:, :, 0:64], -1.0, None, ALU.mult, ALU.bypass))
                p.drain(VE)
                p.opc(VE, lambda e: e.tensor_tensor(stm.full, atm.full.rearrange("p (k o) -> p k o", k=16), prv[:, :, 0:64], ALU.max))
                p.drain(VE)
                vabs = p.opc(VE, lambda e: e.engine_nop())
                p.wait(SE, S["ve"], vabs)
                p.drain(SE)
                p.opc(SE, lambda e: e.activation(atm.full, stm.full, AF.Exp, scale=-1.0))
                p.drain(SE)
                vln = p.opc(SE, lambda e: e.activation(stm.full, atm.full, AF.Ln, bias=1.0))
                p.wait(VE, S["se"], vln)
                p.opc(VE, lambda e: e.tensor_scalar(atm.full, prv[:, :, 0:64], 0.0, None, ALU.max, ALU.bypass))
                p.drain(VE)
                p.opc(VE, lambda e: e.tensor_tensor(dtm.full, atm.full, stm.full, ALU.add))
                p.opc(VE, lambda e: e.tensor_tensor(
                    btm.full, prv[:, :, 64:192].rearrange("p k (o two) -> p k o two", two=2)[:, :, :, 0],
                    prv[:, :, 64:192].rearrange("p k (o two) -> p k o two", two=2)[:, :, :, 1], ALU.add))
                p.opc(VE, lambda e: e.tensor_scalar(btm.full, btm.full, 0.5, None, ALU.mult, ALU.bypass))
                p.opc(VE, lambda e: e.tensor_tensor(
                    ctm.full, prv[:, :, 192:320].rearrange("p k (o two) -> p k o two", two=2)[:, :, :, 0],
                    prv[:, :, 192:320].rearrange("p k (o two) -> p k o two", two=2)[:, :, :, 1], ALU.add))
                p.opc(VE, lambda e: e.tensor_scalar(ctm.full, ctm.full, 0.5, None, ALU.mult, ALU.bypass))
                p.drain(VE)
                vtmq = p.opc(VE, lambda e: e.engine_nop())
                p.wait(PE, S["ve"], vtmq)
                for qi, (src, dst) in enumerate(((dtm, dsm), (btm, bsm), (ctm, csm))):
                    for kt in range(NT):
                        slot = kt % 8
                        if kt % 8 == 0:
                            p.wait(PE, S["ve"])
                        vt = p.opc(PE, lambda e, kt=kt, slot=slot, src=src: e.transpose(
                            pbt[0:64, 128 * slot:128 * (slot + 1)],
                            src[:, 64 * kt:64 * (kt + 1)], ident[:, :]))
                        if kt % 8 == 7:
                            p.wait(VE, S["pe"], vt)
                            g8 = kt // 8
                            p.opc(VE, lambda e, g8=g8, dst=dst: e.tensor_copy(
                                dst[0:64, 1024 * g8:1024 * (g8 + 1)], pbt[0:64, 0:1024]))
                vsm = p.opc(VE, lambda e: e.engine_nop())
                p.wait(SE, S["ve"], vsm)
                p.drain(SE)
                vab_ = p.opc(SE, lambda e: e.activation(abar[0:64, 0:2048], dsm[0:64, 0:2048],
                                                        AF.Exp, scale=acol[0:64, 0:1]))
                p.wait(VE, S["se"], vab_)
                p.opc(VE, lambda e: e.tensor_scalar(dsm[0:64, 0:2048], dsm[0:64, 0:2048],
                                                    acol[0:64, 0:1], None, ALU.mult, ALU.bypass))
                p.drain(VE)
                p.opc(VE, lambda e: e.tensor_scalar(ssc1[0:64, 0:2048], abar[0:64, 0:2048],
                                                    -1.0, acol[0:64, 2:3], ALU.add, ALU.mult))
                p.opc(VE, lambda e: e.tensor_scalar(ssc2[0:64, 0:2048], dsm[0:64, 0:2048],
                                                    1.0 / 6.0, 0.5, ALU.mult, ALU.add))
                p.drain(VE)
                p.opc(VE, lambda e: e.tensor_tensor(ssc2[0:64, 0:2048], ssc2[0:64, 0:2048],
                                                    dsm[0:64, 0:2048], ALU.mult))
                p.opc(VE, lambda e: e.tensor_scalar(ssc2[0:64, 0:2048], ssc2[0:64, 0:2048],
                                                    1.0, None, ALU.add, ALU.bypass))
                p.drain(VE)
                p.opc(VE, lambda e: e.tensor_scalar(ssc1[0:64, 0:2048], ssc1[0:64, 0:2048],
                                                    acol[0:64, 4:5], None, ALU.mult, ALU.bypass))
                p.drain(VE)
                p.opc(VE, lambda e: e.scalar_tensor_tensor(ssc1[0:64, 0:2048], ssc2[0:64, 0:2048],
                                                           acol[0:64, 3:4], ssc1[0:64, 0:2048], ALU.mult, ALU.add))
                p.opc(VE, lambda e: e.tensor_tensor(bbar[0:64, 0:2048], ssc1[0:64, 0:2048],
                                                    bsm[0:64, 0:2048], ALU.mult))
                p.drain(VE)
                p.opc(VE, lambda e: e.tensor_tensor_scan(hsm[0:64, 0:2048], abar[0:64, 0:2048],
                                                         bbar[0:64, 0:2048], 0.0, ALU.mult, ALU.add))
                p.drain(VE)
                p.opc(VE, lambda e: e.tensor_tensor(ysm[0:64, 0:2048], csm[0:64, 0:2048],
                                                    hsm[0:64, 0:2048], ALU.mult))
                p.drain(VE)
                vy = p.opc(VE, lambda e: e.engine_nop())
                p.wait(PE, S["ve"], vy)
                p.wait(PE, S["LG"], v4)
                last_se = {}
                st_last = {}
                for kt in range(NT):
                    pr = kt % 2
                    if kt >= 2 and (kt - 2) in last_se:
                        p.wait(SY, S["se"], last_se[kt - 2])
                    if kt >= 2:
                        p.wait(VE, S["SQ"], st_last[kt - 2])
                    vx = p.dma(SY, xt[pr].full, xres[b, 128 * kt:128 * (kt + 1), :])
                    for og in range(2):
                        bank = pb[2 + og]
                        p.wait(PE, S["ve"])
                        vpe = p.opc(PE, lambda e, kt=kt, og=og, bank=bank: e.matmul(
                            bank[:, 0:512],
                            ysm[0:64, 128 * kt:128 * (kt + 1)],
                            woutb[0:64, 512 * og:512 * (og + 1)],
                            start=True, stop=True))
                    p.wait(VE, S["pe"], vpe)
                    p.wait(VE, S["LQ"], vx)
                    for og in range(2):
                        bank = pb[2 + og]
                        p.opc(VE, lambda e, og=og, bank=bank: e.tensor_tensor(
                            smal[:, 0:512], bank[:, 0:512], opbrep[:, 512 * og:512 * (og + 1)], ALU.add))
                        p.opc(VE, lambda e, og=og, b=b: e.tensor_tensor(
                            smal[:, 0:512], smal[:, 0:512], repL1[(b, "ga")][:, 512 * og:512 * (og + 1)], ALU.mult))
                        p.opc(VE, lambda e, og=og, pr=pr: e.tensor_tensor(
                            xw[pr][:, 512 * og:512 * (og + 1)], smal[:, 0:512],
                            xt[pr][:, 512 * og:512 * (og + 1)], ALU.add))
                    vxw = p.opc(VE, lambda e: e.engine_nop())
                    p.wait(SY, S["ve"], vxw)
                    p.store(xres[b, 128 * kt:128 * (kt + 1), :], xw[pr].full)
                    vxn, _ = ln_h(xw[pr], stat[pr], h4tm, repL1[(b, "cb1")], repL1[(b, "sb")], xnb4)
                    last_se[kt] = vxn
                    p.opc(VE, lambda e, pr=pr: e.tensor_reduce(
                        stat[pr][:, 20:21], h4tm[:, 0:D], AX.X, ALU.max, apply_absolute_value=True))
                    p.drain(VE)
                    p.opc(VE, lambda e, pr=pr: e.tensor_tensor(
                        gacc[:, 0:1], gacc[:, 0:1], stat[pr][:, 20:21], ALU.max))
                    p.drain(VE)
                    vq = p.opc(VE, lambda e: e.engine_nop())
                    p.wait(PE, S["ve"], vq)
                    for c in range(8):
                        vt = p.opc(PE, lambda e, c=c: e.transpose(
                            pbt[:, 128 * c:128 * (c + 1)], h4tm[:, 128 * c:128 * (c + 1)], ident[:, :]))
                    p.wait(VE, S["pe"], vt)
                    vh4 = p.opc(VE, lambda e: e.tensor_copy(h4s.full, pbt[:, 0:1024]))
                    p.wait(SY, S["ve"], vh4)
                    st_last[kt] = p.store(h4_sp[b, kt], h4s.full)
            A16.release(); A32.release()
            p.barrier()

        def p_gmax_cc(gacc, ccin, ccout, dst_col):
            vv = p.opc(VE, lambda e: e.engine_nop())
            p.wait(GE, S["ve"], vv)
            p.opc(GE, lambda e: e.tensor_reduce(persc[0:1, 0:1], gacc[:, 0:1], AX.C, ALU.max))
            p.opc(GE, lambda e: e.memset(persc[0:1, 4:12], 0.0))
            vg = p.opc(GE, lambda e: e.tensor_copy(persc[0:1, 4:5], persc[0:1, 0:1]))
            vd = p.dma(GE, ccin, persc[0:1, 4:12])
            p.wait(GE, S["LG"], vd)
            vcc = p.op(GE, lambda e: e.collective_compute(
                "AllReduce", ALU.max, replica_groups=[list(range(8))],
                ins=[ccin], outs=[ccout]), S["CC"])
            p.wait(GE, S["CC"], vcc)
            vb = p.dma(GE, persc[0:1, 4:12], ccout)
            p.wait(VE, S["LG"], vb)
            p.opc(VE, lambda e, dst_col=dst_col: e.tensor_copy(persc[0:1, dst_col:dst_col + 1], persc[0:1, 4:5]))
            p.drain(VE)
            p.barrier()

        def rowmax_to_row(w_nat, nrows, width, rowf):
            A16.mark(); A32.mark()
            wt = A16.alloc(width)
            nt_ = nrows // 128
            sqc = A32.alloc(nt_ + 1)
            for r in range(nt_):
                vw = p.dma(GE, wt.full, w_nat[128 * r:128 * (r + 1), :])
                p.wait(VE, S["LG"], vw)
                p.opc(VE, lambda e, r=r: e.tensor_reduce(
                    sqc[:, r:r + 1], wt.full, AX.X, ALU.max, apply_absolute_value=True))
                p.drain(VE)
                vv = p.opc(VE, lambda e: e.engine_nop())
                p.wait(GE, S["ve"], vv)
            vv = p.opc(VE, lambda e: e.engine_nop())
            p.wait(SY, S["ve"], vv)
            vst = p.store(scr2[:, 0:nt_], sqc[:, 0:nt_])
            p.wait(SY, S["SQ"], vst)
            vrb = p.dma(SY, rowf.h[0:1, rowf.off:rowf.off + nrows].rearrange("q (r p) -> q r p", r=nt_),
                        scr2[:, 0:nt_].rearrange("p r -> r p"), allow_slow_non_contiguous=True)
            p.wait(VE, S["LQ"], vrb)
            A16.release(); A32.release()

        def quant_w_rows(wi, nr, width, rrg):
            A32.mark()
            q1s = A32.alloc(512)
            for r in range(nr):
                for j0 in range(0, width, 512):
                    w_ = min(512, width - j0)
                    p.opc(VE, lambda e, r=r, j0=j0, w_=w_, q1s=q1s: e.tensor_tensor(
                        q1s[:, 0:w_], wi[:, width * r + j0:width * r + j0 + w_],
                        rrg[:, j0:j0 + w_], ALU.mult))
                    p.opc(VE, lambda e, w_=w_, q1s=q1s: e.tensor_scalar(
                        q1s[:, 0:w_], q1s[:, 0:w_], MAGIC, -MAGIC, ALU.add, ALU.add))
                    p.opc(VE, lambda e, w_=w_, q1s=q1s: e.tensor_scalar(
                        q1s[:, 0:w_], q1s[:, 0:w_], 127.0, -128.0, ALU.min, ALU.max))
                    p.opc(VE, lambda e, r=r, j0=j0, w_=w_, q1s=q1s: e.tensor_copy(
                        wi[:, width * r + j0:width * r + j0 + w_], q1s[:, 0:w_]))
            return p.opc(VE, lambda e: e.engine_nop())

        IW = 2816
        IGW = [512, 512, 512, 512, 512, 256]

        def p_ffn1_A(gcol):
            A16.mark(); A32.mark()
            wgi1 = A16.alloc(8 * IW)
            wui1 = A16.alloc(8 * IW)
            rr_g = A16.alloc(IW)
            er_g = A16.alloc(IW)
            rr_u = A16.alloc(IW)
            er_u = A16.alloc(IW)
            h4t = [A16.alloc(1024) for _ in range(2)]
            h4q = A16.alloc(1024)
            gt1 = A16.alloc(IW)
            ut1 = A16.alloc(IW)
            tt1 = A16.alloc(IW)
            cols = A32.alloc(8)
            gacc2 = A32.alloc(4)
            A32.mark()
            rowf = A32.alloc(IW)
            rowr = A32.alloc(IW)
            vcx = p.dma(SY, persc[0:1, 16:17], clipx)
            p.wait(SE, S["LQ"], vcx)
            p.drain(SE)
            vsg = p.opc(SE, lambda e: e.activation(persc[0:1, 17:18], persc[0:1, 16:17], AF.Sigmoid))
            p.wait(VE, S["se"], vsg)
            p.opc(VE, lambda e, gcol=gcol: e.tensor_tensor(
                persc[0:1, 18:19], persc[0:1, 17:18], persc[0:1, gcol:gcol + 1], ALU.mult))
            p.drain(VE)
            p.opc(VE, lambda e: e.tensor_scalar(persc[0:1, 19:20], persc[0:1, 18:19],
                                                1.0 / 127.0, None, ALU.mult, ALU.bypass))
            p.drain(VE)
            p.opc(VE, lambda e: e.reciprocal(persc[0:1, 20:21], persc[0:1, 19:20]))
            p.drain(VE)
            vs4 = p.opc(VE, lambda e: e.engine_nop())
            p.wait(PE, S["ve"], vs4)
            vpe = p.opc(PE, lambda e: e.matmul(pC[:, 0:1], ones1f[0:1, :], persc[0:1, 20:21],
                                               start=True, stop=True))
            p.wait(VE, S["pe"], vpe)
            p.opc(VE, lambda e: e.tensor_copy(cols[:, 0:1], pC[:, 0:1]))
            p.drain(VE)
            for (wn, wT, rr_, er_) in ((wg1n, wgT1, rr_g, er_g), (wu1n, wuT1, rr_u, er_u)):
                rowmax_to_row(wn, IW, 1024, rowf)
                p.opc(VE, lambda e: e.reciprocal(rowr[0:1, 0:IW], rowf[0:1, 0:IW]))
                p.drain(VE)
                p.opc(VE, lambda e: e.tensor_scalar(rowr[0:1, 0:IW], rowr[0:1, 0:IW], 127.0, None, ALU.mult, ALU.bypass))
                p.drain(VE)
                vrr = p.opc(VE, lambda e: e.engine_nop())
                p.wait(PE, S["ve"], vrr)
                for g0 in range(0, IW, 512):
                    w_ = min(512, IW - g0)
                    p.wait(PE, S["ve"])
                    vpe = p.opc(PE, lambda e, g0=g0, w_=w_: e.matmul(
                        pC[:, 0:w_], ones1f[0:1, :], rowr.h[0:1, rowr.off + g0:rowr.off + g0 + w_],
                        start=True, stop=True))
                    p.wait(VE, S["pe"], vpe)
                    p.opc(VE, lambda e, g0=g0, w_=w_, rr_=rr_: e.tensor_copy(rr_[:, g0:g0 + w_], pC[:, 0:w_]))
                p.opc(VE, lambda e: e.tensor_scalar(rowr[0:1, 0:IW], rowf[0:1, 0:IW], persc[0:1, 19:20],
                                                    1.0 / 127.0, ALU.mult, ALU.mult))
                p.drain(VE)
                ver = p.opc(VE, lambda e: e.engine_nop())
                p.wait(PE, S["ve"], ver)
                for g0 in range(0, IW, 512):
                    w_ = min(512, IW - g0)
                    p.wait(PE, S["ve"])
                    vpe = p.opc(PE, lambda e, g0=g0, w_=w_: e.matmul(
                        pC[:, 0:w_], ones1f[0:1, :], rowr.h[0:1, rowr.off + g0:rowr.off + g0 + w_],
                        start=True, stop=True))
                    p.wait(VE, S["pe"], vpe)
                    p.opc(VE, lambda e, g0=g0, w_=w_, er_=er_: e.tensor_copy(er_[:, g0:g0 + w_], pC[:, 0:w_]))
            A32.release()
            q1h = A32.alloc(1024)
            vwg = p.dma(GE, wgi1.full.rearrange("p (r o) -> p r o", r=8),
                        wgT1.rearrange("(r p) o -> p r o", r=8))
            p.wait(VE, S["LG"], vwg)
            quant_w_rows(wgi1, 8, IW, rr_g)
            vwu = p.dma(GE, wui1.full.rearrange("p (r o) -> p r o", r=8),
                        wuT1.rearrange("(r p) o -> p r o", r=8))
            p.wait(VE, S["LG"], vwu)
            vqu = quant_w_rows(wui1, 8, IW, rr_u)
            p.opc(VE, lambda e: e.memset(gacc2[:, 0:1], 0.0))
            p.drain(VE)
            if DEBUG and "erg" in dbg:
                p.store(dbg["erg"], er_g[:, 0:IW])
                p.store(dbg["rrg"], rr_g[:, 0:IW])
            p.barrier()
            last_tt = {}
            last_vt = 0
            bank_ev1 = {}
            last_silu = 0
            for i in range(2 * NT):
                b, kt = divmod(i, NT)
                pr = i % 2
                vh_ = p.dma(SY, h4t[pr].full, h4_sp[b, kt])
                p.wait(VE, S["LQ"], vh_)
                if i >= 1:
                    p.wait(VE, S["pe"], last_mm)
                p.opc(VE, lambda e, pr=pr: e.tensor_scalar(
                    q1h.full, h4t[pr].full, cols[:, 0:1], MAGIC, ALU.mult, ALU.add))
                p.opc(VE, lambda e: e.tensor_scalar(q1h.full, q1h.full, -MAGIC, 127.0, ALU.add, ALU.min))
                vq = p.opc(VE, lambda e: e.tensor_scalar(h4q.full, q1h.full, -128.0, None, ALU.max, ALU.bypass))
                if i >= 1:
                    p.wait(SE, S["ve"], last_vt)
                p.wait(PE, S["ve"], vq)
                for which, wsrc, er_ in ((0, wgi1, er_g), (1, wui1, er_u)):
                    j0 = 0
                    for gi, gw in enumerate(IGW):
                        gidx = which * 6 + gi
                        bank = pb[gidx % 4]
                        if bank_ev1.get(gidx % 4):
                            ee, vv_ = bank_ev1[gidx % 4]
                            p.wait(PE, ee, vv_)
                        for r in range(8):
                            vpe = p.opc(PE, lambda e, r=r, j0=j0, gw=gw, wsrc=wsrc, bank=bank: e.matmul(
                                bank[:, 0:gw],
                                h4q[:, 128 * r:128 * (r + 1)],
                                wsrc[:, IW * r + j0:IW * r + j0 + gw],
                                start=(r == 0), stop=(r == 7)))
                        p.wait(VE, S["pe"], vpe)
                        if which == 0:
                            p.wait(VE, S["se"], last_silu)
                            vv_ = p.opc(VE, lambda e, j0=j0, gw=gw, bank=bank: e.tensor_tensor(
                                smal[:, 0:gw], bank[:, 0:gw], er_g[:, j0:j0 + gw], ALU.mult))
                            bank_ev1[gidx % 4] = (S["ve"], vv_)
                            vsm = p.opc(VE, lambda e: e.engine_nop())
                            p.wait(SE, S["ve"], vsm)
                            last_silu = p.opc(SE, lambda e, j0=j0, gw=gw: e.activation(
                                gt1[:, j0:j0 + gw], smal[:, 0:gw], AF.Silu))
                        else:
                            vv_ = p.opc(VE, lambda e, j0=j0, gw=gw, bank=bank: e.tensor_tensor(
                                ut1[:, j0:j0 + gw], bank[:, 0:gw], er_u[:, j0:j0 + gw], ALU.mult))
                            bank_ev1[gidx % 4] = (S["ve"], vv_)
                        j0 += gw
                last_mm = vpe
                p.drain(SE)
                vse = p.opc(SE, lambda e: e.nop())
                p.wait(VE, S["se"], vse)
                if i >= 1:
                    p.wait(VE, S["SQ"], last_tt[i - 1])
                vt = p.opc(VE, lambda e: e.tensor_tensor(tt1.full, gt1.full, ut1.full, ALU.mult))
                last_vt = vt
                p.drain(VE)
                p.opc(VE, lambda e: e.tensor_reduce(smal[:, 8:9], tt1.full, AX.X, ALU.max,
                                                    apply_absolute_value=True))
                p.drain(VE)
                p.opc(VE, lambda e: e.tensor_tensor(gacc2[:, 0:1], gacc2[:, 0:1], smal[:, 8:9], ALU.max))
                p.drain(VE)
                vt2 = p.opc(VE, lambda e: e.engine_nop())
                p.wait(SY, S["ve"], vt2)
                last_tt[i] = p.store(t1_sp[b, kt], tt1.full)
                if DEBUG and i == 0 and "tt1" in dbg:
                    p.store(dbg["tt1"], tt1.full)
                    p.store(dbg["h4q0"], h4q.full)
            p.barrier()
            p_gmax_cc(gacc2, cc_in2, cc_out2, 30)
            A16.release(); A32.release()
            p.barrier()

        def p_ffn1_B(repg):
            A16.mark(); A32.mark()
            wdi1 = A16.alloc(22 * 1024)
            rr_d = A16.alloc(1024)
            er_d = A16.alloc(1024)
            ttl = [A16.alloc(IW) for _ in range(2)]
            ti1 = A16.alloc(IW)
            tif1 = A16.alloc(IW)
            q1s2 = A32.alloc(512)
            cols = A32.alloc(8)
            xt = [A32.alloc(1024) for _ in range(2)]
            A32.mark()
            rowf = A32.alloc(1024)
            rowr = A32.alloc(1024)
            vcx = p.dma(SY, persc[0:1, 16:17], clipd)
            p.wait(SE, S["LQ"], vcx)
            p.drain(SE)
            vsg = p.opc(SE, lambda e: e.activation(persc[0:1, 17:18], persc[0:1, 16:17], AF.Sigmoid))
            p.wait(VE, S["se"], vsg)
            p.opc(VE, lambda e: e.tensor_tensor(
                persc[0:1, 18:19], persc[0:1, 17:18], persc[0:1, 30:31], ALU.mult))
            p.drain(VE)
            p.opc(VE, lambda e: e.tensor_scalar(persc[0:1, 19:20], persc[0:1, 18:19],
                                                1.0 / 127.0, None, ALU.mult, ALU.bypass))
            p.drain(VE)
            p.opc(VE, lambda e: e.reciprocal(persc[0:1, 20:21], persc[0:1, 19:20]))
            p.drain(VE)
            vs5 = p.opc(VE, lambda e: e.engine_nop())
            p.wait(PE, S["ve"], vs5)
            vpe = p.opc(PE, lambda e: e.matmul(pC[:, 0:1], ones1f[0:1, :], persc[0:1, 20:21],
                                               start=True, stop=True))
            p.wait(VE, S["pe"], vpe)
            p.opc(VE, lambda e: e.tensor_copy(cols[:, 0:1], pC[:, 0:1]))
            p.drain(VE)
            rowmax_to_row(wd1n, 1024, IW, rowf)
            p.opc(VE, lambda e: e.reciprocal(rowr[0:1, 0:1024], rowf[0:1, 0:1024]))
            p.drain(VE)
            p.opc(VE, lambda e: e.tensor_scalar(rowr[0:1, 0:1024], rowr[0:1, 0:1024], 127.0, None, ALU.mult, ALU.bypass))
            p.drain(VE)
            vrr = p.opc(VE, lambda e: e.engine_nop())
            p.wait(PE, S["ve"], vrr)
            for g0 in range(0, 1024, 512):
                p.wait(PE, S["ve"])
                vpe = p.opc(PE, lambda e, g0=g0: e.matmul(
                    pC[:, 0:512], ones1f[0:1, :], rowr.h[0:1, rowr.off + g0:rowr.off + g0 + 512],
                    start=True, stop=True))
                p.wait(VE, S["pe"], vpe)
                p.opc(VE, lambda e, g0=g0: e.tensor_copy(rr_d[:, g0:g0 + 512], pC[:, 0:512]))
            p.opc(VE, lambda e: e.tensor_scalar(rowr[0:1, 0:1024], rowf[0:1, 0:1024], persc[0:1, 19:20],
                                                1.0 / 127.0, ALU.mult, ALU.mult))
            p.drain(VE)
            ver = p.opc(VE, lambda e: e.engine_nop())
            p.wait(PE, S["ve"], ver)
            for g0 in range(0, 1024, 512):
                p.wait(PE, S["ve"])
                vpe = p.opc(PE, lambda e, g0=g0: e.matmul(
                    pC[:, 0:512], ones1f[0:1, :], rowr.h[0:1, rowr.off + g0:rowr.off + g0 + 512],
                    start=True, stop=True))
                p.wait(VE, S["pe"], vpe)
                p.opc(VE, lambda e, g0=g0: e.tensor_copy(er_d[:, g0:g0 + 512], pC[:, 0:512]))
            A32.release()
            vwd = p.dma(GE, wdi1.full.rearrange("p (r o) -> p r o", r=22),
                        wdT1.rearrange("(r p) o -> p r o", r=22))
            p.wait(VE, S["LG"], vwd)
            quant_w_rows(wdi1, 22, 1024, rr_d)
            p.barrier()
            last_se = {}
            sty_last = {}
            dn_ev = {}
            nloop = 0 if STOP_AFTER == "ffnBsetup" else (2 if STOP_AFTER == "ffnB2" else 2 * NT)
            for i in range(nloop):
                b, kt = divmod(i, NT)
                pr = i % 2
                if i >= 2 and (i - 2) in last_se:
                    p.wait(SY, S["ve"], last_se[i - 2])
                    p.wait(SY, S["SQ"], sty_last[i - 2])
                vtl = p.dma(SY, ttl[pr].full, t1_sp[b, kt])
                vx = p.dma(SY, xt[pr].full, xres[b, 128 * kt:128 * (kt + 1), :])
                p.wait(VE, S["LQ"], vtl)
                for j0 in range(0, IW, 512):
                    w_ = min(512, IW - j0)
                    p.opc(VE, lambda e, j0=j0, w_=w_, pr=pr: e.tensor_scalar(
                        q1s2[:, 0:w_], ttl[pr][:, j0:j0 + w_], cols[:, 0:1], MAGIC, ALU.mult, ALU.add))
                    p.opc(VE, lambda e, w_=w_: e.tensor_scalar(
                        q1s2[:, 0:w_], q1s2[:, 0:w_], -MAGIC, 127.0, ALU.add, ALU.min))
                    p.opc(VE, lambda e, j0=j0, w_=w_: e.tensor_scalar(
                        ti1[:, j0:j0 + w_], q1s2[:, 0:w_], -128.0, None, ALU.max, ALU.bypass))
                vq = p.opc(VE, lambda e: e.engine_nop())
                p.wait(PE, S["ve"], vq)
                tr_ev = 0
                for grp in range(3):
                    n8 = 8 if grp < 2 else 6
                    if grp > 0:
                        p.wait(PE, S["ve"], tr_ev)
                    for c8 in range(n8):
                        c = 8 * grp + c8
                        vt = p.opc(PE, lambda e, c=c, c8=c8: e.transpose(
                            pbt[:, 128 * c8:128 * (c8 + 1)], ti1[:, 128 * c:128 * (c + 1)], ident[:, :]))
                    p.wait(VE, S["pe"], vt)
                    tr_ev = p.opc(VE, lambda e, grp=grp, n8=n8: e.tensor_copy(
                        tif1[:, 1024 * grp:1024 * grp + 128 * n8], pbt[:, 0:128 * n8]))
                vtf = p.opc(VE, lambda e: e.engine_nop())
                p.wait(PE, S["ve"], vtf)
                for og in range(2):
                    bank = pb[4 + og]
                    if dn_ev.get(og):
                        p.wait(PE, S["ve"], dn_ev[og])
                    for r in range(22):
                        vpe = p.opc(PE, lambda e, r=r, og=og, bank=bank: e.matmul(
                            bank[:, 0:512],
                            tif1[:, 128 * r:128 * (r + 1)],
                            wdi1[:, 1024 * r + 512 * og:1024 * r + 512 * (og + 1)],
                            start=(r == 0), stop=(r == 21)))
                p.wait(VE, S["pe"], vpe)
                p.wait(VE, S["LQ"], vx)
                for og in range(2):
                    bank = pb[4 + og]
                    dn_ev[og] = p.opc(VE, lambda e, og=og, bank=bank: e.tensor_tensor(
                        smal[:, 0:512], bank[:, 0:512], er_d[:, 512 * og:512 * (og + 1)], ALU.mult))
                    p.opc(VE, lambda e, og=og, b=b: e.tensor_tensor(
                        smal[:, 0:512], smal[:, 0:512], repg[(b, "gb")][:, 512 * og:512 * (og + 1)], ALU.mult))
                    p.opc(VE, lambda e, og=og: e.tensor_copy(q1s2[:, 0:512], smal[:, 0:512]))
                    vfin = p.opc(VE, lambda e, og=og, pr=pr: e.tensor_tensor(
                        xt[pr][:, 512 * og:512 * (og + 1)], q1s2[:, 0:512],
                        xt[pr][:, 512 * og:512 * (og + 1)], ALU.add))
                last_se[i] = vfin
                p.wait(SY, S["ve"], vfin)
                sty = p.store(y_out[b, 128 * kt:128 * (kt + 1), :], xt[pr].full)
                sty_last[i] = sty
            A16.release(); A32.release()
            p.barrier()

        def p_h1(b, rep, h1fm, h1tm, xnb):
            A32.mark()
            xt = [A32.alloc(1024) for _ in range(2)]
            stat = [A32.alloc(24) for _ in range(2)]
            last_se = {}
            for k in range(NT):
                if k >= 2 and (k - 2) in last_se:
                    p.wait(SY, S["se"], last_se[k - 2])
                vld = p.dma(SY, xt[k % 2].full, x_in[b, 128 * k:128 * (k + 1), :])
                p.wait(VE, S["LQ"], vld)
                vxn, _ = ln_h(xt[k % 2], stat[k % 2], h1tm, rep[(b, "ca1")], rep[(b, "sa")], xnb)
                last_se[k] = vxn
                tm_to_fm(h1tm.full, h1fm, k)
            A32.release()
            p.barrier()

        cst = {"negl": A32.alloc(4), "w1m": A16.alloc(64)}
        p_init()
        p_attn_consts(cst)
        p_mod(0, c0T, wmod0)
        p_mod(1, c1T, wmod1)
        for b in range(NB):
            A16.mark(); A32.mark()
            qfm = A16.alloc(8 * T); kfm = A16.alloc(8 * T)
            vab = A16.alloc(16 * 1040); qn = A16.alloc(8 * T)
            negkb = A32.alloc(32)
            A16.mark(); A32.mark()
            h1fm = A16.alloc(8 * T)
            A16.mark()
            rep1 = {(b, "ca1"): A16.alloc(D), (b, "sa"): A16.alloc(D)}
            build_rep(0, rep1)
            h1tm = A16.alloc(D); xnb = A16.alloc(D)
            p_h1(b, rep1, h1fm.full, h1tm, xnb)
            A16.release()
            p_qkv(b, h1fm.full, qfm, kfm, vab, qn, negkb)
            A16.release(); A32.release()
            p_core(b, qfm, kfm, vab, qn, negkb, cst)
            A16.release(); A32.release()
            A16.mark(); A32.mark()
            rep2 = {(b, "ga"): A16.alloc(D), (b, "cb1"): A16.alloc(D), (b, "sb"): A16.alloc(D)}
            build_rep(0, rep2)
            p_attn_out(b, rep2)
            A16.release(); A32.release()

        if DEBUG and "x1" in dbg:
            for b in range(NB):
                p.store(dbg["x1"][b], xres[b])
            p.barrier()

        p_ffn0_A()
        A16.mark()
        h3fm = [A16.alloc(8 * T) for _ in range(NB)]
        A16.mark(); A32.mark()
        repB = {}
        repL1 = {}
        for b in range(NB):
            repB[(b, "gb")] = A16.alloc(D)
            repL1[(b, "ca1")] = A16.alloc(D)
            repL1[(b, "sa")] = A16.alloc(D)
        build_rep(0, repB)
        build_rep(1, repL1)
        if DEBUG and "repgb" in dbg:
            p.barrier()
            for b in range(NB):
                p.store(dbg["repgb"][b], repB[(b, "gb")].full)
            for b in range(NB):
                p.store(dbg["repca1"][b], repL1[(b, "ca1")].full)
        p_ffn0_B(repB, repL1, h3fm)
        A16.release(); A32.release()

        if DEBUG and "t0" in dbg:
            for b in range(NB):
                p.store(dbg["t0"][b], t0_sp[b, 0])
        if DEBUG and "rmq" in dbg:
            for b in range(NB):
                for kt in range(NT):
                    p.store(dbg["rmq"][b, kt], rmq_sp[b, kt])
        if DEBUG and "h2q" in dbg:
            for b in range(NB):
                for kt in range(2):
                    p.store(dbg["h2q"][b, kt], h2q_sp[b, kt])
        if DEBUG and "x2" in dbg:
            for b in range(NB):
                p.store(dbg["x2"][b], xres[b])
        if DEBUG and "h3fm" in dbg:
            for c in range(8):
                p.store(dbg["h3fm"][c], h3fm[0][:, T * c:T * (c + 1)])
        if DEBUG and ("x2" in dbg or "h3fm" in dbg):
            p.barrier()

        gacc1 = A32.alloc(4)
        p.opc(VE, lambda e: e.memset(gacc1[:, 0:1], 0.0))
        p.drain(VE)
        A16.mark(); A32.mark()
        repL1b = {}
        for b in range(NB):
            repL1b[(b, "ga")] = A16.alloc(D)
            repL1b[(b, "cb1")] = A16.alloc(D)
            repL1b[(b, "sb")] = A16.alloc(D)
        build_rep(1, repL1b)
        p_l1_attn(repL1b, h3fm, gacc1)
        A16.release(); A32.release()
        A16.release()

        if DEBUG and "x3" in dbg:
            for b in range(NB):
                p.store(dbg["x3"][b], xres[b])
            p.barrier()

        if STOP_AFTER not in ("x3",):
            p_gmax_cc(gacc1, cc_in1, cc_out1, 21)
        if STOP_AFTER not in ("x3", "cc1"):
            p_ffn1_A(21)
        if STOP_AFTER not in ("x3", "cc1", "ffnA"):
            A16.mark(); A32.mark()
            repg1 = {}
            for b in range(NB):
                repg1[(b, "gb")] = A16.alloc(D)
            build_rep(1, repg1)
            p_ffn1_B(repg1)
            A16.release(); A32.release()
        if DEBUG and "attnsp" in dbg:
            for b in range(NB):
                p.store(dbg["attnsp"][b], attn_sp[b])
        if DEBUG and "scol" in dbg:
            p.store(dbg["scol"], persc[0:1, 0:32])
        p.barrier()
        p.wait(SY, S["SQ"])
        p.wait(SY, S["LQ"])

        with nc.Block() as block:
            @block.tensor
            def _(e):
                for f in p.streams[PE]: f(e)

            @block.vector
            def _(e):
                for f in p.streams[VE]: f(e)

            @block.scalar
            def _(e):
                for f in p.streams[SE]: f(e)

            @block.gpsimd
            def _(e):
                for f in p.streams[GE]: f(e)

            @block.sync
            def _(e):
                for f in p.streams[SY]: f(e)

    return nc


def prep_inputs(I):
    C = np.ascontiguousarray
    shared = {
        "wqT": C(I["L0_attn_Wq"].T), "wkT": C(I["L0_attn_Wk"].T),
        "wvT": C(I["L0_attn_Wv"].T), "woT": C(I["L0_attn_Wo"].T),
        "lamq1": C(I["L0_attn_lambda_q1"].reshape(32, 1)),
        "lamk1": C(I["L0_attn_lambda_k1"].reshape(32, 1)),
        "lamq2": C(I["L0_attn_lambda_q2"].reshape(32, 1)),
        "lamk2": C(I["L0_attn_lambda_k2"].reshape(32, 1)),
        "sublnw": C(I["L0_attn_subln_w"].reshape(1, 64)),
        "wgT0": C(I["L0_ffn_w_gate"].T), "wuT0": C(I["L0_ffn_w_up"].T),
        "wdT0": C(I["L0_ffn_w_down"].T),
        "wmod0": C(I["L0_wmod"]),
        "convw": C(I["L1_attn_conv_w"][:, 0, :].reshape(8, 128, 4).transpose(1, 0, 2).reshape(128, 32)),
        "convb": C(I["L1_attn_conv_b"].reshape(1, D)),
        "inprojT": C(I["L1_attn_in_proj_w"].T),
        "inprojb": C(I["L1_attn_in_proj_b"].reshape(1, 320)),
        "alog": C(I["L1_attn_A_log"].reshape(64, 1)),
        "outprojT": C(I["L1_attn_out_proj_w"].T),
        "outprojb": C(I["L1_attn_out_proj_b"].reshape(1, D)),
        "clipx": C(I["L1_ffn_clip_x"].reshape(1, 1)),
        "clipd": C(I["L1_ffn_clip_down"].reshape(1, 1)),
        "wgT1": C(I["L1_ffn_w_gate"].T), "wuT1": C(I["L1_ffn_w_up"].T),
        "wdT1": C(I["L1_ffn_w_down"].T),
        "wg1n": C(I["L1_ffn_w_gate"]), "wu1n": C(I["L1_ffn_w_up"]),
        "wd1n": C(I["L1_ffn_w_down"]),
        "wmod1": C(I["L1_wmod"]),
    }
    c0T = C(I["L0_c"].T)
    c1T = C(I["L1_c"].T)
    per_core = []
    for cid in range(8):
        m = dict(shared)
        m["x_in"] = C(I["L0_attn_x"][2 * cid:2 * cid + 2])
        m["c0T"] = C(c0T[:, 2 * cid:2 * cid + 2])
        m["c1T"] = C(c1T[:, 2 * cid:2 * cid + 2])
        per_core.append(m)
    return per_core


def run(I, debug_names=()):
    in_maps = prep_inputs(I)
    nc = build_program(debug_names=debug_names)
    return run_bass_kernel_spmd(nc, in_maps, core_ids=list(range(8)))

def kernel(**inputs):
    in_maps = prep_inputs(inputs)
    nc = build_program()
    res = run_bass_kernel_spmd(nc, in_maps, core_ids=list(range(8)))
    return np.concatenate([np.asarray(r["y"]) for r in res.results], axis=0)
